# version 25
# baseline (speedup 1.0000x reference)
import numpy as np
import ml_dtypes
import concourse.bass as bass
import concourse.mybir as mybir
from concourse.bass_utils import run_bass_kernel_spmd

F32 = mybir.dt.float32
BF16 = mybir.dt.bfloat16
FP8 = mybir.dt.float8e4
U8 = mybir.dt.uint8
I16 = mybir.dt.int16
Act = mybir.ActivationFunctionType
Alu = mybir.AluOpType
AxX = mybir.AxisListType.X

SCALE = 1
B = 4
T = 4096 // SCALE
DM = 1024
NH = 16
D = 64
P = 128
TOK = T // 2
NCH = DM // P
ST = min(512, TOK)
NST = TOK // ST
QS = min(512, TOK)
NQS = T // QS
KPQ = QS // P
NKB = T // P
HO = NH // 2
SG = 2
C_RND = 12582912.0
C_Q9 = 24576.0
FP8_SMALL = 0.03125
EPS = 1e-5
M_CAP = 30.0
DBG = 0

V_OFF = 2 * DM * TOK
ST_OFF = V_OFF + NH * TOK * D
AG1_N = ST_OFF + 2048
AG2_N = (DM // 2) * T
SPAN = V_OFF + (NH // 2) * TOK * D
HDT = (NH // 2) * TOK * D
PAIRS = [[0, 1], [2, 3], [4, 5], [6, 7]]

ENGINES = ("tensor", "vector", "scalar", "gpsimd", "sync")


class Buf:
    __slots__ = ("name", "writers", "readers")

    def __init__(self, name):
        self.name, self.writers, self.readers = name, [], []


class Prog:
    def __init__(self):
        self.ops = {e: [] for e in ENGINES}

    def op(self, eng, fn, reads=(), writes=(), appends=(), inc=True, dma=False,
           cc=False):
        idx = len(self.ops[eng])
        ev = (eng, idx)
        waits = set()
        for b in reads:
            waits.update(b.writers)
        for b in writes:
            waits.update(b.writers)
            waits.update(b.readers)
        for b in appends:
            waits.update(b.readers)
        waits.discard(ev)
        self.ops[eng].append(dict(fn=fn, inc=inc, dma=dma, cc=cc, waits=waits))
        for b in writes:
            b.writers = [ev]
            b.readers = []
        for b in appends:
            b.writers.append(ev)
        for b in reads:
            if not b.readers or b.readers[-1] != ev:
                b.readers.append(ev)

    NDQ = 8

    def seal(self):
        self.cum, self.cover, self.covidx, self.key = {}, {}, {}, {}
        for e in ENGINES:
            ops = self.ops[e]
            if ops and not ops[-1]["inc"]:
                ops[-1]["inc"] = True
            c = 0
            dslot = {}
            ndma = 0
            counts, keys, pre = [], [], []
            for o in ops:
                if o["dma"]:
                    sl = ndma % self.NDQ
                    ndma += 1
                    pre.append((("dma", e, sl), dslot.get(sl, 0)))
                    dslot[sl] = dslot.get(sl, 0) + 16
                    counts.append(dslot[sl])
                    keys.append(("dma", e, sl))
                elif o["cc"]:
                    ccc = getattr(self, "_ccc", {}).get(e, 0) + 1
                    self._ccc = getattr(self, "_ccc", {})
                    self._ccc[e] = ccc
                    pre.append(None)
                    counts.append(ccc)
                    keys.append(("cc", e))
                else:
                    if o["inc"]:
                        c += 1
                    pre.append(None)
                    counts.append(c)
                    keys.append(("c", e))
            self.cum[e], self.key[e] = counts, keys
            self.pre = getattr(self, "pre", {})
            self.pre[e] = pre
            cov, ci = [None] * len(ops), [None] * len(ops)
            nxt = nxi = None
            for i in range(len(ops) - 1, -1, -1):
                if ops[i]["inc"] and not ops[i]["dma"] and not ops[i]["cc"]:
                    nxt, nxi = counts[i], i
                if ops[i]["dma"] or ops[i]["cc"]:
                    cov[i], ci[i] = counts[i], i
                else:
                    cov[i], ci[i] = nxt, nxi
            self.cover[e], self.covidx[e] = cov, ci

    def emit(self, e, eng_obj, sems):
        watermark = {}

        def wait(key, val):
            if val > watermark.get(key, 0):
                eng_obj.wait_ge(sems[key], val)
                watermark[key] = val

        for i, o in enumerate(self.ops[e]):
            tgt = {}
            for (we, wi) in sorted(o["waits"]):
                ci = self.covidx[we][wi]
                assert ci is not None, f"{e}:{i} dep on {we}:{wi} has no covering inc"
                if we == e and ci >= i and not self.ops[we][wi]["dma"]                         and not self.ops[we][wi]["cc"]:
                    continue
                key = self.key[we][ci]
                val = self.cover[we][wi]
                if val > tgt.get(key, 0):
                    tgt[key] = val
            for key in sorted(tgt, key=str):
                wait(key, tgt[key])
            if self.pre[e][i] is not None:
                wait(*self.pre[e][i])
            inst = o["fn"](eng_obj)
            if o["inc"]:
                inst.then_inc(sems[self.key[e][i]], 16 if o["dma"] else 1)


def _bucket_np(n):
    import math
    NBUK, MAXD = 32, 128
    n = np.asarray(n)
    me = NBUK // 2
    small = n < me
    ns = np.maximum(n, 1).astype(np.float64)
    vl = me + (np.log(ns / me) / math.log(MAXD / me) * (NBUK - me)).astype(np.int64)
    vl = np.minimum(vl, NBUK - 1)
    return np.where(small, n, vl)


def host_constants(half):
    cons = {}
    t = np.arange(half * TOK, half * TOK + TOK, dtype=np.float64)
    inv = 1.0 / (10000.0 ** (np.arange(0, D, 2, dtype=np.float64) / D))
    ang = t[None, :] * inv[:, None]
    c32, s32 = np.cos(ang), np.sin(ang)
    cons["ropeC"] = np.tile(c32, (4, 1)).astype(ml_dtypes.bfloat16)
    cons["ropeS"] = np.concatenate([-s32, s32, -s32, s32], 0).astype(ml_dtypes.bfloat16)
    kk = np.arange(P)[:, None]
    qq = np.arange(P)[None, :]
    cons["tri01"] = (kk <= qq).astype(ml_dtypes.bfloat16)
    cons["eye_bf"] = np.eye(P, dtype=ml_dtypes.bfloat16)
    cons["eye_f32"] = np.eye(P, dtype=np.float32)
    cons["ones_p"] = np.ones((P, 1), np.float32)
    cons["ones_r"] = np.ones((1, P), np.float32)
    pm = np.zeros((P, P), np.float32)
    for g_ in range(2):
        for j_ in range(32):
            a_, b_ = g_ * 64 + j_, g_ * 64 + 32 + j_
            pm[a_, b_] = 1.0
            pm[b_, a_] = 1.0
    cons["perm"] = pm.astype(ml_dtypes.bfloat16)
    jr = np.zeros((P, P), np.float32)
    jr[np.arange(P)[::-1], np.arange(P)] = 1.0
    cons["jrev"] = jr.astype(ml_dtypes.bfloat16)
    e = np.arange(384)
    bkt = _bucket_np(np.maximum(e - 127, 0))
    oneh = np.zeros((32, 384), np.float32)
    oneh[bkt, e] = 1.0
    cons["oneh"] = oneh
    return cons


def _groups(nkb):
    g, j = [], 0
    while j < nkb:
        g.append((j, min(SG, nkb - j)))
        j += SG
    return g


def build_program():
    import contextlib
    nc = bass.Bass("TRN2", target_bir_lowering=False, debug=False)
    pr = Prog()

    x0T = nc.dram_tensor("x0T", [DM, TOK], F32, kind="ExternalInput")
    o_xT = nc.dram_tensor("o_xT", [DM, TOK], F32, kind="ExternalOutput")
    Wd, Bd = {}, {}
    for l in (0, 1):
        for w in ("q", "k", "v", "p", "f"):
            Wd[l, w] = nc.dram_tensor(f"W{l}{w}", [DM, DM], F32, kind="ExternalInput")
        for bn in ("bq", "bk", "bp", "bf"):
            Bd[l, bn] = nc.dram_tensor(f"B{l}{bn}", [P, NCH], F32, kind="ExternalInput")
        Bd[l, "bv"] = nc.dram_tensor(f"B{l}bv", [1, DM], F32, kind="ExternalInput")
    d_rel8 = nc.dram_tensor("rel8", [32, HO], F32, kind="ExternalInput")
    d_ropeC = nc.dram_tensor("d_ropeC", [P, TOK], BF16, kind="ExternalInput")
    d_ropeS = nc.dram_tensor("d_ropeS", [P, TOK], BF16, kind="ExternalInput")
    d_tri = nc.dram_tensor("d_tri", [P, P], BF16, kind="ExternalInput")
    d_eyef = nc.dram_tensor("d_eyef", [P, P], F32, kind="ExternalInput")
    d_onesp = nc.dram_tensor("d_onesp", [P, 1], F32, kind="ExternalInput")
    d_onesr = nc.dram_tensor("d_onesr", [1, P], F32, kind="ExternalInput")
    d_oneh = nc.dram_tensor("d_oneh", [32, 384], F32, kind="ExternalInput")
    d_jrev = nc.dram_tensor("d_jrev", [P, P], BF16, kind="ExternalInput")
    d_perm = nc.dram_tensor("d_perm", [P, P], BF16, kind="ExternalInput")

    ag1_in, ag1_out, ag2_in, ag2_out, ar_in, ar_out = {}, {}, {}, {}, {}, {}
    for l in (0, 1):
        ag1_in[l] = nc.dram_tensor(f"ag1i{l}", [1, AG1_N], BF16)
        ag1_out[l] = nc.dram_tensor(f"ag1o{l}", [1, 2 * AG1_N], BF16)
        ag2_in[l] = nc.dram_tensor(f"ag2i{l}", [1, AG2_N], BF16)
        ag2_out[l] = nc.dram_tensor(f"ag2o{l}", [1, 2 * AG2_N + TOK], BF16)
        ar_in[l] = nc.dram_tensor(f"ari{l}", [1, 8], F32)
        ar_out[l] = nc.dram_tensor(f"aro{l}", [1, 8], F32)
    band_dram = nc.dram_tensor("band_dram", [HO, 384], BF16)
    x1loc = nc.dram_tensor("x1loc", [1, DM * TOK], F32)
    qkvloc, ot2loc = {}, {}
    for l in (0, 1):
        qkvloc[l] = nc.dram_tensor(f"qkvloc{l}", [1, 6 * HDT], BF16)
        ot2loc[l] = nc.dram_tensor(f"ot2loc{l}", [1, 2 * (DM // 2) * T], BF16)

    ctx = contextlib.ExitStack()

    def sbt(n, s, d):
        return ctx.enter_context(nc.sbuf_tensor(n, s, d))

    def pst(n, s):
        return ctx.enter_context(nc.psum_tensor(n, s, F32))

    with ctx:
        Xrs = sbt("Xrs", [P, NCH * TOK], BF16)
        zsl = [sbt(f"zsl{i}", [P, NCH * ST], F32) for i in range(2)]
        x1c = sbt("x1c", [P, 1024], F32)
        qst = [sbt("qst0", [P, TOK], BF16)]
        xqb = sbt("xqb", [P, TOK], BF16)
        rt1 = [sbt(f"rt1{i}", [P, ST], BF16) for i in range(2)]
        rt2 = [sbt(f"rt2{i}", [P, ST], BF16) for i in range(2)]
        vstg = [sbt(f"vstg{i}", [P, ST], BF16) for i in range(2)]
        qkt = sbt("qkt", [P, 2 * T], BF16)
        vph = [sbt(f"vph{i}", [P, 65 * NKB], BF16) for i in range(2)]
        atb = [sbt(f"atb{i}", [P, SG * QS], BF16) for i in range(3)]
        ot_sm = [sbt(f"ot_sm{i}", [D, QS], BF16) for i in range(2)]
        ot_sl = [sbt(f"ot_sl{i}", [P, NCH * ST], BF16) for i in range(2)]
        wsl = [sbt(f"wsl{i}", [P, NCH * DM], BF16) for i in range(2)]
        wstg = [sbt(f"wstg{i}", [P, DM], F32) for i in range(2)]
        SCR = max(TOK, DM)
        HSC = max(TOK // 2, DM)
        f8c = sbt("f8c", [P, HSC], FP8)
        mkb = sbt("mkb", [P, ST], I16)
        bgb = sbt("bgb", [P, HSC], BF16)
        tmpF = sbt("tmpF", [P, max(2 * ST, HSC)], F32)
        tmpG = sbt("tmpG", [P, ST], F32)
        ropeC = sbt("ropeCs", [P, TOK], BF16)
        ropeS = sbt("ropeSs", [P, TOK], BF16)
        tri = sbt("tris", [P, P], BF16)
        eyef = sbt("eyefs", [P, P], F32)
        onesp = sbt("onesps", [P, 1], F32)
        onesr = sbt("onesrs", [1, P], F32)
        onesF = sbt("onesFs", [P, D], F32)
        onehs = sbt("onehss", [32, 384], F32)
        jrev = sbt("jrevs", [P, P], BF16)
        permm = sbt("permm", [P, P], BF16)
        hank = sbt("hank", [P, P], BF16)
        rel8s = sbt("rel8ss", [32, HO], F32)
        ebias = sbt("ebias", [P, HO], F32)
        ediag = sbt("ediag", [P, HO * P], BF16)
        eprev = sbt("eprev", [P, HO * P], BF16)
        bcols = sbt("bcols", [P, 8 * NCH], F32)
        vbias = sbt("vbias", [P, DM], BF16)
        bvrow = sbt("bvrow", [1, DM], BF16)
        onesrb = sbt("onesrb", [1, P], BF16)
        bandF = sbt("bandF", [HO, 384], F32)
        bandB = sbt("bandB", [HO, 384], BF16)
        statc = sbt("statc", [P, 8], F32)
        row_m = sbt("row_m", [1, ST], F32)
        row_q2 = sbt("row_q2", [1, ST], F32)
        row_i = row_q2
        g16 = sbt("g16", [1, 16], BF16)
        sc1 = sbt("sc1", [1, 16], F32)
        scC = sbt("scC", [P, 8], F32)
        epsc = sbt("epsc", [P, 1], F32)
        rnorm = wstg[0]

        ps_a = pst("ps_a", [P, SG * QS])
        ps_b = pst("ps_b", [P, SG * QS])
        ps_o = pst("ps_o", [P, 3 * QS])
        ps_m = pst("ps_m", [P, QS])

        sems = {}
        for e in ENGINES:
            sems[("c", e)] = ctx.enter_context(nc.semaphore(f"sc_{e}"))
            for sl in range(Prog.NDQ):
                sems[("dma", e, sl)] = ctx.enter_context(
                    nc.semaphore(f"sd_{e}{sl}"))
        sems[("cc", "gpsimd")] = ctx.enter_context(nc.semaphore("s_cc"))

        bb = {}

        def N(name):
            if name not in bb:
                bb[name] = Buf(name)
            return bb[name]

        def T_(fn, r=(), w=(), a=(), inc=True):
            pr.op("tensor", fn, [N(x) for x in r], [N(x) for x in w],
                  [N(x) for x in a], inc=inc)

        def V_(fn, r=(), w=(), a=()):
            pr.op("vector", fn, [N(x) for x in r], [N(x) for x in w],
                  [N(x) for x in a])

        def S_(fn, r=(), w=(), a=()):
            pr.op("scalar", fn, [N(x) for x in r], [N(x) for x in w],
                  [N(x) for x in a])

        def Gd(fn, r=(), w=(), a=()):
            pr.op("gpsimd", fn, [N(x) for x in r], [N(x) for x in w],
                  [N(x) for x in a], dma=True)

        def Gc(fn, r=(), w=(), a=()):
            pr.op("gpsimd", fn, [N(x) for x in r], [N(x) for x in w],
                  [N(x) for x in a], dma=False)

        def Y_(fn, r=(), w=(), a=()):
            pr.op("sync", fn, [N(x) for x in r], [N(x) for x in w],
                  [N(x) for x in a], dma=True)

        def CC_(fn, r=(), w=()):
            pr.op("gpsimd", fn, [N(x) for x in r], [N(x) for x in w], cc=True)

        bb["rnorm"] = N("wstg0")

        Y_(lambda e: e.dma_start(out=ropeC[:, :], in_=d_ropeC[:, :]), w=["ropeC"])
        Y_(lambda e: e.dma_start(out=ropeS[:, :], in_=d_ropeS[:, :]), w=["ropeS"])
        Y_(lambda e: e.dma_start(out=tri[:, :], in_=d_tri[:, :]), w=["tri"])
        Y_(lambda e: e.dma_start(out=eyef[:, :], in_=d_eyef[:, :]), w=["eyef"])
        Y_(lambda e: e.dma_start(out=onesp[:, :], in_=d_onesp[:, :]), w=["onesp"])
        Y_(lambda e: e.dma_start(out=onesr[:, :], in_=d_onesr[:, :]), w=["onesr"])
        Y_(lambda e: e.dma_start(out=onehs[:, :], in_=d_oneh[:, :]), w=["oneh"])
        Y_(lambda e: e.dma_start(out=jrev[:, :], in_=d_jrev[:, :]), w=["jrev"])
        Y_(lambda e: e.dma_start(out=permm[:, :], in_=d_perm[:, :]), w=["permm"])
        Y_(lambda e: e.dma_start(out=rel8s[:, :], in_=d_rel8[:, :]), w=["rel8"])
        bslot = {}
        si = 0
        for l in (0, 1):
            for bn in ("bq", "bk", "bp", "bf"):
                bslot[l, bn] = si * NCH
                Y_(lambda e, l=l, bn=bn, s0=si * NCH: e.dma_start(
                    out=bcols[:, s0:s0 + NCH], in_=Bd[l, bn][:, :]), w=["bcols"])
                si += 1
        Gd(lambda e: e.dma_start(
            out=bass.AP(Xrs, 0, [[NCH * TOK, P], [TOK, NCH], [1, TOK]]),
            in_=bass.AP(x0T, 0, [[TOK, P], [P * TOK, NCH], [1, TOK]])), w=["Xrs"])
        Gc(lambda e: e.memset(onesF[:, :], 1.0), w=["onesF"])
        Gc(lambda e: e.memset(onesrb[:, :], 1.0), w=["onesrb"])
        Gc(lambda e: e.memset(statc[:, :], 0.0), w=["statc"])
        Gc(lambda e: e.memset(sc1[:, :], 0.0), w=["sc1"])
        Gc(lambda e: e.memset(row_m[:, :], 0.0), w=["rows"])
        Gc(lambda e: e.memset(row_q2[:, :], 0.0), w=["rows"])
        Gc(lambda e: e.memset(row_i[:, :], 0.0), w=["rows"])
        Gc(lambda e: e.memset(rnorm[:, :], 0.0), w=["rnorm"])
        Gc(lambda e: e.memset(ebias[:, :], 0.0), w=["ebias"])
        Gc(lambda e: e.memset(scC[:, :], 0.0), w=["scC"])
        Gc(lambda e: e.memset(epsc[:, :], EPS), w=["epsc"])

        def load_W(l, w, slot):
            Gd(lambda e, l=l, w=w, slot=slot: e.dma_start(
                out=bass.AP(wsl[slot], 0, [[NCH * DM, P], [DM, NCH], [1, DM]]),
                in_=bass.AP(Wd[l, w], 0, [[DM, P], [P * DM, NCH], [1, DM]])),
               w=[f"wsl{slot}"])

        load_W(0, "q", 0)
        load_W(0, "k", 1)

        dynv = {}

        def dyn(e):
            if "a1q0" not in dynv:
                pid = e.partition_id()
                half = (pid & 1) * HDT
                for r in (0, 1):
                    dynv[f"a1q{r}"] = e.snap(r * AG1_N + half)
                    dynv[f"a1k{r}"] = e.snap(r * AG1_N + DM * TOK + half)
                    dynv[f"a1v{r}"] = e.snap(r * AG1_N + V_OFF + half)
                dynv["ot_ev"] = e.snap((pid & 1) * TOK)
                dynv["ot_od"] = e.snap(AG2_N + (pid & 1) * TOK)
            return dynv

        for l in (0, 1):
            a1i, a1o, a2i, a2o = ag1_in[l], ag1_out[l], ag2_in[l], ag2_out[l]
            A1I, A1O, A2I, A2O = f"ag1i{l}", f"ag1o{l}", f"ag2i{l}", f"ag2o{l}"

            rsrc, rsn = (x0T, "x0T") if l == 0 else (x1loc, "x1loc")

            def res_slab(st, src, srcn):
                sg = st % 2
                Gd(lambda e, st=st, sg=sg, src=src: e.dma_start(
                    out=bass.AP(zsl[sg], 0, [[NCH * ST, P], [ST, NCH], [1, ST]]),
                    in_=bass.AP(src, st * ST, [[TOK, P], [P * TOK, NCH], [1, ST]])),
                   r=[srcn], a=[f"zsl{sg}"])

            res_slab(0, rsrc, rsn)
            if NST > 1:
                res_slab(1, rsrc, rsn)

            if l == 0:
                T_(lambda e: e.matmul(ps_m[0:HO, 0:384], rel8s[:, :], onehs[:, :],
                                      start=True, stop=True),
                   r=["rel8", "oneh"], w=["ps_m"])
                V_(lambda e: e.tensor_copy(bandF[:, :], ps_m[0:HO, 0:384]),
                   r=["ps_m"], w=["bandF"])
                V_(lambda e: e.tensor_copy(statc[0:HO, 6:7], bandF[:, 350:351]),
                   r=["bandF"], w=["statc"])
                V_(lambda e: e.tensor_reduce(statc[0:HO, 7:8], bandF[:, :], AxX,
                                             Alu.max), r=["bandF"], w=["statc"])
                V_(lambda e: e.tensor_single_scalar(bandF[:, :], bandF[:, :],
                                                    bandF[:, 350:351], Alu.subtract),
                   r=["bandF"], w=["bandF"])
                S_(lambda e: e.activation(bandB[:, :], bandF[:, :], Act.Exp, scale=0.125),
                   r=["bandF"], w=["bandB"])
                Gd(lambda e: e.dma_start(out=band_dram[:, :], in_=bandB[:, :]),
                   r=["bandB"], w=["band_dram"])
                for h in range(HO):
                    for dst, dn, off in ((ediag, "ediag", 0), (eprev, "eprev", 128)):
                        Gd(lambda e, h=h, off=off: e.dma_start(
                            out=hank[:, :],
                            in_=bass.AP(band_dram, h * 384 + off, [[1, P], [1, P]])),
                           r=["band_dram"], w=["hank"])
                        T_(lambda e: e.matmul(ps_m[:, 0:P], hank[:, :], jrev[:, :],
                                              start=True, stop=True),
                           r=["hank", "jrev"], w=["ps_m"])
                        V_(lambda e, dst=dst, h=h: e.tensor_copy(
                            dst[:, h * P:(h + 1) * P], ps_m[:, 0:P]),
                           r=["ps_m"], a=[dn])
                for h in range(HO):
                    V_(lambda e, h=h: e.tensor_mul(ediag[:, h * P:(h + 1) * P],
                                                   ediag[:, h * P:(h + 1) * P],
                                                   tri[:, :]),
                       r=["ediag", "tri"], w=["ediag"])

            Gc(lambda e: e.memset(statc[:, 0:2], 0.0), w=["statc"])
            for wname, slot, bn, scol, agrow in (
                    ("q", 0, "bq", 0, 0), ("k", 1, "bk", 1, DM)):
                pend_rope = []

                def emit_rope(ic, st, scol, agrow, bn):
                    rb = st % 2
                    T_(lambda e, st=st: e.matmul(
                        ps_m[:, 0:ST], permm[:, :],
                        qst[0][:, st * ST:(st + 1) * ST],
                        start=True, stop=True),
                       r=["permm", "qst0"], w=["ps_m"])
                    V_(lambda e, st=st, rb=rb: e.tensor_mul(
                        rt1[rb][:, :], qst[0][:, st * ST:(st + 1) * ST],
                        ropeC[:, st * ST:(st + 1) * ST]),
                       r=["qst0", "ropeC"], w=[f"rt1{rb}"])
                    V_(lambda e, st=st, rb=rb: e.tensor_mul(
                        rt2[rb][:, :], ps_m[:, 0:ST],
                        ropeS[:, st * ST:(st + 1) * ST]),
                       r=["ps_m", "ropeS"], w=[f"rt2{rb}"])
                    V_(lambda e, rb=rb: e.tensor_add(
                        rt1[rb][:, :], rt1[rb][:, :], rt2[rb][:, :]),
                       r=[f"rt2{rb}"], w=[f"rt1{rb}"])
                    V_(lambda e, rb=rb: e.tensor_reduce(
                        statc[:, 2:3], rt1[rb][:, :], AxX, Alu.max,
                        apply_absolute_value=True),
                       r=[f"rt1{rb}"], w=["statc"])
                    V_(lambda e, scol=scol: e.tensor_tensor(
                        statc[:, scol:scol + 1], statc[:, scol:scol + 1],
                        statc[:, 2:3], Alu.max),
                       r=["statc"], w=["statc"])
                    Y_(lambda e, rb=rb, ic=ic, st=st, agrow=agrow, a1i=a1i: e.dma_start(
                        out=bass.AP(a1i, (agrow + ic * P) * TOK + st * ST,
                                    [[TOK, P], [1, ST]]),
                        in_=rt1[rb][:, :]),
                       r=[f"rt1{rb}"], a=[A1I])

                PBANKS = [(ps_a, 0, "ps_aL"), (ps_b, 0, "ps_bL"),
                          (ps_a, QS, "ps_aR"), (ps_b, QS, "ps_bR")]
                for ic in range(NCH):
                    for st in range(NST):
                        psum, po_, pn = PBANKS[(ic * NST + st) % 4]
                        for kc in range(NCH):
                            T_(lambda e, psum=psum, po_=po_, slot=slot, kc=kc, ic=ic, st=st:
                               e.matmul(psum[:, po_:po_ + ST],
                                        wsl[slot][:, kc * DM + ic * P:kc * DM + (ic + 1) * P],
                                        Xrs[:, kc * TOK + st * ST:kc * TOK + (st + 1) * ST],
                                        start=(kc == 0), stop=(kc == NCH - 1)),
                               r=[f"wsl{slot}", "Xrs"], w=[pn], inc=(kc == NCH - 1))
                        S_(lambda e, psum=psum, po_=po_, st=st, s0=bslot[l, bn], ic=ic:
                           e.add(qst[0][:, st * ST:(st + 1) * ST], psum[:, po_:po_ + ST],
                                 bcols[:, s0 + ic:s0 + ic + 1]),
                           r=[pn, "bcols"], a=["qst0"])
                        pend_rope.append(
                            lambda ic=ic, st=st, scol=scol, agrow=agrow, bn=bn:
                            emit_rope(ic, st, scol, agrow, bn))
                        while len(pend_rope) > (1 if NST > 1 else 0):
                            pend_rope.pop(0)()
                while pend_rope:
                    pend_rope.pop(0)()
                if wname == "q":
                    load_W(l, "v", 0)
                else:
                    load_W(l, "p", 1)

            Gd(lambda e, l=l: e.dma_start(out=bvrow[:, :], in_=Bd[l, "bv"][:, :]),
               w=["bvrow"])
            for j in range(DM // ST):
                T_(lambda e, j=j: e.matmul(ps_m[:, 0:ST], onesrb[:, :],
                                           bvrow[:, j * ST:(j + 1) * ST],
                                           start=True, stop=True),
                   r=["onesrb", "bvrow"], w=["ps_m"])
                V_(lambda e, j=j: e.tensor_copy(vbias[:, j * ST:(j + 1) * ST],
                                                ps_m[:, 0:ST]),
                   r=["ps_m"], w=["vbias"])
            for tt in range(TOK // P):
                for vc in range(DM // ST):
                    sg = (tt * (DM // ST) + vc) % 2
                    psum, po_, pn = PBANKS[(tt * (DM // ST) + vc) % 4]
                    for kc in range(NCH):
                        T_(lambda e, psum=psum, po_=po_, kc=kc, tt=tt, vc=vc: e.matmul(
                            psum[:, po_:po_ + ST],
                            Xrs[:, kc * TOK + tt * P:kc * TOK + (tt + 1) * P],
                            wsl[0][:, kc * DM + vc * ST:kc * DM + (vc + 1) * ST],
                            start=(kc == 0), stop=(kc == NCH - 1)),
                           r=["wsl0", "Xrs"], w=[pn], inc=(kc == NCH - 1))
                    V_(lambda e, psum=psum, po_=po_, sg=sg, vc=vc: e.tensor_add(
                        vstg[sg][:, :], psum[:, po_:po_ + ST],
                        vbias[:, vc * ST:(vc + 1) * ST]),
                       r=[pn, "vbias"], w=[f"vstg{sg}"])
                    Y_(lambda e, sg=sg, tt=tt, vc=vc, a1i=a1i: e.dma_start(
                        out=bass.AP(a1i, V_OFF + vc * (ST // D) * TOK * D + tt * P * D,
                                    [[D, P], [TOK * D, ST // D], [1, D]]),
                        in_=bass.AP(vstg[sg], 0, [[ST, P], [D, ST // D], [1, D]])),
                       r=[f"vstg{sg}"], a=[A1I])

            T_(lambda e: e.transpose(ps_m[0:2, 0:P], statc[:, 0:2], eyef[:, :]),
               r=["statc", "eyef"], w=["ps_m"])
            V_(lambda e: e.tensor_reduce(statc[0:2, 3:4], ps_m[0:2, 0:P], AxX, Alu.max),
               r=["ps_m"], w=["statc"])
            Y_(lambda e, a1i=a1i: e.dma_start(
                out=bass.AP(a1i, ST_OFF + 2, [[2046, 1], [1, 2046]]),
                in_=Xrs[0:1, 0:2046]),
               r=["Xrs"], a=[A1I])
            Gd(lambda e, a1i=a1i: e.dma_start(
                out=bass.AP(a1i, ST_OFF, [[1, 2], [1, 1]]), in_=statc[0:2, 3:4]),
               r=["statc"], a=[A1I])
            CC_(lambda e, l=l: e.collective_compute(
                "AllGather", Alu.bypass, replica_groups=PAIRS,
                ins=[ag1_in[l].ap().opt()], outs=[ag1_out[l].ap().opt()]),
                r=[A1I], w=[A1O])

            fcol = 0
            for ic in range(NCH):
                sg = ic % 2
                Y_(lambda e, l=l, ic=ic, sg=sg: e.dma_start(
                    out=wstg[sg][:, :],
                    in_=bass.AP(Wd[l, "f"], ic * P * DM, [[DM, P], [1, DM]])),
                   w=[f"wstg{sg}"])
                V_(lambda e, sg=sg, l=l: e.tensor_reduce(
                    statc[:, 4:5], wstg[sg][:, :], AxX,
                    Alu.add if l == 0 else Alu.max, apply_absolute_value=True),
                   r=[f"wstg{sg}"], w=["statc"])
                V_(lambda e, l=l: e.tensor_tensor(
                    statc[:, 5:6], statc[:, 5:6], statc[:, 4:5],
                    Alu.add if l == 0 else Alu.max),
                   r=["statc"], w=["statc"])
            T_(lambda e: e.transpose(ps_m[0:1, 0:P], statc[:, 5:6], eyef[:, :]),
               r=["statc", "eyef"], w=["ps_m"])
            V_(lambda e, l=l: e.tensor_reduce(sc1[0:1, 0:1], ps_m[0:1, 0:P], AxX,
                                              Alu.add if l == 0 else Alu.max),
               r=["ps_m"], w=["sc1"])
            Gc(lambda e: e.memset(statc[:, 5:6], 0.0), w=["statc"])
            if l == 0:
                V_(lambda e: e.tensor_scalar(sc1[0:1, 1:2], sc1[0:1, 0:1],
                                             0.5 / (DM * DM), None, Alu.mult),
                   r=["sc1"], w=["sc1"])
                V_(lambda e: e.tensor_scalar(sc1[0:1, 2:3], sc1[0:1, 1:2],
                                             -1.0, None, Alu.mult),
                   r=["sc1"], w=["sc1"])
                pairs = [(1, 0), (2, 1)]
            else:
                V_(lambda e: e.tensor_scalar(sc1[0:1, 1:2], sc1[0:1, 0:1],
                                             1e-12, None, Alu.max),
                   r=["sc1"], w=["sc1"])
                V_(lambda e: e.reciprocal(sc1[0:1, 1:2], sc1[0:1, 1:2]),
                   r=["sc1"], w=["sc1"])
                V_(lambda e: e.tensor_scalar(sc1[0:1, 1:2], sc1[0:1, 1:2],
                                             448.0, None, Alu.mult),
                   r=["sc1"], w=["sc1"])
                V_(lambda e: e.tensor_scalar(sc1[0:1, 2:3], sc1[0:1, 1:2],
                                             0.5, None, Alu.mult),
                   r=["sc1"], w=["sc1"])
                pairs = [(1, 0), (2, 1)]
            for (c_in, c_out) in pairs:
                T_(lambda e, c_in=c_in: e.matmul(ps_m[:, 0:1], onesr[:, :],
                                                 sc1[0:1, c_in:c_in + 1],
                                                 start=True, stop=True),
                   r=["onesr", "sc1"], w=["ps_m"])
                V_(lambda e, c_out=c_out: e.tensor_copy(scC[:, c_out:c_out + 1],
                                                        ps_m[:, 0:1]),
                   r=["ps_m"], w=["scC"])
            for oc in range(NCH):
                sg = oc % 2
                DQ = Y_ if oc % 2 == 0 else Gd
                DQ(lambda e, l=l, oc=oc, sg=sg: e.dma_start(
                    out=wstg[sg][:, :],
                    in_=bass.AP(Wd[l, "f"], oc * P * DM, [[DM, P], [1, DM]])),
                   w=[f"wstg{sg}"])
                for tc in range(NCH):
                    dst = (slice(None), slice(tc * DM + oc * P, tc * DM + (oc + 1) * P))
                    tb = (oc * NCH + tc) % 2
                    psq, pqn = (ps_m[:, 0:P], "ps_m") if tb == 0 else (
                        ps_o[:, 2 * QS:2 * QS + P], "ps_o2")
                    T_(lambda e, sg=sg, tc=tc, psq=psq: e.transpose(
                        psq, wstg[sg][:, tc * P:(tc + 1) * P], eyef[:, :]),
                       r=[f"wstg{sg}", "eyef"], w=[pqn])
                    if l == 0:
                        V_(lambda e, psq=psq: e.tensor_single_scalar(
                            mkb[:, 0:P], psq, scC[:, 0:1], Alu.is_gt),
                           r=[pqn, "scC"], w=["mkb"])
                        V_(lambda e, psq=psq: e.tensor_single_scalar(
                            bgb[:, 0:P], psq, scC[:, 1:2], Alu.is_lt),
                           r=[pqn, "scC"], w=["bgb"])
                        Gc(lambda e, dst=dst: e.tensor_tensor(
                            wsl[0][dst], mkb[:, 0:P], bgb[:, 0:P], Alu.subtract),
                           r=["mkb", "bgb"], a=["wsl0"])
                    else:
                        V_(lambda e, psq=psq: e.tensor_single_scalar(
                            f8c[:, 0:P], psq, scC[:, 1:2], Alu.mult),
                           r=[pqn, "scC"], w=["f8c"])
                        S_(lambda e: e.mul(bgb[:, 0:P], f8c[:, 0:P], 2.0),
                           r=["f8c"], w=["bgb"])
                        V_(lambda e, psq=psq: e.tensor_single_scalar(
                            tmpF[:, 0:P], psq, scC[:, 0:1], Alu.mult),
                           r=[pqn, "scC"], w=["tmpF"])
                        S_(lambda e: e.activation(
                            tmpF[:, P:2 * P], tmpF[:, 0:P], Act.Abs),
                           r=["tmpF"], w=["tmpF"])
                        V_(lambda e: e.tensor_scalar(
                            mkb[:, 0:P], tmpF[:, P:2 * P], FP8_SMALL, None,
                            Alu.is_ge),
                           r=["tmpF"], w=["mkb"])
                        V_(lambda e, dst=dst: e.tensor_scalar(
                            wsl[0][dst], tmpF[:, 0:P], C_Q9, C_Q9, Alu.add,
                            Alu.subtract),
                           r=["tmpF"], a=["wsl0"])
                        V_(lambda e, dst=dst: e.copy_predicated(
                            wsl[0][dst], mkb[:, 0:P], bgb[:, 0:P]),
                           r=["mkb", "bgb", "wsl0"], a=["wsl0"])

            Y_(lambda e, l=l: e.dma_start(
                out=bass.AP(g16, 0, [[16, 1], [2, 2], [1, 2]]),
                in_=bass.AP(ag1_out[l], ST_OFF, [[1, 1], [AG1_N, 2], [1, 2]])),
               r=[A1O], w=["g16"])
            V_(lambda e: e.tensor_reduce(sc1[0:1, 5:6],
                                         bass.AP(g16, 0, [[16, 1], [2, 2]]),
                                         AxX, Alu.max), r=["g16"], w=["sc1"])
            V_(lambda e: e.tensor_reduce(sc1[0:1, 6:7],
                                         bass.AP(g16, 1, [[16, 1], [2, 2]]),
                                         AxX, Alu.max), r=["g16"], w=["sc1"])
            V_(lambda e: e.tensor_tensor(sc1[0:1, 7:8], sc1[0:1, 5:6],
                                         sc1[0:1, 6:7], Alu.mult),
               r=["sc1"], w=["sc1"])
            V_(lambda e: e.tensor_scalar(sc1[0:1, 7:8], sc1[0:1, 7:8], 8.0,
                                         None, Alu.mult), r=["sc1"], w=["sc1"])
            if l == 0:
                T_(lambda e: e.transpose(ps_m[0:1, 0:P], statc[:, 7:8], eyef[:, :]),
                   r=["statc", "eyef"], w=["ps_m"])
                V_(lambda e: e.tensor_reduce(sc1[0:1, 4:5], ps_m[0:1, 0:HO], AxX,
                                             Alu.max), r=["ps_m"], w=["sc1"])
                V_(lambda e: e.tensor_scalar(sc1[0:1, 4:5], sc1[0:1, 4:5], 0.125,
                                             None, Alu.mult), r=["sc1"], w=["sc1"])
                V_(lambda e: e.tensor_tensor(sc1[0:1, 7:8], sc1[0:1, 7:8],
                                             sc1[0:1, 4:5], Alu.add),
                   r=["sc1"], w=["sc1"])
            V_(lambda e: e.tensor_scalar(sc1[0:1, 7:8], sc1[0:1, 7:8], M_CAP,
                                         None, Alu.min), r=["sc1"], w=["sc1"])
            T_(lambda e: e.matmul(ps_m[:, 0:1], onesr[:, :], sc1[0:1, 7:8],
                                  start=True, stop=True),
               r=["onesr", "sc1"], w=["ps_m"])
            V_(lambda e: e.tensor_copy(scC[:, 4:5], ps_m[:, 0:1]),
               r=["ps_m"], w=["scC"])
            if l == 0:
                T_(lambda e: e.transpose(ps_m[0:1, 0:P], statc[:, 6:7], eyef[:, :]),
                   r=["statc", "eyef"], w=["ps_m"])
                V_(lambda e: e.tensor_copy(sc1[0:1, 8:8 + HO], ps_m[0:1, 0:HO]),
                   r=["ps_m"], w=["sc1"])
                T_(lambda e: e.matmul(ps_m[:, 0:HO], onesr[:, :], sc1[0:1, 8:8 + HO],
                                      start=True, stop=True),
                   r=["onesr", "sc1"], w=["ps_m"])
                V_(lambda e: e.tensor_scalar(ebias[:, :], ps_m[:, 0:HO], 0.125,
                                             None, Alu.mult),
                   r=["ps_m"], w=["ebias"])
                V_(lambda e: e.tensor_single_scalar(ebias[:, :], ebias[:, :],
                                                    scC[:, 4:5], Alu.subtract),
                   r=["scC"], w=["ebias"])
            else:
                Gc(lambda e: e.memset(ebias[:, :], 0.0), w=["ebias"])
                V_(lambda e: e.tensor_single_scalar(ebias[:, :], ebias[:, :],
                                                    scC[:, 4:5], Alu.subtract),
                   r=["scC"], w=["ebias"])

            for (r, c, cn) in ((0, 0, "q"), (0, 1, "k"), (1, 0, "q"),
                               (1, 1, "k"), (0, 2, "v"), (1, 2, "v")):
                def fcp(e, r=r, c=c, cn=cn, l=l, a1o=a1o):
                    dv = dyn(e)
                    return e.dma_start(
                        out=qkvloc[l][0:1, (r * 3 + c) * HDT:(r * 3 + c + 1) * HDT],
                        in_=a1o[0:1, bass.ds(dv[f"a1{cn}{r}"], HDT)],
                        bounds_check=None)
                Y_(fcp, r=[A1O], a=[f"qkvloc{l}_{r}{cn}"])

            def head_dma(h):
                hb = h % 2
                rb0 = hb * D
                for r in (0, 1):
                    Y_(lambda e, h=h, rb0=rb0, r=r, l=l: e.dma_start(
                        out=qkt[rb0:rb0 + D, r * TOK:(r + 1) * TOK],
                        in_=bass.AP(qkvloc[l], r * 3 * HDT + h * D * TOK,
                                    [[TOK, D], [1, TOK]])),
                       r=[f"qkvloc{l}_{r}q"], a=[f"qkt{hb}"])
                    Y_(lambda e, h=h, rb0=rb0, r=r, l=l: e.dma_start(
                        out=qkt[rb0:rb0 + D, T + r * TOK:T + (r + 1) * TOK],
                        in_=bass.AP(qkvloc[l], (r * 3 + 1) * HDT + h * D * TOK,
                                    [[TOK, D], [1, TOK]])),
                       r=[f"qkvloc{l}_{r}k"], a=[f"qkt{hb}"])
                    Y_(lambda e, h=h, hb=hb, r=r, l=l: e.dma_start(
                        out=bass.AP(vph[hb], r * (TOK // P) * 65,
                                    [[65 * NKB, P], [65, TOK // P], [1, D]]),
                        in_=bass.AP(qkvloc[l], (r * 3 + 2) * HDT + h * TOK * D,
                                    [[D, P], [P * D, TOK // P], [1, D]])),
                       r=[f"qkvloc{l}_{r}v"], a=[f"vph{hb}"])

            for i in range(2):
                Gc(lambda e, i=i: e.memset(
                    bass.AP(vph[i], 64, [[65 * NKB, P], [65, NKB]]), 1.0),
                   r=[f"vph{i}"], w=[f"vph{i}"])
            head_dma(0)
            pend_av = []
            pend_norm = []

            def emit_S(h, hb, rb0, qs, g0, gsz, gb, psum, pn):
                for j in range(gsz):
                    kb = g0 + j
                    T_(lambda e, psum=psum, rb0=rb0, kb=kb, qs=qs, j=j:
                       e.matmul(psum[:, j * QS:(j + 1) * QS],
                                qkt[rb0:rb0 + D, T + kb * P:T + (kb + 1) * P],
                                qkt[rb0:rb0 + D, qs * QS:(qs + 1) * QS],
                                start=True, stop=True),
                       r=[f"qkt{hb}"], w=[pn + "LR"[j]], inc=(j == gsz - 1))
                S_(lambda e, psum=psum, gb=gb, gsz=gsz, h=h: e.activation(
                    atb[gb][:, 0:gsz * QS], psum[:, 0:gsz * QS], Act.Exp,
                    bias=ebias[:, h:h + 1], scale=0.125),
                   r=[pn + "L", pn + "R", "ebias"][0:gsz + 1] + ["ebias"],
                   w=[f"atb{gb}"])
                qb0 = qs * KPQ
                for j in range(gsz):
                    kb = g0 + j
                    t = kb - qb0
                    nz = min(max(t, 0), KPQ)
                    if nz > 0:
                        V_(lambda e, gb=gb, j=j, nz=nz: e.memset(
                            atb[gb][:, j * QS:j * QS + nz * P], 0.0),
                           w=[f"atb{gb}"])
                    if 0 <= t < KPQ:
                        if l == 0:
                            V_(lambda e, gb=gb, j=j, t=t, h=h: e.tensor_mul(
                                atb[gb][:, j * QS + t * P:j * QS + (t + 1) * P],
                                atb[gb][:, j * QS + t * P:j * QS + (t + 1) * P],
                                ediag[:, h * P:(h + 1) * P]),
                               r=["ediag"], w=[f"atb{gb}"])
                        else:
                            V_(lambda e, gb=gb, j=j, t=t: e.tensor_mul(
                                atb[gb][:, j * QS + t * P:j * QS + (t + 1) * P],
                                atb[gb][:, j * QS + t * P:j * QS + (t + 1) * P],
                                tri[:, :]),
                               r=["tri"], w=[f"atb{gb}"])
                    if l == 0 and 0 <= t + 1 < KPQ:
                        V_(lambda e, gb=gb, j=j, t=t, h=h: e.tensor_mul(
                            atb[gb][:, j * QS + (t + 1) * P:j * QS + (t + 2) * P],
                            atb[gb][:, j * QS + (t + 1) * P:j * QS + (t + 2) * P],
                            eprev[:, h * P:(h + 1) * P]),
                           r=["eprev"], w=[f"atb{gb}"])

            def emit_AV(hb, qs, g0, gsz, gb, nkb, po):
                for j in range(gsz):
                    kb = g0 + j
                    T_(lambda e, hb=hb, kb=kb, gb=gb, j=j, nkb=nkb, po=po:
                       e.matmul(ps_o[0:65, po * QS:po * QS + QS],
                                vph[hb][:, kb * 65:(kb + 1) * 65],
                                atb[gb][:, j * QS:(j + 1) * QS],
                                start=(kb == 0), stop=(kb == nkb - 1)),
                       r=[f"vph{hb}", f"atb{gb}"], w=[f"ps_o{po}"],
                       inc=(kb == nkb - 1))

            def emit_norm(h, hb, qs, po):
                V_(lambda e, po=po: e.tensor_scalar(
                    rnorm[64:65, 0:QS], ps_o[64:65, po * QS:po * QS + QS],
                    1e-35, None, Alu.max),
                   r=[f"ps_o{po}"], w=["rnorm"])
                V_(lambda e: e.reciprocal(rnorm[64:65, 0:QS],
                                          rnorm[64:65, 0:QS]),
                   r=["rnorm"], w=["rnorm"])
                T_(lambda e: e.matmul(ps_m[0:D, 0:QS], onesF[D:D + 1, :],
                                      rnorm[64:65, 0:QS], start=True, stop=True),
                   r=["onesF", "rnorm"], w=["ps_m"])
                V_(lambda e: e.tensor_copy(rnorm[0:D, 0:QS], ps_m[0:D, 0:QS]),
                   r=["ps_m"], w=["rnorm"])
                ob = qs % 2
                V_(lambda e, ob=ob, po=po: e.tensor_mul(
                    ot_sm[ob][:, :], ps_o[0:D, po * QS:po * QS + QS],
                    rnorm[0:D, 0:QS]),
                   r=[f"ps_o{po}", "rnorm"], w=[f"ot_sm{ob}"])
                Y_(lambda e, ob=ob, h=h, qs=qs, a2i=a2i: e.dma_start(
                    out=bass.AP(a2i, h * D * T + qs * QS, [[T, D], [1, QS]]),
                    in_=ot_sm[ob][:, :]),
                   r=[f"ot_sm{ob}"], a=[A2I])

            NORM_LAG = 5

            def pop_av():
                av, fin, fpo = pend_av.pop(0)
                av()
                if fin is not None:
                    pend_norm.append((gctr + NORM_LAG, fpo, fin))

            def pop_norms(ready_only=True, po=None):
                while pend_norm:
                    rdy, npo, fn = pend_norm[0]
                    if ready_only and rdy > gctr and not (po is not None and npo == po):
                        break
                    pend_norm.pop(0)
                    fn()

            gctr = 0
            for h in range(HO):
                hb = h % 2
                rb0 = hb * D
                hctr = 0
                for qs in range(NQS):
                    po = qs % 3
                    nkb = KPQ * (qs + 1)
                    glist = list(_groups(nkb))
                    for gi, (g0, gsz) in enumerate(glist):
                        gb = gctr % 3
                        psum, pn = (ps_a, "ps_a") if gctr % 2 == 0 else (ps_b, "ps_b")
                        emit_S(h, hb, rb0, qs, g0, gsz, gb, psum, pn)
                        pop_norms(po=po if gi == 0 else None)
                        if len(pend_av) >= 2:
                            pop_av()
                        if hctr == 1 and h + 1 < HO:
                            head_dma(h + 1)
                        fin = (lambda h=h, hb=hb, qs=qs, po=po:
                               emit_norm(h, hb, qs, po)) if gi == len(glist) - 1 else None
                        pend_av.append(
                            (lambda hb=hb, qs=qs, g0=g0, gsz=gsz, gb=gb, nkb=nkb,
                             po=po: emit_AV(hb, qs, g0, gsz, gb, nkb, po), fin, po))
                        gctr += 1
                        hctr += 1
            while pend_av:
                pop_av()
                gctr += 1
                pop_norms()
            pop_norms(ready_only=False)
            CC_(lambda e, l=l: e.collective_compute(
                "AllGather", Alu.bypass, replica_groups=PAIRS,
                ins=[ag2_in[l].ap().opt()], outs=[ag2_out[l][0:1, 0:2 * AG2_N].opt()]),
                r=[A2I], w=[A2O])

            Y_(lambda e, a2o=a2o: e.dma_start(
                out=a2o[0:1, 2 * AG2_N:2 * AG2_N + TOK], in_=Xrs[0:1, 0:TOK]),
               r=["Xrs"], a=[A2O])
            for r, reg in ((0, "ot_ev"), (1, "ot_od")):
                def fc2(e, r=r, reg=reg, l=l, a2o=a2o):
                    dv = dyn(e)
                    return e.dma_start(
                        out=ot2loc[l][0:1, r * (DM // 2) * T:(r + 1) * (DM // 2) * T],
                        in_=a2o[0:1, bass.ds(dv[reg], (DM // 2) * T)],
                        bounds_check=None)
                Y_(fc2, r=[A2O], a=[f"ot2loc{l}"])

            def pslice_dma(st, ic):
                sg = st % 2
                r = 0 if ic < NCH // 2 else 1
                row0 = (ic % (NCH // 2)) * P
                Y_(lambda e, st=st, ic=ic, sg=sg, r=r, row0=row0, l=l: e.dma_start(
                    out=ot_sl[sg][:, ic * ST:(ic + 1) * ST],
                    in_=bass.AP(ot2loc[l], r * (DM // 2) * T + row0 * T + st * ST,
                                [[T, P], [1, ST]])),
                   r=[f"ot2loc{l}"], a=[f"ot_sl{sg}"])

            QW = min(TOK, HSC)
            NQH = TOK // QW

            def ln_sums(st):
                sb = st % 2
                zs, zn = zsl[sb], f"zsl{sb}"
                for ic in range(NCH):
                    S_(lambda e, zs=zs, zn=zn, ic=ic: e.activation(
                        tmpF[:, ST:2 * ST], zs[:, ic * ST:(ic + 1) * ST],
                        Act.Square),
                       r=[zn], w=["tmpF"])
                    T_(lambda e, zs=zs, ic=ic: e.matmul(
                        ps_o[0:1, 0:ST], onesp[:, :], zs[:, ic * ST:(ic + 1) * ST],
                        start=(ic == 0), stop=(ic == NCH - 1)),
                       r=["onesp", zn], w=["ps_o0"], inc=True)
                    T_(lambda e, ic=ic: e.matmul(
                        ps_m[0:1, 0:ST], onesp[:, :], tmpF[:, ST:2 * ST],
                        start=(ic == 0), stop=(ic == NCH - 1)),
                       r=["onesp", "tmpF"], w=["ps_m"], inc=True)
                V_(lambda e: e.tensor_scalar(row_m[:, :], ps_o[0:1, 0:ST],
                                             1.0 / DM, None, Alu.mult),
                   r=["ps_o0"], w=["rows"])
                V_(lambda e: e.tensor_scalar(row_q2[:, :], ps_m[0:1, 0:ST],
                                             1.0 / DM, None, Alu.mult),
                   r=["ps_m"], w=["rows"])
                V_(lambda e: e.tensor_mul(x1c[0:1, ST:2 * ST], row_m[:, :],
                                          row_m[:, :]),
                   r=["rows"], w=["x1c"])
                V_(lambda e: e.tensor_tensor(x1c[0:1, ST:2 * ST], row_q2[:, :],
                                             x1c[0:1, ST:2 * ST], Alu.subtract),
                   r=["rows", "x1c"], w=["x1c"])
                S_(lambda e: e.activation(row_i[:, :], x1c[0:1, ST:2 * ST], Act.Ln,
                                          bias=epsc[0:1, 0:1]),
                   r=["x1c", "epsc"], w=["rows"])
                S_(lambda e: e.activation(row_i[:, :], row_i[:, :], Act.Exp,
                                          scale=-0.5), r=["rows"], w=["rows"])

            def ln_apply(st, spill_x1=False, amax=False, write_out=False):
                sb = st % 2
                zs, zn = zsl[sb], f"zsl{sb}"
                T_(lambda e: e.matmul(ps_o[:, QS:QS + ST], onesr[:, :],
                                      row_m[:, :], start=True, stop=True),
                   r=["onesr", "rows"], w=["ps_o1"])
                T_(lambda e: e.matmul(ps_m[:, 0:ST], onesr[:, :], row_i[:, :],
                                      start=True, stop=True),
                   r=["onesr", "rows"], w=["ps_m"])
                for ic in range(NCH):
                    V_(lambda e, zs=zs, zn=zn, ic=ic: e.tensor_tensor(
                        tmpF[:, 0:ST], zs[:, ic * ST:(ic + 1) * ST],
                        ps_o[:, QS:QS + ST], Alu.subtract),
                       r=[zn, "ps_o1"], w=["tmpF"])
                    V_(lambda e, zs=zs, zn=zn, ic=ic: e.tensor_tensor(
                        zs[:, ic * ST:(ic + 1) * ST], tmpF[:, 0:ST],
                        ps_m[:, 0:ST], Alu.mult),
                       r=["tmpF", "ps_m"], a=[zn])
                    Gc(lambda e, zs=zs, zn=zn, ic=ic, st=st: e.tensor_copy(
                        Xrs[:, ic * TOK + st * ST:ic * TOK + (st + 1) * ST],
                        zs[:, ic * ST:(ic + 1) * ST]),
                       r=[zn], a=["Xrs"])
                if spill_x1:
                    Y_(lambda e, zs=zs, st=st: e.dma_start(
                        out=bass.AP(x1loc, st * ST,
                                    [[TOK, P], [P * TOK, NCH], [1, ST]]),
                        in_=bass.AP(zs, 0, [[NCH * ST, P], [ST, NCH], [1, ST]])),
                       r=[zn], a=["x1loc"])
                if amax:
                    V_(lambda e, zs=zs: e.tensor_reduce(
                        statc[:, 4:5], zs[:, :], AxX, Alu.max,
                        apply_absolute_value=True),
                       r=[zn], w=["statc"])
                    V_(lambda e: e.tensor_tensor(statc[:, 5:6], statc[:, 5:6],
                                                 statc[:, 4:5], Alu.max),
                       r=["statc"], w=["statc"])
                if write_out:
                    Y_(lambda e, zs=zs, st=st: e.dma_start(
                        out=bass.AP(o_xT, st * ST,
                                    [[TOK, P], [P * TOK, NCH], [1, ST]]),
                        in_=bass.AP(zs, 0, [[NCH * ST, P], [ST, NCH], [1, ST]])),
                       r=[zn], w=["o_xT"] if st == NST - 1 else [], a=[] if st == NST - 1 else ["o_xT"])

            Gc(lambda e: e.memset(statc[:, 5:6], 0.0), w=["statc"])
            pend_apply = []
            for ic in range(NCH):
                pslice_dma(0, ic)
            for st in range(NST):
                sg = st % 2
                if st + 1 < NST:
                    for ic in range(NCH):
                        pslice_dma(st + 1, ic)
                for oc in range(NCH):
                    psum, po_, pn = PBANKS[oc % 4]
                    for ic in range(NCH):
                        T_(lambda e, psum=psum, po_=po_, ic=ic, oc=oc, sg=sg: e.matmul(
                            psum[:, po_:po_ + ST],
                            wsl[1][:, ic * DM + oc * P:ic * DM + (oc + 1) * P],
                            ot_sl[sg][:, ic * ST:(ic + 1) * ST],
                            start=(ic == 0), stop=(ic == NCH - 1)),
                           r=["wsl1", f"ot_sl{sg}"], w=[pn], inc=(ic == NCH - 1))
                    V_(lambda e, psum=psum, po_=po_, oc=oc, sg=sg,
                       s0=bslot[l, "bp"]: e.scalar_tensor_tensor(
                           zsl[sg][:, oc * ST:(oc + 1) * ST],
                           psum[:, po_:po_ + ST], bcols[:, s0 + oc:s0 + oc + 1],
                           zsl[sg][:, oc * ST:(oc + 1) * ST],
                           Alu.add, Alu.add),
                       r=[pn, "bcols", f"zsl{sg}"], a=[f"zsl{sg}"])
                    if oc == 2 and pend_apply:
                        pend_apply.pop(0)()
                ln_sums(st)
                pend_apply.append(
                    lambda st=st: (ln_apply(st, spill_x1=True, amax=True),
                                   res_slab(st + 2, rsrc, rsn)
                                   if st + 2 < NST else None))
            while pend_apply:
                pend_apply.pop(0)()

            T_(lambda e: e.transpose(ps_m[0:1, 0:P], statc[:, 5:6], eyef[:, :]),
               r=["statc", "eyef"], w=["ps_m"])
            V_(lambda e: e.tensor_reduce(sc1[0:1, 3:4], ps_m[0:1, 0:P], AxX, Alu.max),
               r=["ps_m"], w=["sc1"])
            Y_(lambda e, l=l: e.dma_start(out=ar_in[l][0:1, 0:8],
                                          in_=sc1[0:1, 3:11]),
               r=["sc1"], w=[f"ari{l}"])
            CC_(lambda e, l=l: e.collective_compute(
                "AllReduce", Alu.max, replica_groups=[list(range(8))],
                ins=[ar_in[l].ap().opt()], outs=[ar_out[l].ap().opt()]),
                r=[f"ari{l}"], w=[f"aro{l}"])
            Y_(lambda e, l=l: e.dma_start(out=sc1[0:1, 4:5],
                                          in_=ar_out[l][0:1, 0:1]),
               r=[f"aro{l}"], w=["sc1"])
            if l == 0:
                V_(lambda e: e.reciprocal(sc1[0:1, 9:10], sc1[0:1, 4:5]),
                   r=["sc1"], w=["sc1"])
                V_(lambda e: e.tensor_scalar(sc1[0:1, 9:10], sc1[0:1, 9:10],
                                             127.0, None, Alu.mult),
                   r=["sc1"], w=["sc1"])
                V_(lambda e: e.tensor_scalar(sc1[0:1, 10:11], sc1[0:1, 4:5],
                                             1.0 / 127.0, None, Alu.mult),
                   r=["sc1"], w=["sc1"])
                V_(lambda e: e.tensor_scalar(sc1[0:1, 11:12], sc1[0:1, 0:1],
                                             1.0 / (DM * DM), None, Alu.mult),
                   r=["sc1"], w=["sc1"])
                V_(lambda e: e.tensor_tensor(sc1[0:1, 10:11], sc1[0:1, 10:11],
                                             sc1[0:1, 11:12], Alu.mult),
                   r=["sc1"], w=["sc1"])
                pairs2 = [(9, 2), (10, 5)]
            else:
                V_(lambda e: e.tensor_scalar(sc1[0:1, 9:10], sc1[0:1, 4:5],
                                             1e-12, None, Alu.max),
                   r=["sc1"], w=["sc1"])
                V_(lambda e: e.reciprocal(sc1[0:1, 9:10], sc1[0:1, 9:10]),
                   r=["sc1"], w=["sc1"])
                V_(lambda e: e.tensor_scalar(sc1[0:1, 9:10], sc1[0:1, 9:10],
                                             448.0, None, Alu.mult),
                   r=["sc1"], w=["sc1"])
                V_(lambda e: e.tensor_scalar(sc1[0:1, 10:11], sc1[0:1, 9:10],
                                             0.5, None, Alu.mult),
                   r=["sc1"], w=["sc1"])
                V_(lambda e: e.tensor_tensor(sc1[0:1, 11:12], sc1[0:1, 9:10],
                                             sc1[0:1, 1:2], Alu.mult),
                   r=["sc1"], w=["sc1"])
                V_(lambda e: e.reciprocal(sc1[0:1, 11:12], sc1[0:1, 11:12]),
                   r=["sc1"], w=["sc1"])
                pairs2 = [(9, 2), (10, 3), (11, 5)]
            for (c_in, c_out) in pairs2:
                T_(lambda e, c_in=c_in: e.matmul(
                    ps_m[:, 0:1], onesr[:, :], sc1[0:1, c_in:c_in + 1],
                    start=True, stop=True), r=["onesr", "sc1"], w=["ps_m"])
                V_(lambda e, c_out=c_out: e.tensor_copy(
                    scC[:, c_out:c_out + 1], ps_m[:, 0:1]),
                   r=["ps_m"], w=["scC"])
            res_slab(0, x1loc, "x1loc")
            if NST > 1:
                res_slab(1, x1loc, "x1loc")
            xq_map = []
            caps = [("ot_sl0", ot_sl[0], NCH * ST), ("ot_sl1", ot_sl[1], NCH * ST),
                    ("qst0", qst[0], TOK), ("xqb", xqb, TOK),
                    ("vph0", vph[0], TOK), ("vph1", vph[1], TOK)]
            ci_, off_ = 0, 0
            for ic in range(NCH):
                while caps[ci_][2] - off_ < TOK:
                    ci_, off_ = ci_ + 1, 0
                xq_map.append((caps[ci_][0], caps[ci_][1], off_))
                off_ += TOK
            def quant_tile(st, ic):
                sg = st % 2
                nm, tns, off = xq_map[ic]
                zt = lambda: zsl[sg][:, ic * ST:(ic + 1) * ST]
                if l == 0:
                    V_(lambda e, zt=zt: e.tensor_scalar(
                        x1c[:, 0:ST], zt(), scC[:, 2:3], C_RND, Alu.mult, Alu.add),
                       r=[f"zsl{sg}", "scC"], w=["x1c"])
                    V_(lambda e, tns=tns, off=off, st=st: e.tensor_scalar(
                        tns[:, off + st * ST:off + (st + 1) * ST],
                        x1c[:, 0:ST], C_RND, None, Alu.subtract),
                       r=["x1c"], a=[nm])
                else:
                    S_(lambda e, zt=zt: e.mul(x1c[:, 0:ST], zt(), scC[:, 2:3]),
                       r=[f"zsl{sg}", "scC"], w=["x1c"])
                    V_(lambda e, zt=zt: e.tensor_single_scalar(
                        f8c[:, 0:ST], zt(), scC[:, 3:4], Alu.mult),
                       r=[f"zsl{sg}", "scC"], w=["f8c"])
                    S_(lambda e: e.mul(bgb[:, 0:ST], f8c[:, 0:ST], 2.0),
                       r=["f8c"], w=["bgb"])
                    S_(lambda e: e.activation(
                        x1c[:, ST:2 * ST], x1c[:, 0:ST], Act.Abs),
                       r=["x1c"], w=["x1c"])
                    V_(lambda e: e.tensor_scalar(
                        mkb[:, 0:ST], x1c[:, ST:2 * ST], FP8_SMALL, None,
                        Alu.is_ge),
                       r=["x1c"], w=["mkb"])
                    V_(lambda e, tns=tns, off=off, st=st: e.tensor_scalar(
                        tns[:, off + st * ST:off + (st + 1) * ST],
                        x1c[:, 0:ST], C_Q9, C_Q9, Alu.add, Alu.subtract),
                       r=["x1c"], a=[nm])
                    V_(lambda e, tns=tns, off=off, st=st: e.copy_predicated(
                        tns[:, off + st * ST:off + (st + 1) * ST],
                        mkb[:, 0:ST], bgb[:, 0:ST]),
                       r=["mkb", "bgb", nm], a=[nm])

            for st in range(NST):
                sg = st % 2
                for ic in range(NCH):
                    quant_tile(st, ic)
                for oc in range(NCH):
                    psum, po_, pn = PBANKS[oc % 4]
                    for ic in range(NCH):
                        nm, tns, off = xq_map[ic]
                        T_(lambda e, psum=psum, po_=po_, ic=ic, oc=oc, st=st, tns=tns,
                           off=off: e.matmul(
                               psum[:, po_:po_ + ST],
                               wsl[0][:, ic * DM + oc * P:ic * DM + (oc + 1) * P],
                               tns[:, off + st * ST:off + (st + 1) * ST],
                               start=(ic == 0), stop=(ic == NCH - 1)),
                           r=["wsl0", nm], w=[pn], inc=(ic == NCH - 1))
                    V_(lambda e, psum=psum, po_=po_, oc=oc, s0=bslot[l, "bf"]:
                       e.tensor_scalar(
                           tmpG[:, :], psum[:, po_:po_ + ST], scC[:, 5:6],
                           bcols[:, s0 + oc:s0 + oc + 1], Alu.mult, Alu.add),
                       r=[pn, "scC", "bcols"], w=["tmpG"])
                    V_(lambda e, sg=sg, oc=oc: e.tensor_add(
                        zsl[sg][:, oc * ST:(oc + 1) * ST], tmpG[:, :],
                        zsl[sg][:, oc * ST:(oc + 1) * ST]),
                       r=["tmpG", f"zsl{sg}"], a=[f"zsl{sg}"])
                    if oc == 2 and pend_apply:
                        pend_apply.pop(0)()
                ln_sums(st)
                pend_apply.append(
                    lambda st=st: (ln_apply(st, spill_x1=(l == 0),
                                            write_out=(l == 1)),
                                   res_slab(st + 2, x1loc, "x1loc")
                                   if st + 2 < NST else None))
            while pend_apply:
                pend_apply.pop(0)()
            if l == 0:
                load_W(1, "q", 0)
                load_W(1, "k", 1)

        pr.op("sync", lambda e: e.nop(), [N("o_xT")], [], dma=False)

        pr.seal()
        nc._pr = pr
        with nc.Block() as block:
            @block.tensor
            def _(e):
                pr.emit("tensor", e, sems)

            @block.vector
            def _(e):
                pr.emit("vector", e, sems)

            @block.scalar
            def _(e):
                pr.emit("scalar", e, sems)

            @block.gpsimd
            def _(e):
                pr.emit("gpsimd", e, sems)

            @block.sync
            def _(e):
                pr.emit("sync", e, sems)

    return nc


def _deinterleave_perm():
    p = np.zeros(DM, np.int64)
    for h in range(NH):
        p[h * D:h * D + D // 2] = h * D + 2 * np.arange(D // 2)
        p[h * D + D // 2:(h + 1) * D] = h * D + 2 * np.arange(D // 2) + 1
    return p


def _bias_cols(b):
    return np.ascontiguousarray(b.reshape(NCH, P).T.astype(np.float32))


def kernel(L0_attn_x, L0_attn_Wq, L0_attn_bq, L0_attn_Wk, L0_attn_bk, L0_attn_Wv,
           L0_attn_bv, L0_attn_Wp, L0_attn_bp, L0_attn_rel_bias_table,
           L0_ffn_weight, L0_ffn_bias, L1_attn_Wq, L1_attn_bq, L1_attn_Wk,
           L1_attn_bk, L1_attn_Wv, L1_attn_bv, L1_attn_Wp, L1_attn_bp,
           L1_ffn_weight, L1_ffn_bias):
    perm = _deinterleave_perm()
    W = {
        (0, "q"): L0_attn_Wq, (0, "k"): L0_attn_Wk, (0, "v"): L0_attn_Wv,
        (0, "p"): L0_attn_Wp, (0, "f"): L0_ffn_weight,
        (1, "q"): np.ascontiguousarray(L1_attn_Wq[:, perm]),
        (1, "k"): np.ascontiguousarray(L1_attn_Wk[:, perm]),
        (1, "v"): L1_attn_Wv, (1, "p"): L1_attn_Wp, (1, "f"): L1_ffn_weight,
    }
    bq1 = np.asarray(L1_attn_bq)[perm]
    bk1 = np.asarray(L1_attn_bk)[perm]
    Bc = {
        (0, "bq"): _bias_cols(np.asarray(L0_attn_bq)),
        (0, "bk"): _bias_cols(np.asarray(L0_attn_bk)),
        (0, "bp"): _bias_cols(np.asarray(L0_attn_bp)),
        (0, "bf"): _bias_cols(np.asarray(L0_ffn_bias)),
        (1, "bq"): _bias_cols(bq1),
        (1, "bk"): _bias_cols(bk1),
        (1, "bp"): _bias_cols(np.asarray(L1_attn_bp)),
        (1, "bf"): _bias_cols(np.asarray(L1_ffn_bias)),
    }
    bv = {0: np.asarray(L0_attn_bv).reshape(1, DM).astype(np.float32),
          1: np.asarray(L1_attn_bv).reshape(1, DM).astype(np.float32)}
    relt = np.asarray(L0_attn_rel_bias_table).astype(np.float32)

    nc = build_program()
    in_maps = []
    for c in range(8):
        b, half = c // 2, c % 2
        cons = host_constants(half)
        m = {
            "x0T": np.ascontiguousarray(
                np.asarray(L0_attn_x)[b, half * TOK:(half + 1) * TOK, :].T
            ).astype(np.float32),
            "rel8": np.ascontiguousarray(relt[:, half * HO:(half + 1) * HO]),
            "d_ropeC": cons["ropeC"], "d_ropeS": cons["ropeS"],
            "d_tri": cons["tri01"],
            "d_eyef": cons["eye_f32"], "d_onesp": cons["ones_p"],
            "d_jrev": cons["jrev"], "d_perm": cons["perm"],
            "d_onesr": cons["ones_r"], "d_oneh": cons["oneh"],
        }
        for l in (0, 1):
            for w in ("q", "k", "v", "p", "f"):
                m[f"W{l}{w}"] = np.ascontiguousarray(W[l, w], dtype=np.float32)
            for bn in ("bq", "bk", "bp", "bf"):
                m[f"B{l}{bn}"] = Bc[l, bn]
            m[f"B{l}bv"] = bv[l]
        in_maps.append(m)

    res = run_bass_kernel_spmd(nc, in_maps, core_ids=list(range(8)))
    out = np.empty((B, T, DM), np.float32)
    for c in range(8):
        b, half = c // 2, c % 2
        out[b, half * TOK:(half + 1) * TOK, :] = np.asarray(
            res.results[c]["o_xT"]).T
    return out


# version 27
# speedup vs baseline: 1.1144x; 1.1144x over previous
import numpy as np
import ml_dtypes
import concourse.bass as bass
import concourse.mybir as mybir
from concourse.bass_utils import run_bass_kernel_spmd

F32 = mybir.dt.float32
BF16 = mybir.dt.bfloat16
FP8 = mybir.dt.float8e4
U8 = mybir.dt.uint8
I16 = mybir.dt.int16
Act = mybir.ActivationFunctionType
Alu = mybir.AluOpType
AxX = mybir.AxisListType.X

SCALE = 1
B = 4
T = 4096 // SCALE
DM = 1024
NH = 16
D = 64
P = 128
TOK = T // 2
NCH = DM // P
ST = min(512, TOK)
NST = TOK // ST
QS = min(512, TOK)
NQS = T // QS
KPQ = QS // P
NKB = T // P
HO = NH // 2
SG = 2
C_RND = 12582912.0
C_Q9 = 24576.0
FP8_SMALL = 0.03125
EPS = 1e-5
M_CAP = 30.0
DBG = 0

V_OFF = 2 * DM * TOK
ST_OFF = V_OFF + NH * TOK * D
AG1_N = ST_OFF + 2048
AG2_N = (DM // 2) * T
SPAN = V_OFF + (NH // 2) * TOK * D
HDT = (NH // 2) * TOK * D
PAIRS = [[0, 1], [2, 3], [4, 5], [6, 7]]

ENGINES = ("tensor", "vector", "scalar", "gpsimd", "sync")


class Buf:
    __slots__ = ("name", "writers", "readers")

    def __init__(self, name):
        self.name, self.writers, self.readers = name, [], []


class Prog:
    def __init__(self):
        self.ops = {e: [] for e in ENGINES}

    def op(self, eng, fn, reads=(), writes=(), appends=(), inc=True, dma=False,
           cc=False):
        idx = len(self.ops[eng])
        ev = (eng, idx)
        waits = set()
        for b in reads:
            waits.update(b.writers)
        for b in writes:
            waits.update(b.writers)
            waits.update(b.readers)
        for b in appends:
            waits.update(b.readers)
        waits.discard(ev)
        self.ops[eng].append(dict(fn=fn, inc=inc, dma=dma, cc=cc, waits=waits))
        for b in writes:
            b.writers = [ev]
            b.readers = []
        for b in appends:
            b.writers.append(ev)
        for b in reads:
            if not b.readers or b.readers[-1] != ev:
                b.readers.append(ev)

    NDQ = 8

    def seal(self):
        self.cum, self.cover, self.covidx, self.key = {}, {}, {}, {}
        for e in ENGINES:
            ops = self.ops[e]
            if ops and not ops[-1]["inc"]:
                ops[-1]["inc"] = True
            c = 0
            dslot = {}
            ndma = 0
            counts, keys, pre = [], [], []
            for o in ops:
                if o["dma"]:
                    sl = ndma % self.NDQ
                    ndma += 1
                    pre.append((("dma", e, sl), dslot.get(sl, 0)))
                    dslot[sl] = dslot.get(sl, 0) + 16
                    counts.append(dslot[sl])
                    keys.append(("dma", e, sl))
                elif o["cc"]:
                    ccc = getattr(self, "_ccc", {}).get(e, 0) + 1
                    self._ccc = getattr(self, "_ccc", {})
                    self._ccc[e] = ccc
                    pre.append(None)
                    counts.append(ccc)
                    keys.append(("cc", e))
                else:
                    if o["inc"]:
                        c += 1
                    pre.append(None)
                    counts.append(c)
                    keys.append(("c", e))
            self.cum[e], self.key[e] = counts, keys
            self.pre = getattr(self, "pre", {})
            self.pre[e] = pre
            cov, ci = [None] * len(ops), [None] * len(ops)
            nxt = nxi = None
            for i in range(len(ops) - 1, -1, -1):
                if ops[i]["inc"] and not ops[i]["dma"] and not ops[i]["cc"]:
                    nxt, nxi = counts[i], i
                if ops[i]["dma"] or ops[i]["cc"]:
                    cov[i], ci[i] = counts[i], i
                else:
                    cov[i], ci[i] = nxt, nxi
            self.cover[e], self.covidx[e] = cov, ci

    def emit(self, e, eng_obj, sems):
        watermark = {}

        def wait(key, val):
            if val > watermark.get(key, 0):
                eng_obj.wait_ge(sems[key], val)
                watermark[key] = val

        for i, o in enumerate(self.ops[e]):
            tgt = {}
            for (we, wi) in sorted(o["waits"]):
                ci = self.covidx[we][wi]
                assert ci is not None, f"{e}:{i} dep on {we}:{wi} has no covering inc"
                if we == e and ci >= i and not self.ops[we][wi]["dma"]                         and not self.ops[we][wi]["cc"]:
                    continue
                key = self.key[we][ci]
                val = self.cover[we][wi]
                if val > tgt.get(key, 0):
                    tgt[key] = val
            for key in sorted(tgt, key=str):
                wait(key, tgt[key])
            if self.pre[e][i] is not None:
                wait(*self.pre[e][i])
            inst = o["fn"](eng_obj)
            if o["inc"]:
                inst.then_inc(sems[self.key[e][i]], 16 if o["dma"] else 1)


def _bucket_np(n):
    import math
    NBUK, MAXD = 32, 128
    n = np.asarray(n)
    me = NBUK // 2
    small = n < me
    ns = np.maximum(n, 1).astype(np.float64)
    vl = me + (np.log(ns / me) / math.log(MAXD / me) * (NBUK - me)).astype(np.int64)
    vl = np.minimum(vl, NBUK - 1)
    return np.where(small, n, vl)


def host_constants(half):
    cons = {}
    t = np.arange(half * TOK, half * TOK + TOK, dtype=np.float64)
    inv = 1.0 / (10000.0 ** (np.arange(0, D, 2, dtype=np.float64) / D))
    ang = t[None, :] * inv[:, None]
    c32, s32 = np.cos(ang), np.sin(ang)
    cons["ropeC"] = np.tile(c32, (4, 1)).astype(ml_dtypes.bfloat16)
    cons["ropeS"] = np.concatenate([-s32, s32, -s32, s32], 0).astype(ml_dtypes.bfloat16)
    kk = np.arange(P)[:, None]
    qq = np.arange(P)[None, :]
    cons["tri01"] = (kk <= qq).astype(ml_dtypes.bfloat16)
    cons["eye_bf"] = np.eye(P, dtype=ml_dtypes.bfloat16)
    cons["eye_f32"] = np.eye(P, dtype=np.float32)
    cons["ones_p"] = np.ones((P, 1), np.float32)
    cons["ones_r"] = np.ones((1, P), np.float32)
    pm = np.zeros((P, P), np.float32)
    for g_ in range(2):
        for j_ in range(32):
            a_, b_ = g_ * 64 + j_, g_ * 64 + 32 + j_
            pm[a_, b_] = 1.0
            pm[b_, a_] = 1.0
    cons["perm"] = pm.astype(ml_dtypes.bfloat16)
    jr = np.zeros((P, P), np.float32)
    jr[np.arange(P)[::-1], np.arange(P)] = 1.0
    cons["jrev"] = jr.astype(ml_dtypes.bfloat16)
    e = np.arange(384)
    bkt = _bucket_np(np.maximum(e - 127, 0))
    oneh = np.zeros((32, 384), np.float32)
    oneh[bkt, e] = 1.0
    cons["oneh"] = oneh
    return cons


def _groups(nkb):
    g, j = [], 0
    while j < nkb:
        g.append((j, min(SG, nkb - j)))
        j += SG
    return g


def build_program():
    import contextlib
    nc = bass.Bass("TRN2", target_bir_lowering=False, debug=False)
    pr = Prog()

    x0T = nc.dram_tensor("x0T", [DM, TOK], F32, kind="ExternalInput")
    o_xT = nc.dram_tensor("o_xT", [DM, TOK], F32, kind="ExternalOutput")
    Wd, Bd = {}, {}
    for l in (0, 1):
        for w in ("q", "k", "v", "p", "f"):
            Wd[l, w] = nc.dram_tensor(f"W{l}{w}", [DM, DM], F32, kind="ExternalInput")
        for bn in ("bq", "bk", "bp", "bf"):
            Bd[l, bn] = nc.dram_tensor(f"B{l}{bn}", [P, NCH], F32, kind="ExternalInput")
        Bd[l, "bv"] = nc.dram_tensor(f"B{l}bv", [1, DM], F32, kind="ExternalInput")
    d_rel8 = nc.dram_tensor("rel8", [32, HO], F32, kind="ExternalInput")
    d_ropeC = nc.dram_tensor("d_ropeC", [P, TOK], BF16, kind="ExternalInput")
    d_ropeS = nc.dram_tensor("d_ropeS", [P, TOK], BF16, kind="ExternalInput")
    d_tri = nc.dram_tensor("d_tri", [P, P], BF16, kind="ExternalInput")
    d_eyef = nc.dram_tensor("d_eyef", [P, P], F32, kind="ExternalInput")
    d_onesp = nc.dram_tensor("d_onesp", [P, 1], F32, kind="ExternalInput")
    d_onesr = nc.dram_tensor("d_onesr", [1, P], F32, kind="ExternalInput")
    d_oneh = nc.dram_tensor("d_oneh", [32, 384], F32, kind="ExternalInput")
    d_jrev = nc.dram_tensor("d_jrev", [P, P], BF16, kind="ExternalInput")
    d_perm = nc.dram_tensor("d_perm", [P, P], BF16, kind="ExternalInput")

    ag1_in, ag1_out, ag2_in, ag2_out, ar_in, ar_out = {}, {}, {}, {}, {}, {}
    for l in (0, 1):
        ag1_in[l] = nc.dram_tensor(f"ag1i{l}", [1, AG1_N], BF16)
        ag1_out[l] = nc.dram_tensor(f"ag1o{l}", [1, 2 * AG1_N], BF16)
        ag2_in[l] = nc.dram_tensor(f"ag2i{l}", [1, AG2_N], BF16)
        ag2_out[l] = nc.dram_tensor(f"ag2o{l}", [1, 2 * AG2_N + TOK], BF16)
        ar_in[l] = nc.dram_tensor(f"ari{l}", [1, 8], F32)
        ar_out[l] = nc.dram_tensor(f"aro{l}", [1, 8], F32)
    band_dram = nc.dram_tensor("band_dram", [HO, 384], BF16)
    x1loc = nc.dram_tensor("x1loc", [1, DM * TOK], F32)
    qkvloc, ot2loc = {}, {}
    for l in (0, 1):
        qkvloc[l] = nc.dram_tensor(f"qkvloc{l}", [1, 6 * HDT], BF16)
        ot2loc[l] = nc.dram_tensor(f"ot2loc{l}", [1, 2 * (DM // 2) * T], BF16)

    ctx = contextlib.ExitStack()

    def sbt(n, s, d):
        return ctx.enter_context(nc.sbuf_tensor(n, s, d))

    def pst(n, s):
        return ctx.enter_context(nc.psum_tensor(n, s, F32))

    with ctx:
        Xrs = sbt("Xrs", [P, NCH * TOK], BF16)
        zsl = [sbt(f"zsl{i}", [P, NCH * ST], F32) for i in range(2)]
        x1c = sbt("x1c", [P, 1024], F32)
        qst = [sbt("qst0", [P, TOK], BF16)]
        xqb = sbt("xqb", [P, TOK], BF16)
        rt1 = [sbt(f"rt1{i}", [P, ST], BF16) for i in range(2)]
        rt2 = [sbt(f"rt2{i}", [P, ST], BF16) for i in range(2)]
        vstg = [sbt(f"vstg{i}", [P, ST], BF16) for i in range(2)]
        qkt = sbt("qkt", [P, 2 * T], BF16)
        vph = [sbt(f"vph{i}", [P, 65 * NKB], BF16) for i in range(2)]
        atb = [sbt(f"atb{i}", [P, SG * QS], BF16) for i in range(3)]
        ot_sm = [sbt(f"ot_sm{i}", [D, QS], BF16) for i in range(2)]
        ot_sl = [sbt(f"ot_sl{i}", [P, NCH * ST], BF16) for i in range(2)]
        wsl = [sbt(f"wsl{i}", [P, NCH * DM], BF16) for i in range(2)]
        wstg = [sbt(f"wstg{i}", [P, DM], F32) for i in range(2)]
        SCR = max(TOK, DM)
        HSC = max(TOK // 2, DM)
        f8c = sbt("f8c", [P, HSC], FP8)
        mkb = sbt("mkb", [P, ST], I16)
        bgb = sbt("bgb", [P, HSC], BF16)
        tmpF = sbt("tmpF", [P, max(2 * ST, HSC)], F32)
        tmpG = sbt("tmpG", [P, ST], F32)
        ropeC = sbt("ropeCs", [P, TOK], BF16)
        ropeS = sbt("ropeSs", [P, TOK], BF16)
        tri = sbt("tris", [P, P], BF16)
        eyef = sbt("eyefs", [P, P], F32)
        onesp = sbt("onesps", [P, 1], F32)
        onesr = sbt("onesrs", [1, P], F32)
        onesF = sbt("onesFs", [P, D], F32)
        onehs = sbt("onehss", [32, 384], F32)
        jrev = sbt("jrevs", [P, P], BF16)
        permm = sbt("permm", [P, P], BF16)
        hank = sbt("hank", [P, P], BF16)
        rel8s = sbt("rel8ss", [32, HO], F32)
        ebias = sbt("ebias", [P, HO], F32)
        ediag = sbt("ediag", [P, HO * P], BF16)
        eprev = sbt("eprev", [P, HO * P], BF16)
        bcols = sbt("bcols", [P, 8 * NCH], F32)
        vbias = sbt("vbias", [P, DM], BF16)
        bvrow = sbt("bvrow", [1, DM], BF16)
        onesrb = sbt("onesrb", [1, P], BF16)
        bandF = sbt("bandF", [HO, 384], F32)
        bandB = sbt("bandB", [HO, 384], BF16)
        statc = sbt("statc", [P, 8], F32)
        row_m = sbt("row_m", [1, ST], F32)
        row_q2 = sbt("row_q2", [1, ST], F32)
        row_i = row_q2
        g16 = sbt("g16", [1, 16], BF16)
        sc1 = sbt("sc1", [1, 16], F32)
        scC = sbt("scC", [P, 8], F32)
        epsc = sbt("epsc", [P, 1], F32)
        rnorm = wstg[0]

        ps_a = pst("ps_a", [P, SG * QS])
        ps_b = pst("ps_b", [P, SG * QS])
        ps_o = pst("ps_o", [P, 3 * QS])
        ps_m = pst("ps_m", [P, QS])

        sems = {}
        for e in ENGINES:
            sems[("c", e)] = ctx.enter_context(nc.semaphore(f"sc_{e}"))
            for sl in range(Prog.NDQ):
                sems[("dma", e, sl)] = ctx.enter_context(
                    nc.semaphore(f"sd_{e}{sl}"))
        sems[("cc", "gpsimd")] = ctx.enter_context(nc.semaphore("s_cc"))

        bb = {}

        def N(name):
            if name not in bb:
                bb[name] = Buf(name)
            return bb[name]

        def T_(fn, r=(), w=(), a=(), inc=True):
            pr.op("tensor", fn, [N(x) for x in r], [N(x) for x in w],
                  [N(x) for x in a], inc=inc)

        def V_(fn, r=(), w=(), a=()):
            pr.op("vector", fn, [N(x) for x in r], [N(x) for x in w],
                  [N(x) for x in a])

        def S_(fn, r=(), w=(), a=()):
            pr.op("scalar", fn, [N(x) for x in r], [N(x) for x in w],
                  [N(x) for x in a])

        def Gd(fn, r=(), w=(), a=()):
            pr.op("gpsimd", fn, [N(x) for x in r], [N(x) for x in w],
                  [N(x) for x in a], dma=True)

        def Gc(fn, r=(), w=(), a=()):
            pr.op("gpsimd", fn, [N(x) for x in r], [N(x) for x in w],
                  [N(x) for x in a], dma=False)

        def Y_(fn, r=(), w=(), a=()):
            pr.op("sync", fn, [N(x) for x in r], [N(x) for x in w],
                  [N(x) for x in a], dma=True)

        def CC_(fn, r=(), w=()):
            pr.op("gpsimd", fn, [N(x) for x in r], [N(x) for x in w], cc=True)

        bb["rnorm"] = N("wstg0")

        Y_(lambda e: e.dma_start(out=ropeC[:, :], in_=d_ropeC[:, :]), w=["ropeC"])
        Y_(lambda e: e.dma_start(out=ropeS[:, :], in_=d_ropeS[:, :]), w=["ropeS"])
        Y_(lambda e: e.dma_start(out=tri[:, :], in_=d_tri[:, :]), w=["tri"])
        Y_(lambda e: e.dma_start(out=eyef[:, :], in_=d_eyef[:, :]), w=["eyef"])
        Y_(lambda e: e.dma_start(out=onesp[:, :], in_=d_onesp[:, :]), w=["onesp"])
        Y_(lambda e: e.dma_start(out=onesr[:, :], in_=d_onesr[:, :]), w=["onesr"])
        Y_(lambda e: e.dma_start(out=onehs[:, :], in_=d_oneh[:, :]), w=["oneh"])
        Y_(lambda e: e.dma_start(out=jrev[:, :], in_=d_jrev[:, :]), w=["jrev"])
        Y_(lambda e: e.dma_start(out=permm[:, :], in_=d_perm[:, :]), w=["permm"])
        Y_(lambda e: e.dma_start(out=rel8s[:, :], in_=d_rel8[:, :]), w=["rel8"])
        bslot = {}
        si = 0
        for l in (0, 1):
            for bn in ("bq", "bk", "bp", "bf"):
                bslot[l, bn] = si * NCH
                Y_(lambda e, l=l, bn=bn, s0=si * NCH: e.dma_start(
                    out=bcols[:, s0:s0 + NCH], in_=Bd[l, bn][:, :]), w=["bcols"])
                si += 1
        Gd(lambda e: e.dma_start(
            out=bass.AP(Xrs, 0, [[NCH * TOK, P], [TOK, NCH], [1, TOK]]),
            in_=bass.AP(x0T, 0, [[TOK, P], [P * TOK, NCH], [1, TOK]])), w=["Xrs"])
        Gc(lambda e: e.memset(onesF[:, :], 1.0), w=["onesF"])
        Gc(lambda e: e.memset(onesrb[:, :], 1.0), w=["onesrb"])
        Gc(lambda e: e.memset(statc[:, :], 0.0), w=["statc"])
        Gc(lambda e: e.memset(sc1[:, :], 0.0), w=["sc1"])
        Gc(lambda e: e.memset(row_m[:, :], 0.0), w=["rows"])
        Gc(lambda e: e.memset(row_q2[:, :], 0.0), w=["rows"])
        Gc(lambda e: e.memset(row_i[:, :], 0.0), w=["rows"])
        Gc(lambda e: e.memset(rnorm[:, :], 0.0), w=["rnorm"])
        Gc(lambda e: e.memset(ebias[:, :], 0.0), w=["ebias"])
        Gc(lambda e: e.memset(scC[:, :], 0.0), w=["scC"])
        Gc(lambda e: e.memset(epsc[:, :], EPS), w=["epsc"])

        def load_W(l, w, slot):
            Gd(lambda e, l=l, w=w, slot=slot: e.dma_start(
                out=bass.AP(wsl[slot], 0, [[NCH * DM, P], [DM, NCH], [1, DM]]),
                in_=bass.AP(Wd[l, w], 0, [[DM, P], [P * DM, NCH], [1, DM]])),
               w=[f"wsl{slot}"])

        load_W(0, "q", 0)
        load_W(0, "k", 1)

        dynv = {}

        def dyn(e):
            if "a1q0" not in dynv:
                pid = e.partition_id()
                half = (pid & 1) * HDT
                for r in (0, 1):
                    dynv[f"a1q{r}"] = e.snap(r * AG1_N + half)
                    dynv[f"a1k{r}"] = e.snap(r * AG1_N + DM * TOK + half)
                    dynv[f"a1v{r}"] = e.snap(r * AG1_N + V_OFF + half)
                dynv["ot_ev"] = e.snap((pid & 1) * TOK)
                dynv["ot_od"] = e.snap(AG2_N + (pid & 1) * TOK)
            return dynv

        for l in (0, 1):
            a1i, a1o, a2i, a2o = ag1_in[l], ag1_out[l], ag2_in[l], ag2_out[l]
            A1I, A1O, A2I, A2O = f"ag1i{l}", f"ag1o{l}", f"ag2i{l}", f"ag2o{l}"

            rsrc, rsn = (x0T, "x0T") if l == 0 else (x1loc, "x1loc")

            def res_slab(st, src, srcn):
                sg = st % 2
                Gd(lambda e, st=st, sg=sg, src=src: e.dma_start(
                    out=bass.AP(zsl[sg], 0, [[NCH * ST, P], [ST, NCH], [1, ST]]),
                    in_=bass.AP(src, st * ST, [[TOK, P], [P * TOK, NCH], [1, ST]])),
                   r=[srcn], a=[f"zsl{sg}"])

            res_slab(0, rsrc, rsn)
            if NST > 1:
                res_slab(1, rsrc, rsn)

            if l == 0:
                T_(lambda e: e.matmul(ps_m[0:HO, 0:384], rel8s[:, :], onehs[:, :],
                                      start=True, stop=True),
                   r=["rel8", "oneh"], w=["ps_m"])
                V_(lambda e: e.tensor_copy(bandF[:, :], ps_m[0:HO, 0:384]),
                   r=["ps_m"], w=["bandF"])
                V_(lambda e: e.tensor_copy(statc[0:HO, 6:7], bandF[:, 350:351]),
                   r=["bandF"], w=["statc"])
                V_(lambda e: e.tensor_reduce(statc[0:HO, 7:8], bandF[:, :], AxX,
                                             Alu.max), r=["bandF"], w=["statc"])
                V_(lambda e: e.tensor_single_scalar(bandF[:, :], bandF[:, :],
                                                    bandF[:, 350:351], Alu.subtract),
                   r=["bandF"], w=["bandF"])
                S_(lambda e: e.activation(bandB[:, :], bandF[:, :], Act.Exp, scale=0.125),
                   r=["bandF"], w=["bandB"])
                Gd(lambda e: e.dma_start(out=band_dram[:, :], in_=bandB[:, :]),
                   r=["bandB"], w=["band_dram"])
                for h in range(HO):
                    for dst, dn, off in ((ediag, "ediag", 0), (eprev, "eprev", 128)):
                        Gd(lambda e, h=h, off=off: e.dma_start(
                            out=hank[:, :],
                            in_=bass.AP(band_dram, h * 384 + off, [[1, P], [1, P]])),
                           r=["band_dram"], w=["hank"])
                        T_(lambda e: e.matmul(ps_m[:, 0:P], hank[:, :], jrev[:, :],
                                              start=True, stop=True),
                           r=["hank", "jrev"], w=["ps_m"])
                        V_(lambda e, dst=dst, h=h: e.tensor_copy(
                            dst[:, h * P:(h + 1) * P], ps_m[:, 0:P]),
                           r=["ps_m"], a=[dn])
                for h in range(HO):
                    V_(lambda e, h=h: e.tensor_mul(ediag[:, h * P:(h + 1) * P],
                                                   ediag[:, h * P:(h + 1) * P],
                                                   tri[:, :]),
                       r=["ediag", "tri"], w=["ediag"])

            Gc(lambda e: e.memset(statc[:, 0:2], 0.0), w=["statc"])
            for wname, slot, bn, scol, agrow in (
                    ("q", 0, "bq", 0, 0), ("k", 1, "bk", 1, DM)):
                pend_rope = []

                def emit_rope(ic, st, scol, agrow, bn):
                    rb = st % 2
                    T_(lambda e, st=st: e.matmul(
                        ps_m[:, 0:ST], permm[:, :],
                        qst[0][:, st * ST:(st + 1) * ST],
                        start=True, stop=True),
                       r=["permm", "qst0"], w=["ps_m"])
                    V_(lambda e, st=st, rb=rb: e.tensor_mul(
                        rt1[rb][:, :], qst[0][:, st * ST:(st + 1) * ST],
                        ropeC[:, st * ST:(st + 1) * ST]),
                       r=["qst0", "ropeC"], w=[f"rt1{rb}"])
                    V_(lambda e, st=st, rb=rb: e.tensor_mul(
                        rt2[rb][:, :], ps_m[:, 0:ST],
                        ropeS[:, st * ST:(st + 1) * ST]),
                       r=["ps_m", "ropeS"], w=[f"rt2{rb}"])
                    V_(lambda e, rb=rb: e.tensor_add(
                        rt1[rb][:, :], rt1[rb][:, :], rt2[rb][:, :]),
                       r=[f"rt2{rb}"], w=[f"rt1{rb}"])
                    V_(lambda e, rb=rb: e.tensor_reduce(
                        statc[:, 2:3], rt1[rb][:, :], AxX, Alu.max,
                        apply_absolute_value=True),
                       r=[f"rt1{rb}"], w=["statc"])
                    V_(lambda e, scol=scol: e.tensor_tensor(
                        statc[:, scol:scol + 1], statc[:, scol:scol + 1],
                        statc[:, 2:3], Alu.max),
                       r=["statc"], w=["statc"])
                    Y_(lambda e, rb=rb, ic=ic, st=st, agrow=agrow, a1i=a1i: e.dma_start(
                        out=bass.AP(a1i, (agrow + ic * P) * TOK + st * ST,
                                    [[TOK, P], [1, ST]]),
                        in_=rt1[rb][:, :]),
                       r=[f"rt1{rb}"], a=[A1I])

                PBANKS = [(ps_a, 0, "ps_aL"), (ps_b, 0, "ps_bL"),
                          (ps_a, QS, "ps_aR"), (ps_b, QS, "ps_bR")]
                for ic in range(NCH):
                    for st in range(NST):
                        psum, po_, pn = PBANKS[(ic * NST + st) % 4]
                        for kc in range(NCH):
                            T_(lambda e, psum=psum, po_=po_, slot=slot, kc=kc, ic=ic, st=st:
                               e.matmul(psum[:, po_:po_ + ST],
                                        wsl[slot][:, kc * DM + ic * P:kc * DM + (ic + 1) * P],
                                        Xrs[:, kc * TOK + st * ST:kc * TOK + (st + 1) * ST],
                                        start=(kc == 0), stop=(kc == NCH - 1)),
                               r=[f"wsl{slot}", "Xrs"], w=[pn], inc=(kc == NCH - 1))
                        S_(lambda e, psum=psum, po_=po_, st=st, s0=bslot[l, bn], ic=ic:
                           e.add(qst[0][:, st * ST:(st + 1) * ST], psum[:, po_:po_ + ST],
                                 bcols[:, s0 + ic:s0 + ic + 1]),
                           r=[pn, "bcols"], a=["qst0"])
                        pend_rope.append(
                            lambda ic=ic, st=st, scol=scol, agrow=agrow, bn=bn:
                            emit_rope(ic, st, scol, agrow, bn))
                        while len(pend_rope) > (1 if NST > 1 else 0):
                            pend_rope.pop(0)()
                while pend_rope:
                    pend_rope.pop(0)()
                if wname == "q":
                    load_W(l, "v", 0)
                else:
                    load_W(l, "p", 1)

            Gd(lambda e, l=l: e.dma_start(out=bvrow[:, :], in_=Bd[l, "bv"][:, :]),
               w=["bvrow"])
            for j in range(DM // ST):
                T_(lambda e, j=j: e.matmul(ps_m[:, 0:ST], onesrb[:, :],
                                           bvrow[:, j * ST:(j + 1) * ST],
                                           start=True, stop=True),
                   r=["onesrb", "bvrow"], w=["ps_m"])
                V_(lambda e, j=j: e.tensor_copy(vbias[:, j * ST:(j + 1) * ST],
                                                ps_m[:, 0:ST]),
                   r=["ps_m"], w=["vbias"])
            for tt in range(TOK // P):
                for vc in range(DM // ST):
                    sg = (tt * (DM // ST) + vc) % 2
                    psum, po_, pn = PBANKS[(tt * (DM // ST) + vc) % 4]
                    for kc in range(NCH):
                        T_(lambda e, psum=psum, po_=po_, kc=kc, tt=tt, vc=vc: e.matmul(
                            psum[:, po_:po_ + ST],
                            Xrs[:, kc * TOK + tt * P:kc * TOK + (tt + 1) * P],
                            wsl[0][:, kc * DM + vc * ST:kc * DM + (vc + 1) * ST],
                            start=(kc == 0), stop=(kc == NCH - 1)),
                           r=["wsl0", "Xrs"], w=[pn], inc=(kc == NCH - 1))
                    V_(lambda e, psum=psum, po_=po_, sg=sg, vc=vc: e.tensor_add(
                        vstg[sg][:, :], psum[:, po_:po_ + ST],
                        vbias[:, vc * ST:(vc + 1) * ST]),
                       r=[pn, "vbias"], w=[f"vstg{sg}"])
                    Y_(lambda e, sg=sg, tt=tt, vc=vc, a1i=a1i: e.dma_start(
                        out=bass.AP(a1i, V_OFF + vc * (ST // D) * TOK * D + tt * P * D,
                                    [[D, P], [TOK * D, ST // D], [1, D]]),
                        in_=bass.AP(vstg[sg], 0, [[ST, P], [D, ST // D], [1, D]])),
                       r=[f"vstg{sg}"], a=[A1I])

            T_(lambda e: e.transpose(ps_m[0:2, 0:P], statc[:, 0:2], eyef[:, :]),
               r=["statc", "eyef"], w=["ps_m"])
            V_(lambda e: e.tensor_reduce(statc[0:2, 3:4], ps_m[0:2, 0:P], AxX, Alu.max),
               r=["ps_m"], w=["statc"])
            Y_(lambda e, a1i=a1i: e.dma_start(
                out=bass.AP(a1i, ST_OFF + 2, [[2046, 1], [1, 2046]]),
                in_=Xrs[0:1, 0:2046]),
               r=["Xrs"], a=[A1I])
            Gd(lambda e, a1i=a1i: e.dma_start(
                out=bass.AP(a1i, ST_OFF, [[1, 2], [1, 1]]), in_=statc[0:2, 3:4]),
               r=["statc"], a=[A1I])
            CC_(lambda e, l=l: e.collective_compute(
                "AllGather", Alu.bypass, replica_groups=PAIRS,
                ins=[ag1_in[l].ap().opt()], outs=[ag1_out[l].ap().opt()]),
                r=[A1I], w=[A1O])

            fcol = 0
            for ic in range(NCH):
                sg = ic % 2
                Y_(lambda e, l=l, ic=ic, sg=sg: e.dma_start(
                    out=wstg[sg][:, :],
                    in_=bass.AP(Wd[l, "f"], ic * P * DM, [[DM, P], [1, DM]])),
                   w=[f"wstg{sg}"])
                V_(lambda e, sg=sg, l=l: e.tensor_reduce(
                    statc[:, 4:5], wstg[sg][:, :], AxX,
                    Alu.add if l == 0 else Alu.max, apply_absolute_value=True),
                   r=[f"wstg{sg}"], w=["statc"])
                V_(lambda e, l=l: e.tensor_tensor(
                    statc[:, 5:6], statc[:, 5:6], statc[:, 4:5],
                    Alu.add if l == 0 else Alu.max),
                   r=["statc"], w=["statc"])
            T_(lambda e: e.transpose(ps_m[0:1, 0:P], statc[:, 5:6], eyef[:, :]),
               r=["statc", "eyef"], w=["ps_m"])
            V_(lambda e, l=l: e.tensor_reduce(sc1[0:1, 0:1], ps_m[0:1, 0:P], AxX,
                                              Alu.add if l == 0 else Alu.max),
               r=["ps_m"], w=["sc1"])
            Gc(lambda e: e.memset(statc[:, 5:6], 0.0), w=["statc"])
            if l == 0:
                V_(lambda e: e.tensor_scalar(sc1[0:1, 1:2], sc1[0:1, 0:1],
                                             0.5 / (DM * DM), None, Alu.mult),
                   r=["sc1"], w=["sc1"])
                V_(lambda e: e.tensor_scalar(sc1[0:1, 2:3], sc1[0:1, 1:2],
                                             -1.0, None, Alu.mult),
                   r=["sc1"], w=["sc1"])
                pairs = [(1, 0), (2, 1)]
            else:
                V_(lambda e: e.tensor_scalar(sc1[0:1, 1:2], sc1[0:1, 0:1],
                                             1e-12, None, Alu.max),
                   r=["sc1"], w=["sc1"])
                V_(lambda e: e.reciprocal(sc1[0:1, 1:2], sc1[0:1, 1:2]),
                   r=["sc1"], w=["sc1"])
                V_(lambda e: e.tensor_scalar(sc1[0:1, 1:2], sc1[0:1, 1:2],
                                             448.0, None, Alu.mult),
                   r=["sc1"], w=["sc1"])
                V_(lambda e: e.tensor_scalar(sc1[0:1, 2:3], sc1[0:1, 1:2],
                                             0.5, None, Alu.mult),
                   r=["sc1"], w=["sc1"])
                pairs = [(1, 0), (2, 1)]
            for (c_in, c_out) in pairs:
                T_(lambda e, c_in=c_in: e.matmul(ps_m[:, 0:1], onesr[:, :],
                                                 sc1[0:1, c_in:c_in + 1],
                                                 start=True, stop=True),
                   r=["onesr", "sc1"], w=["ps_m"])
                V_(lambda e, c_out=c_out: e.tensor_copy(scC[:, c_out:c_out + 1],
                                                        ps_m[:, 0:1]),
                   r=["ps_m"], w=["scC"])
            for oc in range(NCH):
                sg = oc % 2
                DQ = Y_ if oc % 2 == 0 else Gd
                DQ(lambda e, l=l, oc=oc, sg=sg: e.dma_start(
                    out=wstg[sg][:, :],
                    in_=bass.AP(Wd[l, "f"], oc * P * DM, [[DM, P], [1, DM]])),
                   w=[f"wstg{sg}"])
                for tc in range(NCH):
                    dst = (slice(None), slice(tc * DM + oc * P, tc * DM + (oc + 1) * P))
                    tb = (oc * NCH + tc) % 2
                    psq, pqn = (ps_m[:, 0:P], "ps_m") if tb == 0 else (
                        ps_o[:, 2 * QS:2 * QS + P], "ps_o2")
                    T_(lambda e, sg=sg, tc=tc, psq=psq: e.transpose(
                        psq, wstg[sg][:, tc * P:(tc + 1) * P], eyef[:, :]),
                       r=[f"wstg{sg}", "eyef"], w=[pqn])
                    if l == 0:
                        V_(lambda e, psq=psq: e.tensor_single_scalar(
                            mkb[:, 0:P], psq, scC[:, 0:1], Alu.is_gt),
                           r=[pqn, "scC"], w=["mkb"])
                        V_(lambda e, psq=psq: e.tensor_single_scalar(
                            bgb[:, 0:P], psq, scC[:, 1:2], Alu.is_lt),
                           r=[pqn, "scC"], w=["bgb"])
                        Gc(lambda e, dst=dst: e.tensor_tensor(
                            wsl[0][dst], mkb[:, 0:P], bgb[:, 0:P], Alu.subtract),
                           r=["mkb", "bgb"], a=["wsl0"])
                    else:
                        V_(lambda e, psq=psq: e.tensor_single_scalar(
                            f8c[:, 0:P], psq, scC[:, 1:2], Alu.mult),
                           r=[pqn, "scC"], w=["f8c"])
                        S_(lambda e: e.mul(bgb[:, 0:P], f8c[:, 0:P], 2.0),
                           r=["f8c"], w=["bgb"])
                        V_(lambda e, psq=psq: e.tensor_single_scalar(
                            tmpF[:, 0:P], psq, scC[:, 0:1], Alu.mult),
                           r=[pqn, "scC"], w=["tmpF"])
                        S_(lambda e: e.activation(
                            tmpF[:, P:2 * P], tmpF[:, 0:P], Act.Abs),
                           r=["tmpF"], w=["tmpF"])
                        V_(lambda e: e.tensor_scalar(
                            mkb[:, 0:P], tmpF[:, P:2 * P], FP8_SMALL, None,
                            Alu.is_ge),
                           r=["tmpF"], w=["mkb"])
                        V_(lambda e, dst=dst: e.tensor_scalar(
                            wsl[0][dst], tmpF[:, 0:P], C_Q9, C_Q9, Alu.add,
                            Alu.subtract),
                           r=["tmpF"], a=["wsl0"])
                        V_(lambda e, dst=dst: e.copy_predicated(
                            wsl[0][dst], mkb[:, 0:P], bgb[:, 0:P]),
                           r=["mkb", "bgb", "wsl0"], a=["wsl0"])

            Y_(lambda e, l=l: e.dma_start(
                out=bass.AP(g16, 0, [[16, 1], [2, 2], [1, 2]]),
                in_=bass.AP(ag1_out[l], ST_OFF, [[1, 1], [AG1_N, 2], [1, 2]])),
               r=[A1O], w=["g16"])
            V_(lambda e: e.tensor_reduce(sc1[0:1, 5:6],
                                         bass.AP(g16, 0, [[16, 1], [2, 2]]),
                                         AxX, Alu.max), r=["g16"], w=["sc1"])
            V_(lambda e: e.tensor_reduce(sc1[0:1, 6:7],
                                         bass.AP(g16, 1, [[16, 1], [2, 2]]),
                                         AxX, Alu.max), r=["g16"], w=["sc1"])
            V_(lambda e: e.tensor_tensor(sc1[0:1, 7:8], sc1[0:1, 5:6],
                                         sc1[0:1, 6:7], Alu.mult),
               r=["sc1"], w=["sc1"])
            V_(lambda e: e.tensor_scalar(sc1[0:1, 7:8], sc1[0:1, 7:8], 8.0,
                                         None, Alu.mult), r=["sc1"], w=["sc1"])
            if l == 0:
                T_(lambda e: e.transpose(ps_m[0:1, 0:P], statc[:, 7:8], eyef[:, :]),
                   r=["statc", "eyef"], w=["ps_m"])
                V_(lambda e: e.tensor_reduce(sc1[0:1, 4:5], ps_m[0:1, 0:HO], AxX,
                                             Alu.max), r=["ps_m"], w=["sc1"])
                V_(lambda e: e.tensor_scalar(sc1[0:1, 4:5], sc1[0:1, 4:5], 0.125,
                                             None, Alu.mult), r=["sc1"], w=["sc1"])
                V_(lambda e: e.tensor_tensor(sc1[0:1, 7:8], sc1[0:1, 7:8],
                                             sc1[0:1, 4:5], Alu.add),
                   r=["sc1"], w=["sc1"])
            V_(lambda e: e.tensor_scalar(sc1[0:1, 7:8], sc1[0:1, 7:8], M_CAP,
                                         None, Alu.min), r=["sc1"], w=["sc1"])
            T_(lambda e: e.matmul(ps_m[:, 0:1], onesr[:, :], sc1[0:1, 7:8],
                                  start=True, stop=True),
               r=["onesr", "sc1"], w=["ps_m"])
            V_(lambda e: e.tensor_copy(scC[:, 4:5], ps_m[:, 0:1]),
               r=["ps_m"], w=["scC"])
            if l == 0:
                T_(lambda e: e.transpose(ps_m[0:1, 0:P], statc[:, 6:7], eyef[:, :]),
                   r=["statc", "eyef"], w=["ps_m"])
                V_(lambda e: e.tensor_copy(sc1[0:1, 8:8 + HO], ps_m[0:1, 0:HO]),
                   r=["ps_m"], w=["sc1"])
                T_(lambda e: e.matmul(ps_m[:, 0:HO], onesr[:, :], sc1[0:1, 8:8 + HO],
                                      start=True, stop=True),
                   r=["onesr", "sc1"], w=["ps_m"])
                V_(lambda e: e.tensor_scalar(ebias[:, :], ps_m[:, 0:HO], 0.125,
                                             None, Alu.mult),
                   r=["ps_m"], w=["ebias"])
                V_(lambda e: e.tensor_single_scalar(ebias[:, :], ebias[:, :],
                                                    scC[:, 4:5], Alu.subtract),
                   r=["scC"], w=["ebias"])
            else:
                Gc(lambda e: e.memset(ebias[:, :], 0.0), w=["ebias"])
                V_(lambda e: e.tensor_single_scalar(ebias[:, :], ebias[:, :],
                                                    scC[:, 4:5], Alu.subtract),
                   r=["scC"], w=["ebias"])

            for (r, c, cn) in ((0, 0, "q"), (0, 1, "k"), (1, 0, "q"),
                               (1, 1, "k"), (0, 2, "v"), (1, 2, "v")):
                def fcp(e, r=r, c=c, cn=cn, l=l, a1o=a1o):
                    dv = dyn(e)
                    return e.dma_start(
                        out=qkvloc[l][0:1, (r * 3 + c) * HDT:(r * 3 + c + 1) * HDT],
                        in_=a1o[0:1, bass.ds(dv[f"a1{cn}{r}"], HDT)],
                        bounds_check=None)
                Y_(fcp, r=[A1O], a=[f"qkvloc{l}_{r}{cn}"])

            def head_dma(h):
                hb = h % 2
                rb0 = hb * D
                for r in (0, 1):
                    Y_(lambda e, h=h, rb0=rb0, r=r, l=l: e.dma_start(
                        out=qkt[rb0:rb0 + D, r * TOK:(r + 1) * TOK],
                        in_=bass.AP(qkvloc[l], r * 3 * HDT + h * D * TOK,
                                    [[TOK, D], [1, TOK]])),
                       r=[f"qkvloc{l}_{r}q"], a=[f"qkt{hb}"])
                    Y_(lambda e, h=h, rb0=rb0, r=r, l=l: e.dma_start(
                        out=qkt[rb0:rb0 + D, T + r * TOK:T + (r + 1) * TOK],
                        in_=bass.AP(qkvloc[l], (r * 3 + 1) * HDT + h * D * TOK,
                                    [[TOK, D], [1, TOK]])),
                       r=[f"qkvloc{l}_{r}k"], a=[f"qkt{hb}"])
                    Y_(lambda e, h=h, hb=hb, r=r, l=l: e.dma_start(
                        out=bass.AP(vph[hb], r * (TOK // P) * 65,
                                    [[65 * NKB, P], [65, TOK // P], [1, D]]),
                        in_=bass.AP(qkvloc[l], (r * 3 + 2) * HDT + h * TOK * D,
                                    [[D, P], [P * D, TOK // P], [1, D]])),
                       r=[f"qkvloc{l}_{r}v"], a=[f"vph{hb}"])

            for i in range(2):
                Gc(lambda e, i=i: e.memset(
                    bass.AP(vph[i], 64, [[65 * NKB, P], [65, NKB]]), 1.0),
                   r=[f"vph{i}"], w=[f"vph{i}"])
            head_dma(0)
            pend_av = []
            pend_norm = []

            def emit_S(h, hb, rb0, qs, g0, gsz, gb, psum, pn):
                for j in range(gsz):
                    kb = g0 + j
                    T_(lambda e, psum=psum, rb0=rb0, kb=kb, qs=qs, j=j:
                       e.matmul(psum[:, j * QS:(j + 1) * QS],
                                qkt[rb0:rb0 + D, T + kb * P:T + (kb + 1) * P],
                                qkt[rb0:rb0 + D, qs * QS:(qs + 1) * QS],
                                start=True, stop=True),
                       r=[f"qkt{hb}"], w=[pn + "LR"[j]], inc=(j == gsz - 1))
                S_(lambda e, psum=psum, gb=gb, gsz=gsz, h=h: e.activation(
                    atb[gb][:, 0:gsz * QS], psum[:, 0:gsz * QS], Act.Exp,
                    bias=ebias[:, h:h + 1], scale=0.125),
                   r=[pn + "L", pn + "R", "ebias"][0:gsz + 1] + ["ebias"],
                   w=[f"atb{gb}"])
                qb0 = qs * KPQ
                for j in range(gsz):
                    kb = g0 + j
                    t = kb - qb0
                    nz = min(max(t, 0), KPQ)
                    if nz > 0:
                        V_(lambda e, gb=gb, j=j, nz=nz: e.memset(
                            atb[gb][:, j * QS:j * QS + nz * P], 0.0),
                           w=[f"atb{gb}"])
                    if 0 <= t < KPQ:
                        if l == 0:
                            V_(lambda e, gb=gb, j=j, t=t, h=h: e.tensor_mul(
                                atb[gb][:, j * QS + t * P:j * QS + (t + 1) * P],
                                atb[gb][:, j * QS + t * P:j * QS + (t + 1) * P],
                                ediag[:, h * P:(h + 1) * P]),
                               r=["ediag"], w=[f"atb{gb}"])
                        else:
                            V_(lambda e, gb=gb, j=j, t=t: e.tensor_mul(
                                atb[gb][:, j * QS + t * P:j * QS + (t + 1) * P],
                                atb[gb][:, j * QS + t * P:j * QS + (t + 1) * P],
                                tri[:, :]),
                               r=["tri"], w=[f"atb{gb}"])
                    if l == 0 and 0 <= t + 1 < KPQ:
                        V_(lambda e, gb=gb, j=j, t=t, h=h: e.tensor_mul(
                            atb[gb][:, j * QS + (t + 1) * P:j * QS + (t + 2) * P],
                            atb[gb][:, j * QS + (t + 1) * P:j * QS + (t + 2) * P],
                            eprev[:, h * P:(h + 1) * P]),
                           r=["eprev"], w=[f"atb{gb}"])

            def emit_AV(hb, qs, g0, gsz, gb, nkb, po):
                for j in range(gsz):
                    kb = g0 + j
                    T_(lambda e, hb=hb, kb=kb, gb=gb, j=j, nkb=nkb, po=po:
                       e.matmul(ps_o[0:65, po * QS:po * QS + QS],
                                vph[hb][:, kb * 65:(kb + 1) * 65],
                                atb[gb][:, j * QS:(j + 1) * QS],
                                start=(kb == 0), stop=(kb == nkb - 1)),
                       r=[f"vph{hb}", f"atb{gb}"], w=[f"ps_o{po}"],
                       inc=(kb == nkb - 1))

            def emit_norm(h, hb, qs, po):
                V_(lambda e, po=po: e.tensor_scalar(
                    rnorm[64:65, 0:QS], ps_o[64:65, po * QS:po * QS + QS],
                    1e-35, None, Alu.max),
                   r=[f"ps_o{po}"], w=["rnorm"])
                V_(lambda e: e.reciprocal(rnorm[64:65, 0:QS],
                                          rnorm[64:65, 0:QS]),
                   r=["rnorm"], w=["rnorm"])
                T_(lambda e: e.matmul(ps_m[0:D, 0:QS], onesF[D:D + 1, :],
                                      rnorm[64:65, 0:QS], start=True, stop=True),
                   r=["onesF", "rnorm"], w=["ps_m"])
                V_(lambda e: e.tensor_copy(rnorm[0:D, 0:QS], ps_m[0:D, 0:QS]),
                   r=["ps_m"], w=["rnorm"])
                ob = qs % 2
                V_(lambda e, ob=ob, po=po: e.tensor_mul(
                    ot_sm[ob][:, :], ps_o[0:D, po * QS:po * QS + QS],
                    rnorm[0:D, 0:QS]),
                   r=[f"ps_o{po}", "rnorm"], w=[f"ot_sm{ob}"])
                Y_(lambda e, ob=ob, h=h, qs=qs, a2i=a2i: e.dma_start(
                    out=bass.AP(a2i, h * D * T + qs * QS, [[T, D], [1, QS]]),
                    in_=ot_sm[ob][:, :]),
                   r=[f"ot_sm{ob}"], a=[A2I])

            NORM_LAG = 8

            def pop_av():
                av, fin, fpo = pend_av.pop(0)
                av()
                if fin is not None:
                    pend_norm.append((gctr + NORM_LAG, fpo, fin))

            def pop_norms(ready_only=True, po=None):
                while pend_norm:
                    rdy, npo, fn = pend_norm[0]
                    if ready_only and rdy > gctr and not (po is not None and npo == po):
                        break
                    pend_norm.pop(0)
                    fn()

            gctr = 0
            for h in range(HO):
                hb = h % 2
                rb0 = hb * D
                hctr = 0
                for qs in range(NQS):
                    po = qs % 3
                    nkb = KPQ * (qs + 1)
                    glist = list(_groups(nkb))
                    for gi, (g0, gsz) in enumerate(glist):
                        gb = gctr % 3
                        psum, pn = (ps_a, "ps_a") if gctr % 2 == 0 else (ps_b, "ps_b")
                        emit_S(h, hb, rb0, qs, g0, gsz, gb, psum, pn)
                        pop_norms(po=po if gi == 0 else None)
                        if len(pend_av) >= 2:
                            pop_av()
                        if hctr == 1 and h + 1 < HO:
                            head_dma(h + 1)
                        fin = (lambda h=h, hb=hb, qs=qs, po=po:
                               emit_norm(h, hb, qs, po)) if gi == len(glist) - 1 else None
                        pend_av.append(
                            (lambda hb=hb, qs=qs, g0=g0, gsz=gsz, gb=gb, nkb=nkb,
                             po=po: emit_AV(hb, qs, g0, gsz, gb, nkb, po), fin, po))
                        gctr += 1
                        hctr += 1
            while pend_av:
                pop_av()
                gctr += 1
                pop_norms()
            pop_norms(ready_only=False)
            CC_(lambda e, l=l: e.collective_compute(
                "AllGather", Alu.bypass, replica_groups=PAIRS,
                ins=[ag2_in[l].ap().opt()], outs=[ag2_out[l][0:1, 0:2 * AG2_N].opt()]),
                r=[A2I], w=[A2O])

            Y_(lambda e, a2o=a2o: e.dma_start(
                out=a2o[0:1, 2 * AG2_N:2 * AG2_N + TOK], in_=Xrs[0:1, 0:TOK]),
               r=["Xrs"], a=[A2O])
            for r, reg in ((0, "ot_ev"), (1, "ot_od")):
                def fc2(e, r=r, reg=reg, l=l, a2o=a2o):
                    dv = dyn(e)
                    return e.dma_start(
                        out=ot2loc[l][0:1, r * (DM // 2) * T:(r + 1) * (DM // 2) * T],
                        in_=a2o[0:1, bass.ds(dv[reg], (DM // 2) * T)],
                        bounds_check=None)
                Y_(fc2, r=[A2O], a=[f"ot2loc{l}"])

            def pslice_dma(st, ic):
                sg = st % 2
                r = 0 if ic < NCH // 2 else 1
                row0 = (ic % (NCH // 2)) * P
                Y_(lambda e, st=st, ic=ic, sg=sg, r=r, row0=row0, l=l: e.dma_start(
                    out=ot_sl[sg][:, ic * ST:(ic + 1) * ST],
                    in_=bass.AP(ot2loc[l], r * (DM // 2) * T + row0 * T + st * ST,
                                [[T, P], [1, ST]])),
                   r=[f"ot2loc{l}"], a=[f"ot_sl{sg}"])

            QW = min(TOK, HSC)
            NQH = TOK // QW

            def ln_sums(st):
                sb = st % 2
                zs, zn = zsl[sb], f"zsl{sb}"
                for ic in range(NCH):
                    S_(lambda e, zs=zs, zn=zn, ic=ic: e.activation(
                        tmpF[:, ST:2 * ST], zs[:, ic * ST:(ic + 1) * ST],
                        Act.Square),
                       r=[zn], w=["tmpF"])
                    T_(lambda e, zs=zs, ic=ic: e.matmul(
                        ps_o[0:1, 0:ST], onesp[:, :], zs[:, ic * ST:(ic + 1) * ST],
                        start=(ic == 0), stop=(ic == NCH - 1)),
                       r=["onesp", zn], w=["ps_o0"], inc=True)
                    T_(lambda e, ic=ic: e.matmul(
                        ps_m[0:1, 0:ST], onesp[:, :], tmpF[:, ST:2 * ST],
                        start=(ic == 0), stop=(ic == NCH - 1)),
                       r=["onesp", "tmpF"], w=["ps_m"], inc=True)
                V_(lambda e: e.tensor_scalar(row_m[:, :], ps_o[0:1, 0:ST],
                                             1.0 / DM, None, Alu.mult),
                   r=["ps_o0"], w=["rows"])
                V_(lambda e: e.tensor_scalar(row_q2[:, :], ps_m[0:1, 0:ST],
                                             1.0 / DM, None, Alu.mult),
                   r=["ps_m"], w=["rows"])
                V_(lambda e: e.tensor_mul(x1c[0:1, ST:2 * ST], row_m[:, :],
                                          row_m[:, :]),
                   r=["rows"], w=["x1c"])
                V_(lambda e: e.tensor_tensor(x1c[0:1, ST:2 * ST], row_q2[:, :],
                                             x1c[0:1, ST:2 * ST], Alu.subtract),
                   r=["rows", "x1c"], w=["x1c"])
                S_(lambda e: e.activation(row_i[:, :], x1c[0:1, ST:2 * ST], Act.Ln,
                                          bias=epsc[0:1, 0:1]),
                   r=["x1c", "epsc"], w=["rows"])
                S_(lambda e: e.activation(row_i[:, :], row_i[:, :], Act.Exp,
                                          scale=-0.5), r=["rows"], w=["rows"])

            def ln_apply(st, spill_x1=False, amax=False, write_out=False):
                sb = st % 2
                zs, zn = zsl[sb], f"zsl{sb}"
                T_(lambda e: e.matmul(ps_o[:, QS:QS + ST], onesr[:, :],
                                      row_m[:, :], start=True, stop=True),
                   r=["onesr", "rows"], w=["ps_o1"])
                T_(lambda e: e.matmul(ps_m[:, 0:ST], onesr[:, :], row_i[:, :],
                                      start=True, stop=True),
                   r=["onesr", "rows"], w=["ps_m"])
                for ic in range(NCH):
                    V_(lambda e, zs=zs, zn=zn, ic=ic: e.tensor_tensor(
                        tmpF[:, 0:ST], zs[:, ic * ST:(ic + 1) * ST],
                        ps_o[:, QS:QS + ST], Alu.subtract),
                       r=[zn, "ps_o1"], w=["tmpF"])
                    V_(lambda e, zs=zs, zn=zn, ic=ic: e.tensor_tensor(
                        zs[:, ic * ST:(ic + 1) * ST], tmpF[:, 0:ST],
                        ps_m[:, 0:ST], Alu.mult),
                       r=["tmpF", "ps_m"], a=[zn])
                    Gc(lambda e, zs=zs, zn=zn, ic=ic, st=st: e.tensor_copy(
                        Xrs[:, ic * TOK + st * ST:ic * TOK + (st + 1) * ST],
                        zs[:, ic * ST:(ic + 1) * ST]),
                       r=[zn], a=["Xrs"])
                if spill_x1:
                    Y_(lambda e, zs=zs, st=st: e.dma_start(
                        out=bass.AP(x1loc, st * ST,
                                    [[TOK, P], [P * TOK, NCH], [1, ST]]),
                        in_=bass.AP(zs, 0, [[NCH * ST, P], [ST, NCH], [1, ST]])),
                       r=[zn], a=["x1loc"])
                if amax:
                    V_(lambda e, zs=zs: e.tensor_reduce(
                        statc[:, 4:5], zs[:, :], AxX, Alu.max,
                        apply_absolute_value=True),
                       r=[zn], w=["statc"])
                    V_(lambda e: e.tensor_tensor(statc[:, 5:6], statc[:, 5:6],
                                                 statc[:, 4:5], Alu.max),
                       r=["statc"], w=["statc"])
                if write_out:
                    Y_(lambda e, zs=zs, st=st: e.dma_start(
                        out=bass.AP(o_xT, st * ST,
                                    [[TOK, P], [P * TOK, NCH], [1, ST]]),
                        in_=bass.AP(zs, 0, [[NCH * ST, P], [ST, NCH], [1, ST]])),
                       r=[zn], w=["o_xT"] if st == NST - 1 else [], a=[] if st == NST - 1 else ["o_xT"])

            Gc(lambda e: e.memset(statc[:, 5:6], 0.0), w=["statc"])
            pend_apply = []
            for ic in range(NCH):
                pslice_dma(0, ic)
            for st in range(NST):
                sg = st % 2
                if st + 1 < NST:
                    for ic in range(NCH):
                        pslice_dma(st + 1, ic)
                for oc in range(NCH):
                    psum, po_, pn = PBANKS[oc % 4]
                    for ic in range(NCH):
                        T_(lambda e, psum=psum, po_=po_, ic=ic, oc=oc, sg=sg: e.matmul(
                            psum[:, po_:po_ + ST],
                            wsl[1][:, ic * DM + oc * P:ic * DM + (oc + 1) * P],
                            ot_sl[sg][:, ic * ST:(ic + 1) * ST],
                            start=(ic == 0), stop=(ic == NCH - 1)),
                           r=["wsl1", f"ot_sl{sg}"], w=[pn], inc=(ic == NCH - 1))
                    V_(lambda e, psum=psum, po_=po_, oc=oc, sg=sg,
                       s0=bslot[l, "bp"]: e.scalar_tensor_tensor(
                           zsl[sg][:, oc * ST:(oc + 1) * ST],
                           psum[:, po_:po_ + ST], bcols[:, s0 + oc:s0 + oc + 1],
                           zsl[sg][:, oc * ST:(oc + 1) * ST],
                           Alu.add, Alu.add),
                       r=[pn, "bcols", f"zsl{sg}"], a=[f"zsl{sg}"])
                    if oc == 2 and pend_apply:
                        pend_apply.pop(0)()
                ln_sums(st)
                pend_apply.append(
                    lambda st=st: (ln_apply(st, spill_x1=True, amax=True),
                                   res_slab(st + 2, rsrc, rsn)
                                   if st + 2 < NST else None))
            while pend_apply:
                pend_apply.pop(0)()

            T_(lambda e: e.transpose(ps_m[0:1, 0:P], statc[:, 5:6], eyef[:, :]),
               r=["statc", "eyef"], w=["ps_m"])
            V_(lambda e: e.tensor_reduce(sc1[0:1, 3:4], ps_m[0:1, 0:P], AxX, Alu.max),
               r=["ps_m"], w=["sc1"])
            Y_(lambda e, l=l: e.dma_start(out=ar_in[l][0:1, 0:8],
                                          in_=sc1[0:1, 3:11]),
               r=["sc1"], w=[f"ari{l}"])
            CC_(lambda e, l=l: e.collective_compute(
                "AllReduce", Alu.max, replica_groups=[list(range(8))],
                ins=[ar_in[l].ap().opt()], outs=[ar_out[l].ap().opt()]),
                r=[f"ari{l}"], w=[f"aro{l}"])
            Y_(lambda e, l=l: e.dma_start(out=sc1[0:1, 4:5],
                                          in_=ar_out[l][0:1, 0:1]),
               r=[f"aro{l}"], w=["sc1"])
            if l == 0:
                V_(lambda e: e.reciprocal(sc1[0:1, 9:10], sc1[0:1, 4:5]),
                   r=["sc1"], w=["sc1"])
                V_(lambda e: e.tensor_scalar(sc1[0:1, 9:10], sc1[0:1, 9:10],
                                             127.0, None, Alu.mult),
                   r=["sc1"], w=["sc1"])
                V_(lambda e: e.tensor_scalar(sc1[0:1, 10:11], sc1[0:1, 4:5],
                                             1.0 / 127.0, None, Alu.mult),
                   r=["sc1"], w=["sc1"])
                V_(lambda e: e.tensor_scalar(sc1[0:1, 11:12], sc1[0:1, 0:1],
                                             1.0 / (DM * DM), None, Alu.mult),
                   r=["sc1"], w=["sc1"])
                V_(lambda e: e.tensor_tensor(sc1[0:1, 10:11], sc1[0:1, 10:11],
                                             sc1[0:1, 11:12], Alu.mult),
                   r=["sc1"], w=["sc1"])
                pairs2 = [(9, 2), (10, 5)]
            else:
                V_(lambda e: e.tensor_scalar(sc1[0:1, 9:10], sc1[0:1, 4:5],
                                             1e-12, None, Alu.max),
                   r=["sc1"], w=["sc1"])
                V_(lambda e: e.reciprocal(sc1[0:1, 9:10], sc1[0:1, 9:10]),
                   r=["sc1"], w=["sc1"])
                V_(lambda e: e.tensor_scalar(sc1[0:1, 9:10], sc1[0:1, 9:10],
                                             448.0, None, Alu.mult),
                   r=["sc1"], w=["sc1"])
                V_(lambda e: e.tensor_scalar(sc1[0:1, 10:11], sc1[0:1, 9:10],
                                             0.5, None, Alu.mult),
                   r=["sc1"], w=["sc1"])
                V_(lambda e: e.tensor_tensor(sc1[0:1, 11:12], sc1[0:1, 9:10],
                                             sc1[0:1, 1:2], Alu.mult),
                   r=["sc1"], w=["sc1"])
                V_(lambda e: e.reciprocal(sc1[0:1, 11:12], sc1[0:1, 11:12]),
                   r=["sc1"], w=["sc1"])
                pairs2 = [(9, 2), (10, 3), (11, 5)]
            for (c_in, c_out) in pairs2:
                T_(lambda e, c_in=c_in: e.matmul(
                    ps_m[:, 0:1], onesr[:, :], sc1[0:1, c_in:c_in + 1],
                    start=True, stop=True), r=["onesr", "sc1"], w=["ps_m"])
                V_(lambda e, c_out=c_out: e.tensor_copy(
                    scC[:, c_out:c_out + 1], ps_m[:, 0:1]),
                   r=["ps_m"], w=["scC"])
            res_slab(0, x1loc, "x1loc")
            if NST > 1:
                res_slab(1, x1loc, "x1loc")
            xq_map = []
            caps = [("ot_sl0", ot_sl[0], NCH * ST), ("ot_sl1", ot_sl[1], NCH * ST),
                    ("qst0", qst[0], TOK), ("xqb", xqb, TOK),
                    ("vph0", vph[0], TOK), ("vph1", vph[1], TOK)]
            ci_, off_ = 0, 0
            for ic in range(NCH):
                while caps[ci_][2] - off_ < TOK:
                    ci_, off_ = ci_ + 1, 0
                xq_map.append((caps[ci_][0], caps[ci_][1], off_))
                off_ += TOK
            def quant_tile(st, ic):
                sg = st % 2
                nm, tns, off = xq_map[ic]
                zt = lambda: zsl[sg][:, ic * ST:(ic + 1) * ST]
                if l == 0:
                    V_(lambda e, zt=zt: e.tensor_scalar(
                        x1c[:, 0:ST], zt(), scC[:, 2:3], C_RND, Alu.mult, Alu.add),
                       r=[f"zsl{sg}", "scC"], w=["x1c"])
                    V_(lambda e, tns=tns, off=off, st=st: e.tensor_scalar(
                        tns[:, off + st * ST:off + (st + 1) * ST],
                        x1c[:, 0:ST], C_RND, None, Alu.subtract),
                       r=["x1c"], a=[nm])
                else:
                    S_(lambda e, zt=zt: e.mul(x1c[:, 0:ST], zt(), scC[:, 2:3]),
                       r=[f"zsl{sg}", "scC"], w=["x1c"])
                    S_(lambda e, zt=zt: e.activation(
                        f8c[:, 0:ST], zt(), Act.Identity, scale=scC[:, 3:4]),
                       r=[f"zsl{sg}", "scC"], w=["f8c"])
                    S_(lambda e: e.mul(bgb[:, 0:ST], f8c[:, 0:ST], 2.0),
                       r=["f8c"], w=["bgb"])
                    S_(lambda e: e.activation(
                        x1c[:, ST:2 * ST], x1c[:, 0:ST], Act.Abs),
                       r=["x1c"], w=["x1c"])
                    V_(lambda e: e.tensor_scalar(
                        mkb[:, 0:ST], x1c[:, ST:2 * ST], FP8_SMALL, None,
                        Alu.is_ge),
                       r=["x1c"], w=["mkb"])
                    V_(lambda e, tns=tns, off=off, st=st: e.tensor_scalar(
                        tns[:, off + st * ST:off + (st + 1) * ST],
                        x1c[:, 0:ST], C_Q9, C_Q9, Alu.add, Alu.subtract),
                       r=["x1c"], a=[nm])
                    V_(lambda e, tns=tns, off=off, st=st: e.copy_predicated(
                        tns[:, off + st * ST:off + (st + 1) * ST],
                        mkb[:, 0:ST], bgb[:, 0:ST]),
                       r=["mkb", "bgb", nm], a=[nm])

            for st in range(NST):
                sg = st % 2
                for ic in range(NCH):
                    quant_tile(st, ic)
                for oc in range(NCH):
                    psum, po_, pn = PBANKS[oc % 4]
                    for ic in range(NCH):
                        nm, tns, off = xq_map[ic]
                        T_(lambda e, psum=psum, po_=po_, ic=ic, oc=oc, st=st, tns=tns,
                           off=off: e.matmul(
                               psum[:, po_:po_ + ST],
                               wsl[0][:, ic * DM + oc * P:ic * DM + (oc + 1) * P],
                               tns[:, off + st * ST:off + (st + 1) * ST],
                               start=(ic == 0), stop=(ic == NCH - 1)),
                           r=["wsl0", nm], w=[pn], inc=(ic == NCH - 1))
                    V_(lambda e, psum=psum, po_=po_, oc=oc, s0=bslot[l, "bf"]:
                       e.tensor_scalar(
                           tmpG[:, :], psum[:, po_:po_ + ST], scC[:, 5:6],
                           bcols[:, s0 + oc:s0 + oc + 1], Alu.mult, Alu.add),
                       r=[pn, "scC", "bcols"], w=["tmpG"])
                    Gc(lambda e, sg=sg, oc=oc: e.tensor_add(
                        zsl[sg][:, oc * ST:(oc + 1) * ST],
                        zsl[sg][:, oc * ST:(oc + 1) * ST], tmpG[:, :]),
                       r=["tmpG", f"zsl{sg}"], a=[f"zsl{sg}"])
                    if oc == 2 and pend_apply:
                        pend_apply.pop(0)()
                ln_sums(st)
                pend_apply.append(
                    lambda st=st: (ln_apply(st, spill_x1=(l == 0),
                                            write_out=(l == 1)),
                                   res_slab(st + 2, x1loc, "x1loc")
                                   if st + 2 < NST else None))
            while pend_apply:
                pend_apply.pop(0)()
            if l == 0:
                load_W(1, "q", 0)
                load_W(1, "k", 1)

        pr.op("sync", lambda e: e.nop(), [N("o_xT")], [], dma=False)

        pr.seal()
        nc._pr = pr
        with nc.Block() as block:
            @block.tensor
            def _(e):
                pr.emit("tensor", e, sems)

            @block.vector
            def _(e):
                pr.emit("vector", e, sems)

            @block.scalar
            def _(e):
                pr.emit("scalar", e, sems)

            @block.gpsimd
            def _(e):
                pr.emit("gpsimd", e, sems)

            @block.sync
            def _(e):
                pr.emit("sync", e, sems)

    return nc


def _deinterleave_perm():
    p = np.zeros(DM, np.int64)
    for h in range(NH):
        p[h * D:h * D + D // 2] = h * D + 2 * np.arange(D // 2)
        p[h * D + D // 2:(h + 1) * D] = h * D + 2 * np.arange(D // 2) + 1
    return p


def _bias_cols(b):
    return np.ascontiguousarray(b.reshape(NCH, P).T.astype(np.float32))


def kernel(L0_attn_x, L0_attn_Wq, L0_attn_bq, L0_attn_Wk, L0_attn_bk, L0_attn_Wv,
           L0_attn_bv, L0_attn_Wp, L0_attn_bp, L0_attn_rel_bias_table,
           L0_ffn_weight, L0_ffn_bias, L1_attn_Wq, L1_attn_bq, L1_attn_Wk,
           L1_attn_bk, L1_attn_Wv, L1_attn_bv, L1_attn_Wp, L1_attn_bp,
           L1_ffn_weight, L1_ffn_bias):
    perm = _deinterleave_perm()
    W = {
        (0, "q"): L0_attn_Wq, (0, "k"): L0_attn_Wk, (0, "v"): L0_attn_Wv,
        (0, "p"): L0_attn_Wp, (0, "f"): L0_ffn_weight,
        (1, "q"): np.ascontiguousarray(L1_attn_Wq[:, perm]),
        (1, "k"): np.ascontiguousarray(L1_attn_Wk[:, perm]),
        (1, "v"): L1_attn_Wv, (1, "p"): L1_attn_Wp, (1, "f"): L1_ffn_weight,
    }
    bq1 = np.asarray(L1_attn_bq)[perm]
    bk1 = np.asarray(L1_attn_bk)[perm]
    Bc = {
        (0, "bq"): _bias_cols(np.asarray(L0_attn_bq)),
        (0, "bk"): _bias_cols(np.asarray(L0_attn_bk)),
        (0, "bp"): _bias_cols(np.asarray(L0_attn_bp)),
        (0, "bf"): _bias_cols(np.asarray(L0_ffn_bias)),
        (1, "bq"): _bias_cols(bq1),
        (1, "bk"): _bias_cols(bk1),
        (1, "bp"): _bias_cols(np.asarray(L1_attn_bp)),
        (1, "bf"): _bias_cols(np.asarray(L1_ffn_bias)),
    }
    bv = {0: np.asarray(L0_attn_bv).reshape(1, DM).astype(np.float32),
          1: np.asarray(L1_attn_bv).reshape(1, DM).astype(np.float32)}
    relt = np.asarray(L0_attn_rel_bias_table).astype(np.float32)

    nc = build_program()
    in_maps = []
    for c in range(8):
        b, half = c // 2, c % 2
        cons = host_constants(half)
        m = {
            "x0T": np.ascontiguousarray(
                np.asarray(L0_attn_x)[b, half * TOK:(half + 1) * TOK, :].T
            ).astype(np.float32),
            "rel8": np.ascontiguousarray(relt[:, half * HO:(half + 1) * HO]),
            "d_ropeC": cons["ropeC"], "d_ropeS": cons["ropeS"],
            "d_tri": cons["tri01"],
            "d_eyef": cons["eye_f32"], "d_onesp": cons["ones_p"],
            "d_jrev": cons["jrev"], "d_perm": cons["perm"],
            "d_onesr": cons["ones_r"], "d_oneh": cons["oneh"],
        }
        for l in (0, 1):
            for w in ("q", "k", "v", "p", "f"):
                m[f"W{l}{w}"] = np.ascontiguousarray(W[l, w], dtype=np.float32)
            for bn in ("bq", "bk", "bp", "bf"):
                m[f"B{l}{bn}"] = Bc[l, bn]
            m[f"B{l}bv"] = bv[l]
        in_maps.append(m)

    res = run_bass_kernel_spmd(nc, in_maps, core_ids=list(range(8)))
    out = np.empty((B, T, DM), np.float32)
    for c in range(8):
        b, half = c // 2, c % 2
        out[b, half * TOK:(half + 1) * TOK, :] = np.asarray(
            res.results[c]["o_xT"]).T
    return out


# version 28
# speedup vs baseline: 1.1627x; 1.0433x over previous
import numpy as np
import ml_dtypes
import concourse.bass as bass
import concourse.mybir as mybir
from concourse.bass_utils import run_bass_kernel_spmd

F32 = mybir.dt.float32
BF16 = mybir.dt.bfloat16
FP8 = mybir.dt.float8e4
U8 = mybir.dt.uint8
I16 = mybir.dt.int16
Act = mybir.ActivationFunctionType
Alu = mybir.AluOpType
AxX = mybir.AxisListType.X

SCALE = 1
B = 4
T = 4096 // SCALE
DM = 1024
NH = 16
D = 64
P = 128
TOK = T // 2
NCH = DM // P
ST = min(512, TOK)
NST = TOK // ST
QS = min(512, TOK)
NQS = T // QS
KPQ = QS // P
NKB = T // P
HO = NH // 2
SG = 2
C_RND = 12582912.0
C_Q9 = 24576.0
FP8_SMALL = 0.03125
EPS = 1e-5
M_CAP = 30.0
DBG = 0

V_OFF = 2 * DM * TOK
ST_OFF = V_OFF + NH * TOK * D
AG1_N = ST_OFF + 2048
AG2_N = (DM // 2) * T
SPAN = V_OFF + (NH // 2) * TOK * D
HDT = (NH // 2) * TOK * D
PAIRS = [[0, 1], [2, 3], [4, 5], [6, 7]]

ENGINES = ("tensor", "vector", "scalar", "gpsimd", "sync")


class Buf:
    __slots__ = ("name", "writers", "readers")

    def __init__(self, name):
        self.name, self.writers, self.readers = name, [], []


class Prog:
    def __init__(self):
        self.ops = {e: [] for e in ENGINES}

    def op(self, eng, fn, reads=(), writes=(), appends=(), inc=True, dma=False,
           cc=False):
        idx = len(self.ops[eng])
        ev = (eng, idx)
        waits = set()
        for b in reads:
            waits.update(b.writers)
        for b in writes:
            waits.update(b.writers)
            waits.update(b.readers)
        for b in appends:
            waits.update(b.readers)
        waits.discard(ev)
        self.ops[eng].append(dict(fn=fn, inc=inc, dma=dma, cc=cc, waits=waits))
        for b in writes:
            b.writers = [ev]
            b.readers = []
        for b in appends:
            b.writers.append(ev)
        for b in reads:
            if not b.readers or b.readers[-1] != ev:
                b.readers.append(ev)

    NDQ = 8

    def seal(self):
        self.cum, self.cover, self.covidx, self.key = {}, {}, {}, {}
        for e in ENGINES:
            ops = self.ops[e]
            if ops and not ops[-1]["inc"]:
                ops[-1]["inc"] = True
            c = 0
            dslot = {}
            ndma = 0
            counts, keys, pre = [], [], []
            for o in ops:
                if o["dma"]:
                    sl = ndma % self.NDQ
                    ndma += 1
                    pre.append((("dma", e, sl), dslot.get(sl, 0)))
                    dslot[sl] = dslot.get(sl, 0) + 16
                    counts.append(dslot[sl])
                    keys.append(("dma", e, sl))
                elif o["cc"]:
                    ccc = getattr(self, "_ccc", {}).get(e, 0) + 1
                    self._ccc = getattr(self, "_ccc", {})
                    self._ccc[e] = ccc
                    pre.append(None)
                    counts.append(ccc)
                    keys.append(("cc", e))
                else:
                    if o["inc"]:
                        c += 1
                    pre.append(None)
                    counts.append(c)
                    keys.append(("c", e))
            self.cum[e], self.key[e] = counts, keys
            self.pre = getattr(self, "pre", {})
            self.pre[e] = pre
            cov, ci = [None] * len(ops), [None] * len(ops)
            nxt = nxi = None
            for i in range(len(ops) - 1, -1, -1):
                if ops[i]["inc"] and not ops[i]["dma"] and not ops[i]["cc"]:
                    nxt, nxi = counts[i], i
                if ops[i]["dma"] or ops[i]["cc"]:
                    cov[i], ci[i] = counts[i], i
                else:
                    cov[i], ci[i] = nxt, nxi
            self.cover[e], self.covidx[e] = cov, ci

    def emit(self, e, eng_obj, sems):
        watermark = {}

        def wait(key, val):
            if val > watermark.get(key, 0):
                eng_obj.wait_ge(sems[key], val)
                watermark[key] = val

        for i, o in enumerate(self.ops[e]):
            tgt = {}
            for (we, wi) in sorted(o["waits"]):
                ci = self.covidx[we][wi]
                assert ci is not None, f"{e}:{i} dep on {we}:{wi} has no covering inc"
                if we == e and ci >= i and not self.ops[we][wi]["dma"]                         and not self.ops[we][wi]["cc"]:
                    continue
                key = self.key[we][ci]
                val = self.cover[we][wi]
                if val > tgt.get(key, 0):
                    tgt[key] = val
            for key in sorted(tgt, key=str):
                wait(key, tgt[key])
            if self.pre[e][i] is not None:
                wait(*self.pre[e][i])
            inst = o["fn"](eng_obj)
            if o["inc"]:
                inst.then_inc(sems[self.key[e][i]], 16 if o["dma"] else 1)


def _bucket_np(n):
    import math
    NBUK, MAXD = 32, 128
    n = np.asarray(n)
    me = NBUK // 2
    small = n < me
    ns = np.maximum(n, 1).astype(np.float64)
    vl = me + (np.log(ns / me) / math.log(MAXD / me) * (NBUK - me)).astype(np.int64)
    vl = np.minimum(vl, NBUK - 1)
    return np.where(small, n, vl)


def host_constants(half):
    cons = {}
    t = np.arange(half * TOK, half * TOK + TOK, dtype=np.float64)
    inv = 1.0 / (10000.0 ** (np.arange(0, D, 2, dtype=np.float64) / D))
    ang = t[None, :] * inv[:, None]
    c32, s32 = np.cos(ang), np.sin(ang)
    cons["ropeC"] = np.tile(c32, (4, 1)).astype(ml_dtypes.bfloat16)
    cons["ropeS"] = np.concatenate([-s32, s32, -s32, s32], 0).astype(ml_dtypes.bfloat16)
    kk = np.arange(P)[:, None]
    qq = np.arange(P)[None, :]
    cons["tri01"] = (kk <= qq).astype(ml_dtypes.bfloat16)
    cons["eye_bf"] = np.eye(P, dtype=ml_dtypes.bfloat16)
    cons["eye_f32"] = np.eye(P, dtype=np.float32)
    cons["ones_p"] = np.ones((P, 1), np.float32)
    cons["ones_r"] = np.ones((1, P), np.float32)
    pm = np.zeros((P, P), np.float32)
    for g_ in range(2):
        for j_ in range(32):
            a_, b_ = g_ * 64 + j_, g_ * 64 + 32 + j_
            pm[a_, b_] = 1.0
            pm[b_, a_] = 1.0
    cons["perm"] = pm.astype(ml_dtypes.bfloat16)
    jr = np.zeros((P, P), np.float32)
    jr[np.arange(P)[::-1], np.arange(P)] = 1.0
    cons["jrev"] = jr.astype(ml_dtypes.bfloat16)
    e = np.arange(384)
    bkt = _bucket_np(np.maximum(e - 127, 0))
    oneh = np.zeros((32, 384), np.float32)
    oneh[bkt, e] = 1.0
    cons["oneh"] = oneh
    return cons


def _groups(nkb):
    g, j = [], 0
    while j < nkb:
        g.append((j, min(SG, nkb - j)))
        j += SG
    return g


def build_program():
    import contextlib
    nc = bass.Bass("TRN2", target_bir_lowering=False, debug=False)
    pr = Prog()

    x0T = nc.dram_tensor("x0T", [DM, TOK], F32, kind="ExternalInput")
    o_xT = nc.dram_tensor("o_xT", [DM, TOK], F32, kind="ExternalOutput")
    Wd, Bd = {}, {}
    for l in (0, 1):
        for w in ("q", "k", "v", "p", "f"):
            Wd[l, w] = nc.dram_tensor(f"W{l}{w}", [DM, DM], F32, kind="ExternalInput")
        for bn in ("bq", "bk", "bp", "bf"):
            Bd[l, bn] = nc.dram_tensor(f"B{l}{bn}", [P, NCH], F32, kind="ExternalInput")
        Bd[l, "bv"] = nc.dram_tensor(f"B{l}bv", [1, DM], F32, kind="ExternalInput")
    d_rel8 = nc.dram_tensor("rel8", [32, HO], F32, kind="ExternalInput")
    d_ropeC = nc.dram_tensor("d_ropeC", [P, TOK], BF16, kind="ExternalInput")
    d_ropeS = nc.dram_tensor("d_ropeS", [P, TOK], BF16, kind="ExternalInput")
    d_tri = nc.dram_tensor("d_tri", [P, P], BF16, kind="ExternalInput")
    d_eyef = nc.dram_tensor("d_eyef", [P, P], F32, kind="ExternalInput")
    d_onesp = nc.dram_tensor("d_onesp", [P, 1], F32, kind="ExternalInput")
    d_onesr = nc.dram_tensor("d_onesr", [1, P], F32, kind="ExternalInput")
    d_oneh = nc.dram_tensor("d_oneh", [32, 384], F32, kind="ExternalInput")
    d_jrev = nc.dram_tensor("d_jrev", [P, P], BF16, kind="ExternalInput")
    d_perm = nc.dram_tensor("d_perm", [P, P], BF16, kind="ExternalInput")

    ag1_in, ag1_out, ag2_in, ag2_out, ar_in, ar_out = {}, {}, {}, {}, {}, {}
    for l in (0, 1):
        ag1_in[l] = nc.dram_tensor(f"ag1i{l}", [1, AG1_N], BF16)
        ag1_out[l] = nc.dram_tensor(f"ag1o{l}", [1, 2 * AG1_N], BF16)
        ag2_in[l] = nc.dram_tensor(f"ag2i{l}", [1, AG2_N + TOK], BF16)
        ag2_out[l] = nc.dram_tensor(f"ag2o{l}", [1, 2 * AG2_N + TOK], BF16)
        ar_in[l] = nc.dram_tensor(f"ari{l}", [1, 8], F32)
        ar_out[l] = nc.dram_tensor(f"aro{l}", [1, 8], F32)
    band_dram = nc.dram_tensor("band_dram", [HO, 384], BF16)
    x1loc = nc.dram_tensor("x1loc", [1, DM * TOK], F32)
    qkvloc, ot2loc = {}, {}
    for l in (0, 1):
        qkvloc[l] = nc.dram_tensor(f"qkvloc{l}", [1, 6 * HDT], BF16)
        ot2loc[l] = nc.dram_tensor(f"ot2loc{l}", [1, 2 * (DM // 2) * T], BF16)

    ctx = contextlib.ExitStack()

    def sbt(n, s, d):
        return ctx.enter_context(nc.sbuf_tensor(n, s, d))

    def pst(n, s):
        return ctx.enter_context(nc.psum_tensor(n, s, F32))

    with ctx:
        Xrs = sbt("Xrs", [P, NCH * TOK], BF16)
        zsl = [sbt(f"zsl{i}", [P, NCH * ST], F32) for i in range(2)]
        x1c = sbt("x1c", [P, 1024], F32)
        qst = [sbt("qst0", [P, TOK], BF16)]
        xqb = sbt("xqb", [P, TOK], BF16)
        rt1 = [sbt(f"rt1{i}", [P, ST], BF16) for i in range(2)]
        rt2 = [sbt(f"rt2{i}", [P, ST], BF16) for i in range(2)]
        vstg = [sbt(f"vstg{i}", [P, ST], BF16) for i in range(2)]
        qkt = sbt("qkt", [P, 2 * T], BF16)
        vph = [sbt(f"vph{i}", [P, 65 * NKB], BF16) for i in range(2)]
        atb = [sbt(f"atb{i}", [P, SG * QS], BF16) for i in range(3)]
        ot_sm = [sbt(f"ot_sm{i}", [D, QS], BF16) for i in range(2)]
        ot_sl = [sbt(f"ot_sl{i}", [P, NCH * ST], BF16) for i in range(2)]
        wsl = [sbt(f"wsl{i}", [P, NCH * DM], BF16) for i in range(2)]
        wstg = [sbt(f"wstg{i}", [P, DM], F32) for i in range(2)]
        SCR = max(TOK, DM)
        HSC = max(TOK // 2, DM)
        f8c = sbt("f8c", [P, HSC], FP8)
        mkb = sbt("mkb", [P, ST], I16)
        bgb = sbt("bgb", [P, HSC], BF16)
        tmpF = sbt("tmpF", [P, max(2 * ST, HSC)], F32)
        ropeC = sbt("ropeCs", [P, TOK], BF16)
        ropeS = sbt("ropeSs", [P, TOK], BF16)
        tri = sbt("tris", [P, P], BF16)
        eyef = sbt("eyefs", [P, P], F32)
        onesp = sbt("onesps", [P, 1], F32)
        onesr = sbt("onesrs", [1, P], F32)
        onesF = sbt("onesFs", [P, D], F32)
        onehs = sbt("onehss", [32, 384], F32)
        jrev = sbt("jrevs", [P, P], BF16)
        permm = sbt("permm", [P, P], BF16)
        hank = sbt("hank", [P, P], BF16)
        rel8s = sbt("rel8ss", [32, HO], F32)
        ebias = sbt("ebias", [P, HO], F32)
        ediag = sbt("ediag", [P, HO * P], BF16)
        eprev = sbt("eprev", [P, HO * P], BF16)
        bcols = sbt("bcols", [P, 8 * NCH], F32)
        vbias = sbt("vbias", [P, DM], BF16)
        bvrow = sbt("bvrow", [1, DM], BF16)
        onesrb = sbt("onesrb", [1, P], BF16)
        bandF = sbt("bandF", [HO, 384], F32)
        bandB = sbt("bandB", [HO, 384], BF16)
        statc = sbt("statc", [P, 8], F32)
        row_m = sbt("row_m", [1, ST], F32)
        row_q2 = sbt("row_q2", [1, ST], F32)
        row_v = sbt("row_v", [1, ST], F32)
        row_i = row_q2
        g16 = sbt("g16", [1, 16], BF16)
        sc1 = sbt("sc1", [1, 16], F32)
        scC = sbt("scC", [P, 8], F32)
        tf32 = sbt("tf32", [P, P], F32)
        epsc = sbt("epsc", [P, 1], F32)
        rnorm = wstg[0]

        ps_a = pst("ps_a", [P, SG * QS])
        ps_b = pst("ps_b", [P, SG * QS])
        ps_o = pst("ps_o", [P, 2 * QS])
        ps_m = pst("ps_m", [P, QS])

        sems = {}
        for e in ENGINES:
            sems[("c", e)] = ctx.enter_context(nc.semaphore(f"sc_{e}"))
            for sl in range(Prog.NDQ):
                sems[("dma", e, sl)] = ctx.enter_context(
                    nc.semaphore(f"sd_{e}{sl}"))
        sems[("cc", "gpsimd")] = ctx.enter_context(nc.semaphore("s_cc"))

        bb = {}

        def N(name):
            if name not in bb:
                bb[name] = Buf(name)
            return bb[name]

        def T_(fn, r=(), w=(), a=(), inc=True):
            pr.op("tensor", fn, [N(x) for x in r], [N(x) for x in w],
                  [N(x) for x in a], inc=inc)

        def V_(fn, r=(), w=(), a=()):
            pr.op("vector", fn, [N(x) for x in r], [N(x) for x in w],
                  [N(x) for x in a])

        def S_(fn, r=(), w=(), a=()):
            pr.op("scalar", fn, [N(x) for x in r], [N(x) for x in w],
                  [N(x) for x in a])

        def Gd(fn, r=(), w=(), a=()):
            pr.op("gpsimd", fn, [N(x) for x in r], [N(x) for x in w],
                  [N(x) for x in a], dma=True)

        def Gc(fn, r=(), w=(), a=()):
            pr.op("gpsimd", fn, [N(x) for x in r], [N(x) for x in w],
                  [N(x) for x in a], dma=False)

        def Y_(fn, r=(), w=(), a=()):
            pr.op("sync", fn, [N(x) for x in r], [N(x) for x in w],
                  [N(x) for x in a], dma=True)

        def CC_(fn, r=(), w=()):
            pr.op("gpsimd", fn, [N(x) for x in r], [N(x) for x in w], cc=True)

        bb["rnorm"] = N("wstg0")

        Y_(lambda e: e.dma_start(out=ropeC[:, :], in_=d_ropeC[:, :]), w=["ropeC"])
        Y_(lambda e: e.dma_start(out=ropeS[:, :], in_=d_ropeS[:, :]), w=["ropeS"])
        Y_(lambda e: e.dma_start(out=tri[:, :], in_=d_tri[:, :]), w=["tri"])
        Y_(lambda e: e.dma_start(out=eyef[:, :], in_=d_eyef[:, :]), w=["eyef"])
        Y_(lambda e: e.dma_start(out=onesp[:, :], in_=d_onesp[:, :]), w=["onesp"])
        Y_(lambda e: e.dma_start(out=onesr[:, :], in_=d_onesr[:, :]), w=["onesr"])
        Y_(lambda e: e.dma_start(out=onehs[:, :], in_=d_oneh[:, :]), w=["oneh"])
        Y_(lambda e: e.dma_start(out=jrev[:, :], in_=d_jrev[:, :]), w=["jrev"])
        Y_(lambda e: e.dma_start(out=permm[:, :], in_=d_perm[:, :]), w=["permm"])
        Y_(lambda e: e.dma_start(out=rel8s[:, :], in_=d_rel8[:, :]), w=["rel8"])
        bslot = {}
        si = 0
        for l in (0, 1):
            for bn in ("bq", "bk", "bp", "bf"):
                bslot[l, bn] = si * NCH
                Y_(lambda e, l=l, bn=bn, s0=si * NCH: e.dma_start(
                    out=bcols[:, s0:s0 + NCH], in_=Bd[l, bn][:, :]), w=["bcols"])
                si += 1
        Gd(lambda e: e.dma_start(
            out=bass.AP(Xrs, 0, [[NCH * TOK, P], [TOK, NCH], [1, TOK]]),
            in_=bass.AP(x0T, 0, [[TOK, P], [P * TOK, NCH], [1, TOK]])), w=["Xrs"])
        Gc(lambda e: e.memset(onesF[:, :], 1.0), w=["onesF"])
        Gc(lambda e: e.memset(onesrb[:, :], 1.0), w=["onesrb"])
        Gc(lambda e: e.memset(statc[:, :], 0.0), w=["statc"])
        Gc(lambda e: e.memset(sc1[:, :], 0.0), w=["sc1"])
        Gc(lambda e: e.memset(row_m[:, :], 0.0), w=["rows"])
        Gc(lambda e: e.memset(row_q2[:, :], 0.0), w=["rows"])
        Gc(lambda e: e.memset(row_v[:, :], 0.0), w=["rows"])
        Gc(lambda e: e.memset(row_i[:, :], 0.0), w=["rows"])
        Gc(lambda e: e.memset(rnorm[:, :], 0.0), w=["rnorm"])
        Gc(lambda e: e.memset(ebias[:, :], 0.0), w=["ebias"])
        Gc(lambda e: e.memset(scC[:, :], 0.0), w=["scC"])
        Gc(lambda e: e.memset(epsc[:, :], EPS), w=["epsc"])

        def load_W(l, w, slot):
            Gd(lambda e, l=l, w=w, slot=slot: e.dma_start(
                out=bass.AP(wsl[slot], 0, [[NCH * DM, P], [DM, NCH], [1, DM]]),
                in_=bass.AP(Wd[l, w], 0, [[DM, P], [P * DM, NCH], [1, DM]])),
               w=[f"wsl{slot}"])

        load_W(0, "q", 0)
        load_W(0, "k", 1)

        dynv = {}

        def dyn(e):
            if "a1q0" not in dynv:
                pid = e.partition_id()
                half = (pid & 1) * HDT
                for r in (0, 1):
                    dynv[f"a1q{r}"] = e.snap(r * AG1_N + half)
                    dynv[f"a1k{r}"] = e.snap(r * AG1_N + DM * TOK + half)
                    dynv[f"a1v{r}"] = e.snap(r * AG1_N + V_OFF + half)
                dynv["ot_loc"] = e.snap((pid & 1) * TOK)
                dynv["ot_par"] = e.snap((1 - (pid & 1)) * AG2_N
                                        + (pid & 1) * TOK)
            return dynv

        for l in (0, 1):
            a1i, a1o, a2i, a2o = ag1_in[l], ag1_out[l], ag2_in[l], ag2_out[l]
            A1I, A1O, A2I, A2O = f"ag1i{l}", f"ag1o{l}", f"ag2i{l}", f"ag2o{l}"

            rsrc, rsn = (x0T, "x0T") if l == 0 else (x1loc, "x1loc")

            def res_slab(st, src, srcn):
                sg = st % 2
                Gd(lambda e, st=st, sg=sg, src=src: e.dma_start(
                    out=bass.AP(zsl[sg], 0, [[NCH * ST, P], [ST, NCH], [1, ST]]),
                    in_=bass.AP(src, st * ST, [[TOK, P], [P * TOK, NCH], [1, ST]])),
                   r=[srcn], a=[f"zsl{sg}"])

            res_slab(0, rsrc, rsn)
            if NST > 1:
                res_slab(1, rsrc, rsn)

            if l == 0:
                T_(lambda e: e.matmul(ps_m[0:HO, 0:384], rel8s[:, :], onehs[:, :],
                                      start=True, stop=True),
                   r=["rel8", "oneh"], w=["ps_m"])
                V_(lambda e: e.tensor_copy(bandF[:, :], ps_m[0:HO, 0:384]),
                   r=["ps_m"], w=["bandF"])
                V_(lambda e: e.tensor_copy(statc[0:HO, 6:7], bandF[:, 350:351]),
                   r=["bandF"], w=["statc"])
                V_(lambda e: e.tensor_reduce(statc[0:HO, 7:8], bandF[:, :], AxX,
                                             Alu.max), r=["bandF"], w=["statc"])
                V_(lambda e: e.tensor_single_scalar(bandF[:, :], bandF[:, :],
                                                    bandF[:, 350:351], Alu.subtract),
                   r=["bandF"], w=["bandF"])
                S_(lambda e: e.activation(bandB[:, :], bandF[:, :], Act.Exp, scale=0.125),
                   r=["bandF"], w=["bandB"])
                Gd(lambda e: e.dma_start(out=band_dram[:, :], in_=bandB[:, :]),
                   r=["bandB"], w=["band_dram"])
                for h in range(HO):
                    for dst, dn, off in ((ediag, "ediag", 0), (eprev, "eprev", 128)):
                        Gd(lambda e, h=h, off=off: e.dma_start(
                            out=hank[:, :],
                            in_=bass.AP(band_dram, h * 384 + off, [[1, P], [1, P]])),
                           r=["band_dram"], w=["hank"])
                        T_(lambda e: e.matmul(ps_m[:, 0:P], hank[:, :], jrev[:, :],
                                              start=True, stop=True),
                           r=["hank", "jrev"], w=["ps_m"])
                        V_(lambda e, dst=dst, h=h: e.tensor_copy(
                            dst[:, h * P:(h + 1) * P], ps_m[:, 0:P]),
                           r=["ps_m"], a=[dn])
                for h in range(HO):
                    V_(lambda e, h=h: e.tensor_mul(ediag[:, h * P:(h + 1) * P],
                                                   ediag[:, h * P:(h + 1) * P],
                                                   tri[:, :]),
                       r=["ediag", "tri"], w=["ediag"])

            Gc(lambda e: e.memset(statc[:, 0:2], 0.0), w=["statc"])
            for wname, slot, bn, scol, agrow in (
                    ("q", 0, "bq", 0, 0), ("k", 1, "bk", 1, DM)):
                pend_rope = []

                def emit_rope(ic, st, scol, agrow, bn):
                    rb = st % 2
                    T_(lambda e, st=st: e.matmul(
                        ps_m[:, 0:ST], permm[:, :],
                        qst[0][:, st * ST:(st + 1) * ST],
                        start=True, stop=True),
                       r=["permm", "qst0"], w=["ps_m"])
                    V_(lambda e, st=st, rb=rb: e.tensor_mul(
                        rt1[rb][:, :], qst[0][:, st * ST:(st + 1) * ST],
                        ropeC[:, st * ST:(st + 1) * ST]),
                       r=["qst0", "ropeC"], w=[f"rt1{rb}"])
                    V_(lambda e, st=st, rb=rb: e.tensor_mul(
                        rt2[rb][:, :], ps_m[:, 0:ST],
                        ropeS[:, st * ST:(st + 1) * ST]),
                       r=["ps_m", "ropeS"], w=[f"rt2{rb}"])
                    V_(lambda e, rb=rb: e.tensor_add(
                        rt1[rb][:, :], rt1[rb][:, :], rt2[rb][:, :]),
                       r=[f"rt2{rb}"], w=[f"rt1{rb}"])
                    V_(lambda e, rb=rb: e.tensor_reduce(
                        statc[:, 2:3], rt1[rb][:, :], AxX, Alu.max,
                        apply_absolute_value=True),
                       r=[f"rt1{rb}"], w=["statc"])
                    V_(lambda e, scol=scol: e.tensor_tensor(
                        statc[:, scol:scol + 1], statc[:, scol:scol + 1],
                        statc[:, 2:3], Alu.max),
                       r=["statc"], w=["statc"])
                    Y_(lambda e, rb=rb, ic=ic, st=st, agrow=agrow, a1i=a1i: e.dma_start(
                        out=bass.AP(a1i, (agrow + ic * P) * TOK + st * ST,
                                    [[TOK, P], [1, ST]]),
                        in_=rt1[rb][:, :]),
                       r=[f"rt1{rb}"], a=[A1I])

                for ic in range(NCH):
                    for st in range(NST):
                        psum, pn = (ps_a, "ps_a") if (ic * NST + st) % 2 == 0 else (ps_b, "ps_b")
                        for kc in range(NCH):
                            T_(lambda e, psum=psum, slot=slot, kc=kc, ic=ic, st=st:
                               e.matmul(psum[:, 0:ST],
                                        wsl[slot][:, kc * DM + ic * P:kc * DM + (ic + 1) * P],
                                        Xrs[:, kc * TOK + st * ST:kc * TOK + (st + 1) * ST],
                                        start=(kc == 0), stop=(kc == NCH - 1)),
                               r=[f"wsl{slot}", "Xrs"], w=[pn], inc=(kc == NCH - 1))
                        S_(lambda e, psum=psum, st=st, s0=bslot[l, bn], ic=ic:
                           e.add(qst[0][:, st * ST:(st + 1) * ST], psum[:, 0:ST],
                                 bcols[:, s0 + ic:s0 + ic + 1]),
                           r=[pn, "bcols"], a=["qst0"])
                        pend_rope.append(
                            lambda ic=ic, st=st, scol=scol, agrow=agrow, bn=bn:
                            emit_rope(ic, st, scol, agrow, bn))
                        while len(pend_rope) > (1 if NST > 1 else 0):
                            pend_rope.pop(0)()
                while pend_rope:
                    pend_rope.pop(0)()
                if wname == "q":
                    load_W(l, "v", 0)
                else:
                    load_W(l, "p", 1)

            Gd(lambda e, l=l: e.dma_start(out=bvrow[:, :], in_=Bd[l, "bv"][:, :]),
               w=["bvrow"])
            for j in range(DM // ST):
                T_(lambda e, j=j: e.matmul(ps_m[:, 0:ST], onesrb[:, :],
                                           bvrow[:, j * ST:(j + 1) * ST],
                                           start=True, stop=True),
                   r=["onesrb", "bvrow"], w=["ps_m"])
                V_(lambda e, j=j: e.tensor_copy(vbias[:, j * ST:(j + 1) * ST],
                                                ps_m[:, 0:ST]),
                   r=["ps_m"], w=["vbias"])
            for tt in range(TOK // P):
                for vc in range(DM // ST):
                    sg = (tt * (DM // ST) + vc) % 2
                    psum, pn = (ps_a, "ps_a") if sg == 0 else (ps_b, "ps_b")
                    for kc in range(NCH):
                        T_(lambda e, psum=psum, kc=kc, tt=tt, vc=vc: e.matmul(
                            psum[:, 0:ST],
                            Xrs[:, kc * TOK + tt * P:kc * TOK + (tt + 1) * P],
                            wsl[0][:, kc * DM + vc * ST:kc * DM + (vc + 1) * ST],
                            start=(kc == 0), stop=(kc == NCH - 1)),
                           r=["wsl0", "Xrs"], w=[pn], inc=(kc == NCH - 1))
                    V_(lambda e, psum=psum, sg=sg, vc=vc: e.tensor_add(
                        vstg[sg][:, :], psum[:, 0:ST], vbias[:, vc * ST:(vc + 1) * ST]),
                       r=[pn, "vbias"], w=[f"vstg{sg}"])
                    Y_(lambda e, sg=sg, tt=tt, vc=vc, a1i=a1i: e.dma_start(
                        out=bass.AP(a1i, V_OFF + vc * (ST // D) * TOK * D + tt * P * D,
                                    [[D, P], [TOK * D, ST // D], [1, D]]),
                        in_=bass.AP(vstg[sg], 0, [[ST, P], [D, ST // D], [1, D]])),
                       r=[f"vstg{sg}"], a=[A1I])

            T_(lambda e: e.transpose(ps_m[0:2, 0:P], statc[:, 0:2], eyef[:, :]),
               r=["statc", "eyef"], w=["ps_m"])
            V_(lambda e: e.tensor_reduce(statc[0:2, 3:4], ps_m[0:2, 0:P], AxX, Alu.max),
               r=["ps_m"], w=["statc"])
            Y_(lambda e, a1i=a1i: e.dma_start(
                out=bass.AP(a1i, ST_OFF + 2, [[2046, 1], [1, 2046]]),
                in_=Xrs[0:1, 0:2046]),
               r=["Xrs"], a=[A1I])
            Gd(lambda e, a1i=a1i: e.dma_start(
                out=bass.AP(a1i, ST_OFF, [[1, 2], [1, 1]]), in_=statc[0:2, 3:4]),
               r=["statc"], a=[A1I])
            CC_(lambda e, l=l: e.collective_compute(
                "AllGather", Alu.bypass, replica_groups=PAIRS,
                ins=[ag1_in[l].ap().opt()], outs=[ag1_out[l].ap().opt()]),
                r=[A1I], w=[A1O])

            fcol = 0
            for ic in range(NCH):
                sg = ic % 2
                Y_(lambda e, l=l, ic=ic, sg=sg: e.dma_start(
                    out=wstg[sg][:, :],
                    in_=bass.AP(Wd[l, "f"], ic * P * DM, [[DM, P], [1, DM]])),
                   w=[f"wstg{sg}"])
                V_(lambda e, sg=sg, l=l: e.tensor_reduce(
                    statc[:, 4:5], wstg[sg][:, :], AxX,
                    Alu.add if l == 0 else Alu.max, apply_absolute_value=True),
                   r=[f"wstg{sg}"], w=["statc"])
                V_(lambda e, l=l: e.tensor_tensor(
                    statc[:, 5:6], statc[:, 5:6], statc[:, 4:5],
                    Alu.add if l == 0 else Alu.max),
                   r=["statc"], w=["statc"])
            T_(lambda e: e.transpose(ps_m[0:1, 0:P], statc[:, 5:6], eyef[:, :]),
               r=["statc", "eyef"], w=["ps_m"])
            V_(lambda e, l=l: e.tensor_reduce(sc1[0:1, 0:1], ps_m[0:1, 0:P], AxX,
                                              Alu.add if l == 0 else Alu.max),
               r=["ps_m"], w=["sc1"])
            Gc(lambda e: e.memset(statc[:, 5:6], 0.0), w=["statc"])
            if l == 0:
                V_(lambda e: e.tensor_scalar(sc1[0:1, 1:2], sc1[0:1, 0:1],
                                             0.5 / (DM * DM), None, Alu.mult),
                   r=["sc1"], w=["sc1"])
                V_(lambda e: e.tensor_scalar(sc1[0:1, 2:3], sc1[0:1, 1:2],
                                             -1.0, None, Alu.mult),
                   r=["sc1"], w=["sc1"])
                pairs = [(1, 0), (2, 1)]
            else:
                V_(lambda e: e.tensor_scalar(sc1[0:1, 1:2], sc1[0:1, 0:1],
                                             1e-12, None, Alu.max),
                   r=["sc1"], w=["sc1"])
                V_(lambda e: e.reciprocal(sc1[0:1, 1:2], sc1[0:1, 1:2]),
                   r=["sc1"], w=["sc1"])
                V_(lambda e: e.tensor_scalar(sc1[0:1, 1:2], sc1[0:1, 1:2],
                                             448.0, None, Alu.mult),
                   r=["sc1"], w=["sc1"])
                V_(lambda e: e.tensor_scalar(sc1[0:1, 2:3], sc1[0:1, 1:2],
                                             0.5, None, Alu.mult),
                   r=["sc1"], w=["sc1"])
                pairs = [(1, 0), (2, 1)]
            for (c_in, c_out) in pairs:
                T_(lambda e, c_in=c_in: e.matmul(ps_m[:, 0:1], onesr[:, :],
                                                 sc1[0:1, c_in:c_in + 1],
                                                 start=True, stop=True),
                   r=["onesr", "sc1"], w=["ps_m"])
                V_(lambda e, c_out=c_out: e.tensor_copy(scC[:, c_out:c_out + 1],
                                                        ps_m[:, 0:1]),
                   r=["ps_m"], w=["scC"])
            for oc in range(NCH):
                sg = oc % 2
                Y_(lambda e, l=l, oc=oc, sg=sg: e.dma_start(
                    out=wstg[sg][:, :],
                    in_=bass.AP(Wd[l, "f"], oc * P * DM, [[DM, P], [1, DM]])),
                   w=[f"wstg{sg}"])
                for tc in range(NCH):
                    dst = (slice(None), slice(tc * DM + oc * P, tc * DM + (oc + 1) * P))
                    T_(lambda e, sg=sg, tc=tc: e.transpose(
                        ps_m[:, 0:P], wstg[sg][:, tc * P:(tc + 1) * P], eyef[:, :]),
                       r=[f"wstg{sg}", "eyef"], w=["ps_m"])
                    V_(lambda e: e.tensor_copy(tf32[:, :], ps_m[:, 0:P]),
                       r=["ps_m"], w=["tf32"])
                    if l == 0:
                        V_(lambda e: e.tensor_single_scalar(
                            mkb[:, 0:P], tf32[:, :], scC[:, 0:1], Alu.is_gt),
                           r=["tf32", "scC"], w=["mkb"])
                        V_(lambda e: e.tensor_single_scalar(
                            bgb[:, 0:P], tf32[:, :], scC[:, 1:2], Alu.is_lt),
                           r=["tf32", "scC"], w=["bgb"])
                        V_(lambda e, dst=dst: e.tensor_tensor(
                            wsl[0][dst], mkb[:, 0:P], bgb[:, 0:P], Alu.subtract),
                           r=["mkb", "bgb"], a=["wsl0"])
                    else:
                        V_(lambda e: e.tensor_single_scalar(
                            f8c[:, 0:P], tf32[:, :], scC[:, 1:2], Alu.mult),
                           r=["tf32", "scC"], w=["f8c"])
                        V_(lambda e: e.tensor_scalar(
                            bgb[:, 0:P], f8c[:, 0:P], 2.0, None, Alu.mult),
                           r=["f8c"], w=["bgb"])
                        V_(lambda e: e.tensor_single_scalar(
                            tmpF[:, 0:P], tf32[:, :], scC[:, 0:1], Alu.mult),
                           r=["tf32", "scC"], w=["tmpF"])
                        S_(lambda e: e.activation(
                            tmpF[:, P:2 * P], tmpF[:, 0:P], Act.Abs),
                           r=["tmpF"], w=["tmpF"])
                        V_(lambda e: e.tensor_scalar(
                            mkb[:, 0:P], tmpF[:, P:2 * P], FP8_SMALL, None,
                            Alu.is_ge),
                           r=["tmpF"], w=["mkb"])
                        V_(lambda e, dst=dst: e.tensor_scalar(
                            wsl[0][dst], tmpF[:, 0:P], C_Q9, C_Q9, Alu.add,
                            Alu.subtract),
                           r=["tmpF"], a=["wsl0"])
                        V_(lambda e, dst=dst: e.copy_predicated(
                            wsl[0][dst], mkb[:, 0:P], bgb[:, 0:P]),
                           r=["mkb", "bgb", "wsl0"], a=["wsl0"])

            Y_(lambda e, l=l: e.dma_start(
                out=bass.AP(g16, 0, [[16, 1], [2, 2], [1, 2]]),
                in_=bass.AP(ag1_out[l], ST_OFF, [[1, 1], [AG1_N, 2], [1, 2]])),
               r=[A1O], w=["g16"])
            V_(lambda e: e.tensor_reduce(sc1[0:1, 5:6],
                                         bass.AP(g16, 0, [[16, 1], [2, 2]]),
                                         AxX, Alu.max), r=["g16"], w=["sc1"])
            V_(lambda e: e.tensor_reduce(sc1[0:1, 6:7],
                                         bass.AP(g16, 1, [[16, 1], [2, 2]]),
                                         AxX, Alu.max), r=["g16"], w=["sc1"])
            V_(lambda e: e.tensor_tensor(sc1[0:1, 7:8], sc1[0:1, 5:6],
                                         sc1[0:1, 6:7], Alu.mult),
               r=["sc1"], w=["sc1"])
            V_(lambda e: e.tensor_scalar(sc1[0:1, 7:8], sc1[0:1, 7:8], 8.0,
                                         None, Alu.mult), r=["sc1"], w=["sc1"])
            if l == 0:
                T_(lambda e: e.transpose(ps_m[0:1, 0:P], statc[:, 7:8], eyef[:, :]),
                   r=["statc", "eyef"], w=["ps_m"])
                V_(lambda e: e.tensor_reduce(sc1[0:1, 4:5], ps_m[0:1, 0:HO], AxX,
                                             Alu.max), r=["ps_m"], w=["sc1"])
                V_(lambda e: e.tensor_scalar(sc1[0:1, 4:5], sc1[0:1, 4:5], 0.125,
                                             None, Alu.mult), r=["sc1"], w=["sc1"])
                V_(lambda e: e.tensor_tensor(sc1[0:1, 7:8], sc1[0:1, 7:8],
                                             sc1[0:1, 4:5], Alu.add),
                   r=["sc1"], w=["sc1"])
            V_(lambda e: e.tensor_scalar(sc1[0:1, 7:8], sc1[0:1, 7:8], M_CAP,
                                         None, Alu.min), r=["sc1"], w=["sc1"])
            T_(lambda e: e.matmul(ps_m[:, 0:1], onesr[:, :], sc1[0:1, 7:8],
                                  start=True, stop=True),
               r=["onesr", "sc1"], w=["ps_m"])
            V_(lambda e: e.tensor_copy(scC[:, 4:5], ps_m[:, 0:1]),
               r=["ps_m"], w=["scC"])
            if l == 0:
                T_(lambda e: e.transpose(ps_m[0:1, 0:P], statc[:, 6:7], eyef[:, :]),
                   r=["statc", "eyef"], w=["ps_m"])
                V_(lambda e: e.tensor_copy(sc1[0:1, 8:8 + HO], ps_m[0:1, 0:HO]),
                   r=["ps_m"], w=["sc1"])
                T_(lambda e: e.matmul(ps_m[:, 0:HO], onesr[:, :], sc1[0:1, 8:8 + HO],
                                      start=True, stop=True),
                   r=["onesr", "sc1"], w=["ps_m"])
                V_(lambda e: e.tensor_scalar(ebias[:, :], ps_m[:, 0:HO], 0.125,
                                             None, Alu.mult),
                   r=["ps_m"], w=["ebias"])
                V_(lambda e: e.tensor_single_scalar(ebias[:, :], ebias[:, :],
                                                    scC[:, 4:5], Alu.subtract),
                   r=["scC"], w=["ebias"])
            else:
                Gc(lambda e: e.memset(ebias[:, :], 0.0), w=["ebias"])
                V_(lambda e: e.tensor_single_scalar(ebias[:, :], ebias[:, :],
                                                    scC[:, 4:5], Alu.subtract),
                   r=["scC"], w=["ebias"])

            for (r, c, cn) in ((0, 0, "q"), (0, 1, "k"), (1, 0, "q"),
                               (1, 1, "k"), (0, 2, "v"), (1, 2, "v")):
                def fcp(e, r=r, c=c, cn=cn, l=l, a1o=a1o):
                    dv = dyn(e)
                    return e.dma_start(
                        out=qkvloc[l][0:1, (r * 3 + c) * HDT:(r * 3 + c + 1) * HDT],
                        in_=a1o[0:1, bass.ds(dv[f"a1{cn}{r}"], HDT)],
                        bounds_check=None)
                Y_(fcp, r=[A1O], a=[f"qkvloc{l}_{r}{cn}"])

            def head_dma(h):
                hb = h % 2
                rb0 = hb * D
                for r in (0, 1):
                    Y_(lambda e, h=h, rb0=rb0, r=r, l=l: e.dma_start(
                        out=qkt[rb0:rb0 + D, r * TOK:(r + 1) * TOK],
                        in_=bass.AP(qkvloc[l], r * 3 * HDT + h * D * TOK,
                                    [[TOK, D], [1, TOK]])),
                       r=[f"qkvloc{l}_{r}q"], a=[f"qkt{hb}"])
                    Y_(lambda e, h=h, rb0=rb0, r=r, l=l: e.dma_start(
                        out=qkt[rb0:rb0 + D, T + r * TOK:T + (r + 1) * TOK],
                        in_=bass.AP(qkvloc[l], (r * 3 + 1) * HDT + h * D * TOK,
                                    [[TOK, D], [1, TOK]])),
                       r=[f"qkvloc{l}_{r}k"], a=[f"qkt{hb}"])
                    Y_(lambda e, h=h, hb=hb, r=r, l=l: e.dma_start(
                        out=bass.AP(vph[hb], r * (TOK // P) * 65,
                                    [[65 * NKB, P], [65, TOK // P], [1, D]]),
                        in_=bass.AP(qkvloc[l], (r * 3 + 2) * HDT + h * TOK * D,
                                    [[D, P], [P * D, TOK // P], [1, D]])),
                       r=[f"qkvloc{l}_{r}v"], a=[f"vph{hb}"])

            for i in range(2):
                Gc(lambda e, i=i: e.memset(
                    bass.AP(vph[i], 64, [[65 * NKB, P], [65, NKB]]), 1.0),
                   r=[f"vph{i}"], w=[f"vph{i}"])
            head_dma(0)
            pend_av = []
            pend_norm = []

            def emit_S(h, hb, rb0, qs, g0, gsz, gb, psum, pn):
                for j in range(gsz):
                    kb = g0 + j
                    T_(lambda e, psum=psum, rb0=rb0, kb=kb, qs=qs, j=j:
                       e.matmul(psum[:, j * QS:(j + 1) * QS],
                                qkt[rb0:rb0 + D, T + kb * P:T + (kb + 1) * P],
                                qkt[rb0:rb0 + D, qs * QS:(qs + 1) * QS],
                                start=True, stop=True),
                       r=[f"qkt{hb}"], w=[pn], inc=(j == gsz - 1))
                S_(lambda e, psum=psum, gb=gb, gsz=gsz, h=h: e.activation(
                    atb[gb][:, 0:gsz * QS], psum[:, 0:gsz * QS], Act.Exp,
                    bias=ebias[:, h:h + 1], scale=0.125),
                   r=[pn, "ebias"], w=[f"atb{gb}"])
                qb0 = qs * KPQ
                for j in range(gsz):
                    kb = g0 + j
                    t = kb - qb0
                    nz = min(max(t, 0), KPQ)
                    if nz > 0:
                        V_(lambda e, gb=gb, j=j, nz=nz: e.memset(
                            atb[gb][:, j * QS:j * QS + nz * P], 0.0),
                           w=[f"atb{gb}"])
                    if 0 <= t < KPQ:
                        if l == 0:
                            V_(lambda e, gb=gb, j=j, t=t, h=h: e.tensor_mul(
                                atb[gb][:, j * QS + t * P:j * QS + (t + 1) * P],
                                atb[gb][:, j * QS + t * P:j * QS + (t + 1) * P],
                                ediag[:, h * P:(h + 1) * P]),
                               r=["ediag"], w=[f"atb{gb}"])
                        else:
                            V_(lambda e, gb=gb, j=j, t=t: e.tensor_mul(
                                atb[gb][:, j * QS + t * P:j * QS + (t + 1) * P],
                                atb[gb][:, j * QS + t * P:j * QS + (t + 1) * P],
                                tri[:, :]),
                               r=["tri"], w=[f"atb{gb}"])
                    if l == 0 and 0 <= t + 1 < KPQ:
                        V_(lambda e, gb=gb, j=j, t=t, h=h: e.tensor_mul(
                            atb[gb][:, j * QS + (t + 1) * P:j * QS + (t + 2) * P],
                            atb[gb][:, j * QS + (t + 1) * P:j * QS + (t + 2) * P],
                            eprev[:, h * P:(h + 1) * P]),
                           r=["eprev"], w=[f"atb{gb}"])

            def emit_AV(hb, qs, g0, gsz, gb, nkb, po):
                for j in range(gsz):
                    kb = g0 + j
                    T_(lambda e, hb=hb, kb=kb, gb=gb, j=j, nkb=nkb, po=po:
                       e.matmul(ps_o[0:65, po * QS:po * QS + QS],
                                vph[hb][:, kb * 65:(kb + 1) * 65],
                                atb[gb][:, j * QS:(j + 1) * QS],
                                start=(kb == 0), stop=(kb == nkb - 1)),
                       r=[f"vph{hb}", f"atb{gb}"], w=[f"ps_o{po}"],
                       inc=(kb == nkb - 1))

            def emit_norm(h, hb, qs, po):
                V_(lambda e, po=po: e.tensor_scalar(
                    rnorm[64:65, 0:QS], ps_o[64:65, po * QS:po * QS + QS],
                    1e-35, None, Alu.max),
                   r=[f"ps_o{po}"], w=["rnorm"])
                V_(lambda e: e.reciprocal(rnorm[64:65, 0:QS],
                                          rnorm[64:65, 0:QS]),
                   r=["rnorm"], w=["rnorm"])
                T_(lambda e: e.matmul(ps_m[0:D, 0:QS], onesF[D:D + 1, :],
                                      rnorm[64:65, 0:QS], start=True, stop=True),
                   r=["onesF", "rnorm"], w=["ps_m"])
                V_(lambda e: e.tensor_copy(rnorm[0:D, 0:QS], ps_m[0:D, 0:QS]),
                   r=["ps_m"], w=["rnorm"])
                ob = qs % 2
                V_(lambda e, ob=ob, po=po: e.tensor_mul(
                    ot_sm[ob][:, :], ps_o[0:D, po * QS:po * QS + QS],
                    rnorm[0:D, 0:QS]),
                   r=[f"ps_o{po}", "rnorm"], w=[f"ot_sm{ob}"])
                Y_(lambda e, ob=ob, h=h, qs=qs, a2i=a2i: e.dma_start(
                    out=bass.AP(a2i, h * D * T + qs * QS, [[T, D], [1, QS]]),
                    in_=ot_sm[ob][:, :]),
                   r=[f"ot_sm{ob}"], a=[A2I])

            NORM_LAG = 3

            def pop_av():
                av, fin, fpo = pend_av.pop(0)
                av()
                if fin is not None:
                    pend_norm.append((gctr + NORM_LAG, fpo, fin))

            def pop_norms(ready_only=True, po=None):
                while pend_norm:
                    rdy, npo, fn = pend_norm[0]
                    if ready_only and rdy > gctr and not (po is not None and npo == po):
                        break
                    pend_norm.pop(0)
                    fn()

            gctr = 0
            for h in range(HO):
                hb = h % 2
                rb0 = hb * D
                hctr = 0
                for qs in range(NQS):
                    po = qs % 2
                    nkb = KPQ * (qs + 1)
                    glist = list(_groups(nkb))
                    for gi, (g0, gsz) in enumerate(glist):
                        gb = gctr % 3
                        psum, pn = (ps_a, "ps_a") if gctr % 2 == 0 else (ps_b, "ps_b")
                        emit_S(h, hb, rb0, qs, g0, gsz, gb, psum, pn)
                        pop_norms(po=po if gi == 0 else None)
                        if len(pend_av) >= 2:
                            pop_av()
                        if hctr == 1 and h + 1 < HO:
                            head_dma(h + 1)
                        fin = (lambda h=h, hb=hb, qs=qs, po=po:
                               emit_norm(h, hb, qs, po)) if gi == len(glist) - 1 else None
                        pend_av.append(
                            (lambda hb=hb, qs=qs, g0=g0, gsz=gsz, gb=gb, nkb=nkb,
                             po=po: emit_AV(hb, qs, g0, gsz, gb, nkb, po), fin, po))
                        gctr += 1
                        hctr += 1
            while pend_av:
                pop_av()
                gctr += 1
                pop_norms()
            pop_norms(ready_only=False)
            Y_(lambda e, a2i=a2i: e.dma_start(
                out=a2i[0:1, AG2_N:AG2_N + TOK], in_=Xrs[0:1, 0:TOK]),
               r=["Xrs"], a=[A2I])

            def floc(e, l=l, a2i=a2i):
                dv = dyn(e)
                return e.dma_start(
                    out=ot2loc[l][0:1, 0:(DM // 2) * T],
                    in_=a2i[0:1, bass.ds(dv["ot_loc"], (DM // 2) * T)],
                    bounds_check=None)
            Y_(floc, r=[A2I], a=[f"ot2loc{l}_0"])
            CC_(lambda e, l=l: e.collective_compute(
                "AllGather", Alu.bypass, replica_groups=PAIRS,
                ins=[ag2_in[l][0:1, 0:AG2_N].opt()],
                outs=[ag2_out[l][0:1, 0:2 * AG2_N].opt()]),
                r=[A2I], w=[A2O])

            Y_(lambda e, a2o=a2o: e.dma_start(
                out=a2o[0:1, 2 * AG2_N:2 * AG2_N + TOK], in_=Xrs[0:1, 0:TOK]),
               r=["Xrs"], a=[A2O])
            QT = P * T
            for j in range(4):
                def fpar(e, j=j, l=l, a2o=a2o):
                    dv = dyn(e)
                    return e.dma_start(
                        out=ot2loc[l][0:1, (DM // 2) * T + j * QT:
                                      (DM // 2) * T + (j + 1) * QT],
                        in_=a2o[0:1, bass.ds(dv["ot_par"], (DM // 2) * T)][
                            0:1, j * QT:(j + 1) * QT],
                        bounds_check=None)
                Y_(fpar, r=[A2O], a=[f"ot2loc{l}_1{j}"])

            def pslice_dma(st, ic):
                sg = st % 2
                r = 0 if ic < NCH // 2 else 1
                row0 = (ic % (NCH // 2)) * P
                rn = (f"ot2loc{l}_0" if r == 0
                      else f"ot2loc{l}_1{ic - NCH // 2}")
                Y_(lambda e, st=st, ic=ic, sg=sg, r=r, row0=row0, l=l: e.dma_start(
                    out=ot_sl[sg][:, ic * ST:(ic + 1) * ST],
                    in_=bass.AP(ot2loc[l], r * (DM // 2) * T + row0 * T + st * ST,
                                [[T, P], [1, ST]])),
                   r=[rn], a=[f"ot_sl{sg}"])

            QW = min(TOK, HSC)
            NQH = TOK // QW

            def ln_sums(st):
                sb = st % 2
                zs, zn = zsl[sb], f"zsl{sb}"
                for ic in range(NCH):
                    S_(lambda e, zs=zs, zn=zn, ic=ic: e.activation(
                        tmpF[:, ST:2 * ST], zs[:, ic * ST:(ic + 1) * ST],
                        Act.Square),
                       r=[zn], w=["tmpF"])
                    T_(lambda e, zs=zs, ic=ic: e.matmul(
                        ps_o[0:1, 0:ST], onesp[:, :], zs[:, ic * ST:(ic + 1) * ST],
                        start=(ic == 0), stop=(ic == NCH - 1)),
                       r=["onesp", zn], w=["ps_o0"], inc=True)
                    T_(lambda e, ic=ic: e.matmul(
                        ps_m[0:1, 0:ST], onesp[:, :], tmpF[:, ST:2 * ST],
                        start=(ic == 0), stop=(ic == NCH - 1)),
                       r=["onesp", "tmpF"], w=["ps_m"], inc=True)
                V_(lambda e: e.tensor_scalar(row_m[:, :], ps_o[0:1, 0:ST],
                                             1.0 / DM, None, Alu.mult),
                   r=["ps_o0"], w=["rows"])
                V_(lambda e: e.tensor_scalar(row_q2[:, :], ps_m[0:1, 0:ST],
                                             1.0 / DM, None, Alu.mult),
                   r=["ps_m"], w=["rows"])
                V_(lambda e: e.tensor_mul(row_v[:, :], row_m[:, :], row_m[:, :]),
                   r=["rows"], w=["rows"])
                V_(lambda e: e.tensor_tensor(row_v[:, :], row_q2[:, :],
                                             row_v[:, :], Alu.subtract),
                   r=["rows"], w=["rows"])
                S_(lambda e: e.activation(row_i[:, :], row_v[:, :], Act.Ln,
                                          bias=epsc[0:1, 0:1]),
                   r=["rows", "epsc"], w=["rows"])
                S_(lambda e: e.activation(row_i[:, :], row_i[:, :], Act.Exp,
                                          scale=-0.5), r=["rows"], w=["rows"])

            def ln_apply(st, spill_x1=False, amax=False, write_out=False):
                sb = st % 2
                zs, zn = zsl[sb], f"zsl{sb}"
                T_(lambda e: e.matmul(ps_o[:, QS:QS + ST], onesr[:, :],
                                      row_m[:, :], start=True, stop=True),
                   r=["onesr", "rows"], w=["ps_o1"])
                T_(lambda e: e.matmul(ps_m[:, 0:ST], onesr[:, :], row_i[:, :],
                                      start=True, stop=True),
                   r=["onesr", "rows"], w=["ps_m"])
                for ic in range(NCH):
                    V_(lambda e, zs=zs, zn=zn, ic=ic: e.tensor_tensor(
                        tmpF[:, 0:ST], zs[:, ic * ST:(ic + 1) * ST],
                        ps_o[:, QS:QS + ST], Alu.subtract),
                       r=[zn, "ps_o1"], w=["tmpF"])
                    V_(lambda e, zs=zs, zn=zn, ic=ic: e.tensor_tensor(
                        zs[:, ic * ST:(ic + 1) * ST], tmpF[:, 0:ST],
                        ps_m[:, 0:ST], Alu.mult),
                       r=["tmpF", "ps_m"], a=[zn])
                    V_(lambda e, zs=zs, zn=zn, ic=ic, st=st: e.tensor_copy(
                        Xrs[:, ic * TOK + st * ST:ic * TOK + (st + 1) * ST],
                        zs[:, ic * ST:(ic + 1) * ST]),
                       r=[zn], a=["Xrs"])
                if spill_x1:
                    Y_(lambda e, zs=zs, st=st: e.dma_start(
                        out=bass.AP(x1loc, st * ST,
                                    [[TOK, P], [P * TOK, NCH], [1, ST]]),
                        in_=bass.AP(zs, 0, [[NCH * ST, P], [ST, NCH], [1, ST]])),
                       r=[zn], a=["x1loc"])
                if amax:
                    V_(lambda e, zs=zs: e.tensor_reduce(
                        statc[:, 4:5], zs[:, :], AxX, Alu.max,
                        apply_absolute_value=True),
                       r=[zn], w=["statc"])
                    V_(lambda e: e.tensor_tensor(statc[:, 5:6], statc[:, 5:6],
                                                 statc[:, 4:5], Alu.max),
                       r=["statc"], w=["statc"])
                if write_out:
                    Y_(lambda e, zs=zs, st=st: e.dma_start(
                        out=bass.AP(o_xT, st * ST,
                                    [[TOK, P], [P * TOK, NCH], [1, ST]]),
                        in_=bass.AP(zs, 0, [[NCH * ST, P], [ST, NCH], [1, ST]])),
                       r=[zn], w=["o_xT"] if st == NST - 1 else [], a=[] if st == NST - 1 else ["o_xT"])

            Gc(lambda e: e.memset(statc[:, 5:6], 0.0), w=["statc"])
            pend_apply = []
            for ic in range(NCH):
                pslice_dma(0, ic)
            for st in range(NST):
                sg = st % 2
                if st + 1 < NST:
                    for ic in range(NCH):
                        pslice_dma(st + 1, ic)
                for oc in range(NCH):
                    psum, pn = (ps_a, "ps_a") if oc % 2 == 0 else (ps_b, "ps_b")
                    for ic in range(NCH):
                        T_(lambda e, psum=psum, ic=ic, oc=oc, sg=sg: e.matmul(
                            psum[:, 0:ST],
                            wsl[1][:, ic * DM + oc * P:ic * DM + (oc + 1) * P],
                            ot_sl[sg][:, ic * ST:(ic + 1) * ST],
                            start=(ic == 0), stop=(ic == NCH - 1)),
                           r=["wsl1", f"ot_sl{sg}"], w=[pn], inc=(ic == NCH - 1))
                    V_(lambda e, psum=psum, oc=oc, sg=sg,
                       s0=bslot[l, "bp"]: e.scalar_tensor_tensor(
                           zsl[sg][:, oc * ST:(oc + 1) * ST],
                           psum[:, 0:ST], bcols[:, s0 + oc:s0 + oc + 1],
                           zsl[sg][:, oc * ST:(oc + 1) * ST],
                           Alu.add, Alu.add),
                       r=[pn, "bcols", f"zsl{sg}"], a=[f"zsl{sg}"])
                    if oc == 2 and pend_apply:
                        pend_apply.pop(0)()
                ln_sums(st)
                pend_apply.append(
                    lambda st=st: (ln_apply(st, spill_x1=True, amax=True),
                                   res_slab(st + 2, rsrc, rsn)
                                   if st + 2 < NST else None))
            while pend_apply:
                pend_apply.pop(0)()

            T_(lambda e: e.transpose(ps_m[0:1, 0:P], statc[:, 5:6], eyef[:, :]),
               r=["statc", "eyef"], w=["ps_m"])
            V_(lambda e: e.tensor_reduce(sc1[0:1, 3:4], ps_m[0:1, 0:P], AxX, Alu.max),
               r=["ps_m"], w=["sc1"])
            Y_(lambda e, l=l: e.dma_start(out=ar_in[l][0:1, 0:8],
                                          in_=sc1[0:1, 3:11]),
               r=["sc1"], w=[f"ari{l}"])
            CC_(lambda e, l=l: e.collective_compute(
                "AllReduce", Alu.max, replica_groups=[list(range(8))],
                ins=[ar_in[l].ap().opt()], outs=[ar_out[l].ap().opt()]),
                r=[f"ari{l}"], w=[f"aro{l}"])
            Y_(lambda e, l=l: e.dma_start(out=sc1[0:1, 4:5],
                                          in_=ar_out[l][0:1, 0:1]),
               r=[f"aro{l}"], w=["sc1"])
            if l == 0:
                V_(lambda e: e.reciprocal(sc1[0:1, 9:10], sc1[0:1, 4:5]),
                   r=["sc1"], w=["sc1"])
                V_(lambda e: e.tensor_scalar(sc1[0:1, 9:10], sc1[0:1, 9:10],
                                             127.0, None, Alu.mult),
                   r=["sc1"], w=["sc1"])
                V_(lambda e: e.tensor_scalar(sc1[0:1, 10:11], sc1[0:1, 4:5],
                                             1.0 / 127.0, None, Alu.mult),
                   r=["sc1"], w=["sc1"])
                V_(lambda e: e.tensor_scalar(sc1[0:1, 11:12], sc1[0:1, 0:1],
                                             1.0 / (DM * DM), None, Alu.mult),
                   r=["sc1"], w=["sc1"])
                V_(lambda e: e.tensor_tensor(sc1[0:1, 10:11], sc1[0:1, 10:11],
                                             sc1[0:1, 11:12], Alu.mult),
                   r=["sc1"], w=["sc1"])
                pairs2 = [(9, 2), (10, 5)]
            else:
                V_(lambda e: e.tensor_scalar(sc1[0:1, 9:10], sc1[0:1, 4:5],
                                             1e-12, None, Alu.max),
                   r=["sc1"], w=["sc1"])
                V_(lambda e: e.reciprocal(sc1[0:1, 9:10], sc1[0:1, 9:10]),
                   r=["sc1"], w=["sc1"])
                V_(lambda e: e.tensor_scalar(sc1[0:1, 9:10], sc1[0:1, 9:10],
                                             448.0, None, Alu.mult),
                   r=["sc1"], w=["sc1"])
                V_(lambda e: e.tensor_scalar(sc1[0:1, 10:11], sc1[0:1, 9:10],
                                             0.5, None, Alu.mult),
                   r=["sc1"], w=["sc1"])
                V_(lambda e: e.tensor_tensor(sc1[0:1, 11:12], sc1[0:1, 9:10],
                                             sc1[0:1, 1:2], Alu.mult),
                   r=["sc1"], w=["sc1"])
                V_(lambda e: e.reciprocal(sc1[0:1, 11:12], sc1[0:1, 11:12]),
                   r=["sc1"], w=["sc1"])
                pairs2 = [(9, 2), (10, 3), (11, 5)]
            for (c_in, c_out) in pairs2:
                T_(lambda e, c_in=c_in: e.matmul(
                    ps_m[:, 0:1], onesr[:, :], sc1[0:1, c_in:c_in + 1],
                    start=True, stop=True), r=["onesr", "sc1"], w=["ps_m"])
                V_(lambda e, c_out=c_out: e.tensor_copy(
                    scC[:, c_out:c_out + 1], ps_m[:, 0:1]),
                   r=["ps_m"], w=["scC"])
            res_slab(0, x1loc, "x1loc")
            if NST > 1:
                res_slab(1, x1loc, "x1loc")
            xq_map = []
            caps = [("ot_sl0", ot_sl[0], NCH * ST), ("ot_sl1", ot_sl[1], NCH * ST),
                    ("qst0", qst[0], TOK), ("xqb", xqb, TOK),
                    ("vph0", vph[0], TOK), ("vph1", vph[1], TOK)]
            ci_, off_ = 0, 0
            for ic in range(NCH):
                while caps[ci_][2] - off_ < TOK:
                    ci_, off_ = ci_ + 1, 0
                xq_map.append((caps[ci_][0], caps[ci_][1], off_))
                off_ += TOK
            def quant_tile(st, ic):
                sg = st % 2
                nm, tns, off = xq_map[ic]
                zt = lambda: zsl[sg][:, ic * ST:(ic + 1) * ST]
                if l == 0:
                    V_(lambda e, zt=zt: e.tensor_scalar(
                        x1c[:, 0:ST], zt(), scC[:, 2:3], C_RND, Alu.mult, Alu.add),
                       r=[f"zsl{sg}", "scC"], w=["x1c"])
                    V_(lambda e, tns=tns, off=off, st=st: e.tensor_scalar(
                        tns[:, off + st * ST:off + (st + 1) * ST],
                        x1c[:, 0:ST], C_RND, None, Alu.subtract),
                       r=["x1c"], a=[nm])
                else:
                    S_(lambda e, zt=zt: e.mul(x1c[:, 0:ST], zt(), scC[:, 2:3]),
                       r=[f"zsl{sg}", "scC"], w=["x1c"])
                    V_(lambda e, zt=zt: e.tensor_single_scalar(
                        f8c[:, 0:ST], zt(), scC[:, 3:4], Alu.mult),
                       r=[f"zsl{sg}", "scC"], w=["f8c"])
                    V_(lambda e: e.tensor_scalar(
                        bgb[:, 0:ST], f8c[:, 0:ST], 2.0, None, Alu.mult),
                       r=["f8c"], w=["bgb"])
                    S_(lambda e: e.activation(
                        x1c[:, ST:2 * ST], x1c[:, 0:ST], Act.Abs),
                       r=["x1c"], w=["x1c"])
                    V_(lambda e: e.tensor_scalar(
                        mkb[:, 0:ST], x1c[:, ST:2 * ST], FP8_SMALL, None,
                        Alu.is_ge),
                       r=["x1c"], w=["mkb"])
                    V_(lambda e, tns=tns, off=off, st=st: e.tensor_scalar(
                        tns[:, off + st * ST:off + (st + 1) * ST],
                        x1c[:, 0:ST], C_Q9, C_Q9, Alu.add, Alu.subtract),
                       r=["x1c"], a=[nm])
                    V_(lambda e, tns=tns, off=off, st=st: e.copy_predicated(
                        tns[:, off + st * ST:off + (st + 1) * ST],
                        mkb[:, 0:ST], bgb[:, 0:ST]),
                       r=["mkb", "bgb", nm], a=[nm])

            for st in range(NST):
                sg = st % 2
                for ic in range(NCH):
                    quant_tile(st, ic)
                for oc in range(NCH):
                    psum, pn = (ps_a, "ps_a") if oc % 2 == 0 else (ps_b, "ps_b")
                    for ic in range(NCH):
                        nm, tns, off = xq_map[ic]
                        T_(lambda e, psum=psum, ic=ic, oc=oc, st=st, tns=tns,
                           off=off: e.matmul(
                               psum[:, 0:ST],
                               wsl[0][:, ic * DM + oc * P:ic * DM + (oc + 1) * P],
                               tns[:, off + st * ST:off + (st + 1) * ST],
                               start=(ic == 0), stop=(ic == NCH - 1)),
                           r=["wsl0", nm], w=[pn], inc=(ic == NCH - 1))
                    V_(lambda e, psum=psum, oc=oc, s0=bslot[l, "bf"]:
                       e.tensor_scalar(
                           tmpF[:, 0:ST], psum[:, 0:ST], scC[:, 5:6],
                           bcols[:, s0 + oc:s0 + oc + 1], Alu.mult, Alu.add),
                       r=[pn, "scC", "bcols"], w=["tmpF"])
                    V_(lambda e, sg=sg, oc=oc: e.tensor_add(
                        zsl[sg][:, oc * ST:(oc + 1) * ST], tmpF[:, 0:ST],
                        zsl[sg][:, oc * ST:(oc + 1) * ST]),
                       r=["tmpF", f"zsl{sg}"], a=[f"zsl{sg}"])
                    if oc == 2 and pend_apply:
                        pend_apply.pop(0)()
                ln_sums(st)
                pend_apply.append(
                    lambda st=st: (ln_apply(st, spill_x1=(l == 0),
                                            write_out=(l == 1)),
                                   res_slab(st + 2, x1loc, "x1loc")
                                   if st + 2 < NST else None))
            while pend_apply:
                pend_apply.pop(0)()
            if l == 0:
                load_W(1, "q", 0)
                load_W(1, "k", 1)

        pr.op("sync", lambda e: e.nop(), [N("o_xT")], [], dma=False)

        pr.seal()
        nc._pr = pr
        with nc.Block() as block:
            @block.tensor
            def _(e):
                pr.emit("tensor", e, sems)

            @block.vector
            def _(e):
                pr.emit("vector", e, sems)

            @block.scalar
            def _(e):
                pr.emit("scalar", e, sems)

            @block.gpsimd
            def _(e):
                pr.emit("gpsimd", e, sems)

            @block.sync
            def _(e):
                pr.emit("sync", e, sems)

    return nc


def _deinterleave_perm():
    p = np.zeros(DM, np.int64)
    for h in range(NH):
        p[h * D:h * D + D // 2] = h * D + 2 * np.arange(D // 2)
        p[h * D + D // 2:(h + 1) * D] = h * D + 2 * np.arange(D // 2) + 1
    return p


def _bias_cols(b):
    return np.ascontiguousarray(b.reshape(NCH, P).T.astype(np.float32))


def kernel(L0_attn_x, L0_attn_Wq, L0_attn_bq, L0_attn_Wk, L0_attn_bk, L0_attn_Wv,
           L0_attn_bv, L0_attn_Wp, L0_attn_bp, L0_attn_rel_bias_table,
           L0_ffn_weight, L0_ffn_bias, L1_attn_Wq, L1_attn_bq, L1_attn_Wk,
           L1_attn_bk, L1_attn_Wv, L1_attn_bv, L1_attn_Wp, L1_attn_bp,
           L1_ffn_weight, L1_ffn_bias):
    perm = _deinterleave_perm()
    W = {
        (0, "q"): L0_attn_Wq, (0, "k"): L0_attn_Wk, (0, "v"): L0_attn_Wv,
        (0, "p"): L0_attn_Wp, (0, "f"): L0_ffn_weight,
        (1, "q"): np.ascontiguousarray(L1_attn_Wq[:, perm]),
        (1, "k"): np.ascontiguousarray(L1_attn_Wk[:, perm]),
        (1, "v"): L1_attn_Wv, (1, "p"): L1_attn_Wp, (1, "f"): L1_ffn_weight,
    }
    bq1 = np.asarray(L1_attn_bq)[perm]
    bk1 = np.asarray(L1_attn_bk)[perm]
    Bc = {
        (0, "bq"): _bias_cols(np.asarray(L0_attn_bq)),
        (0, "bk"): _bias_cols(np.asarray(L0_attn_bk)),
        (0, "bp"): _bias_cols(np.asarray(L0_attn_bp)),
        (0, "bf"): _bias_cols(np.asarray(L0_ffn_bias)),
        (1, "bq"): _bias_cols(bq1),
        (1, "bk"): _bias_cols(bk1),
        (1, "bp"): _bias_cols(np.asarray(L1_attn_bp)),
        (1, "bf"): _bias_cols(np.asarray(L1_ffn_bias)),
    }
    bv = {0: np.asarray(L0_attn_bv).reshape(1, DM).astype(np.float32),
          1: np.asarray(L1_attn_bv).reshape(1, DM).astype(np.float32)}
    relt = np.asarray(L0_attn_rel_bias_table).astype(np.float32)

    nc = build_program()
    in_maps = []
    for c in range(8):
        b, half = c // 2, c % 2
        cons = host_constants(half)
        m = {
            "x0T": np.ascontiguousarray(
                np.asarray(L0_attn_x)[b, half * TOK:(half + 1) * TOK, :].T
            ).astype(np.float32),
            "rel8": np.ascontiguousarray(relt[:, half * HO:(half + 1) * HO]),
            "d_ropeC": cons["ropeC"], "d_ropeS": cons["ropeS"],
            "d_tri": cons["tri01"],
            "d_eyef": cons["eye_f32"], "d_onesp": cons["ones_p"],
            "d_jrev": cons["jrev"], "d_perm": cons["perm"],
            "d_onesr": cons["ones_r"], "d_oneh": cons["oneh"],
        }
        for l in (0, 1):
            for w in ("q", "k", "v", "p", "f"):
                wa = np.asarray(W[l, w])
                if w == "p" and half == 1:
                    wa = np.concatenate([wa[DM // 2:], wa[:DM // 2]], axis=0)
                m[f"W{l}{w}"] = np.ascontiguousarray(wa, dtype=np.float32)
            for bn in ("bq", "bk", "bp", "bf"):
                m[f"B{l}{bn}"] = Bc[l, bn]
            m[f"B{l}bv"] = bv[l]
        in_maps.append(m)

    res = run_bass_kernel_spmd(nc, in_maps, core_ids=list(range(8)))
    out = np.empty((B, T, DM), np.float32)
    for c in range(8):
        b, half = c // 2, c % 2
        out[b, half * TOK:(half + 1) * TOK, :] = np.asarray(
            res.results[c]["o_xT"]).T
    return out


# version 31
# speedup vs baseline: 1.2050x; 1.0364x over previous
import numpy as np
import ml_dtypes
import concourse.bass as bass
import concourse.mybir as mybir
from concourse.bass_utils import run_bass_kernel_spmd

F32 = mybir.dt.float32
BF16 = mybir.dt.bfloat16
FP8 = mybir.dt.float8e4
U8 = mybir.dt.uint8
I16 = mybir.dt.int16
Act = mybir.ActivationFunctionType
Alu = mybir.AluOpType
AxX = mybir.AxisListType.X

SCALE = 1
B = 4
T = 4096 // SCALE
DM = 1024
NH = 16
D = 64
P = 128
TOK = T // 2
NCH = DM // P
ST = min(512, TOK)
NST = TOK // ST
QS = min(512, TOK)
NQS = T // QS
KPQ = QS // P
NKB = T // P
HO = NH // 2
SG = 2
C_RND = 12582912.0
C_Q9 = 24576.0
FP8_SMALL = 0.03125
EPS = 1e-5
M_CAP = 30.0
DBG = 0

V_OFF = 2 * DM * TOK
ST_OFF = V_OFF + NH * TOK * D
AG1_N = ST_OFF + 2048
AG2_N = (DM // 2) * T
SPAN = V_OFF + (NH // 2) * TOK * D
HDT = (NH // 2) * TOK * D
PAIRS = [[0, 1], [2, 3], [4, 5], [6, 7]]

ENGINES = ("tensor", "vector", "scalar", "gpsimd", "sync")


class Buf:
    __slots__ = ("name", "writers", "readers")

    def __init__(self, name):
        self.name, self.writers, self.readers = name, [], []


class Prog:
    def __init__(self):
        self.ops = {e: [] for e in ENGINES}

    def op(self, eng, fn, reads=(), writes=(), appends=(), inc=True, dma=False,
           cc=False):
        idx = len(self.ops[eng])
        ev = (eng, idx)
        waits = set()
        for b in reads:
            waits.update(b.writers)
        for b in writes:
            waits.update(b.writers)
            waits.update(b.readers)
        for b in appends:
            waits.update(b.readers)
        waits.discard(ev)
        self.ops[eng].append(dict(fn=fn, inc=inc, dma=dma, cc=cc, waits=waits))
        for b in writes:
            b.writers = [ev]
            b.readers = []
        for b in appends:
            b.writers.append(ev)
        for b in reads:
            if not b.readers or b.readers[-1] != ev:
                b.readers.append(ev)

    NDQ = 8

    def seal(self):
        self.cum, self.cover, self.covidx, self.key = {}, {}, {}, {}
        for e in ENGINES:
            ops = self.ops[e]
            if ops and not ops[-1]["inc"]:
                ops[-1]["inc"] = True
            c = 0
            dslot = {}
            ndma = 0
            counts, keys, pre = [], [], []
            for o in ops:
                if o["dma"]:
                    sl = ndma % self.NDQ
                    ndma += 1
                    pre.append((("dma", e, sl), dslot.get(sl, 0)))
                    dslot[sl] = dslot.get(sl, 0) + 16
                    counts.append(dslot[sl])
                    keys.append(("dma", e, sl))
                elif o["cc"]:
                    ccc = getattr(self, "_ccc", {}).get(e, 0) + 1
                    self._ccc = getattr(self, "_ccc", {})
                    self._ccc[e] = ccc
                    pre.append(None)
                    counts.append(ccc)
                    keys.append(("cc", e))
                else:
                    if o["inc"]:
                        c += 1
                    pre.append(None)
                    counts.append(c)
                    keys.append(("c", e))
            self.cum[e], self.key[e] = counts, keys
            self.pre = getattr(self, "pre", {})
            self.pre[e] = pre
            cov, ci = [None] * len(ops), [None] * len(ops)
            nxt = nxi = None
            for i in range(len(ops) - 1, -1, -1):
                if ops[i]["inc"] and not ops[i]["dma"] and not ops[i]["cc"]:
                    nxt, nxi = counts[i], i
                if ops[i]["dma"] or ops[i]["cc"]:
                    cov[i], ci[i] = counts[i], i
                else:
                    cov[i], ci[i] = nxt, nxi
            self.cover[e], self.covidx[e] = cov, ci

    def emit(self, e, eng_obj, sems):
        watermark = {}

        def wait(key, val):
            if val > watermark.get(key, 0):
                eng_obj.wait_ge(sems[key], val)
                watermark[key] = val

        for i, o in enumerate(self.ops[e]):
            tgt = {}
            for (we, wi) in sorted(o["waits"]):
                ci = self.covidx[we][wi]
                assert ci is not None, f"{e}:{i} dep on {we}:{wi} has no covering inc"
                if we == e and ci >= i and not self.ops[we][wi]["dma"]                         and not self.ops[we][wi]["cc"]:
                    continue
                key = self.key[we][ci]
                val = self.cover[we][wi]
                if val > tgt.get(key, 0):
                    tgt[key] = val
            for key in sorted(tgt, key=str):
                wait(key, tgt[key])
            if self.pre[e][i] is not None:
                wait(*self.pre[e][i])
            inst = o["fn"](eng_obj)
            if o["inc"]:
                inst.then_inc(sems[self.key[e][i]], 16 if o["dma"] else 1)


def _bucket_np(n):
    import math
    NBUK, MAXD = 32, 128
    n = np.asarray(n)
    me = NBUK // 2
    small = n < me
    ns = np.maximum(n, 1).astype(np.float64)
    vl = me + (np.log(ns / me) / math.log(MAXD / me) * (NBUK - me)).astype(np.int64)
    vl = np.minimum(vl, NBUK - 1)
    return np.where(small, n, vl)


def host_constants(half):
    cons = {}
    t = np.arange(half * TOK, half * TOK + TOK, dtype=np.float64)
    inv = 1.0 / (10000.0 ** (np.arange(0, D, 2, dtype=np.float64) / D))
    ang = t[None, :] * inv[:, None]
    c32, s32 = np.cos(ang), np.sin(ang)
    cons["ropeC"] = np.tile(c32, (4, 1)).astype(ml_dtypes.bfloat16)
    cons["ropeS"] = np.concatenate([-s32, s32, -s32, s32], 0).astype(ml_dtypes.bfloat16)
    kk = np.arange(P)[:, None]
    qq = np.arange(P)[None, :]
    cons["tri01"] = (kk <= qq).astype(ml_dtypes.bfloat16)
    cons["eye_bf"] = np.eye(P, dtype=ml_dtypes.bfloat16)
    cons["eye_f32"] = np.eye(P, dtype=np.float32)
    cons["ones_p"] = np.ones((P, 1), np.float32)
    cons["ones_r"] = np.ones((1, P), np.float32)
    pm = np.zeros((P, P), np.float32)
    for g_ in range(2):
        for j_ in range(32):
            a_, b_ = g_ * 64 + j_, g_ * 64 + 32 + j_
            pm[a_, b_] = 1.0
            pm[b_, a_] = 1.0
    cons["perm"] = pm.astype(ml_dtypes.bfloat16)
    jr = np.zeros((P, P), np.float32)
    jr[np.arange(P)[::-1], np.arange(P)] = 1.0
    cons["jrev"] = jr.astype(ml_dtypes.bfloat16)
    e = np.arange(384)
    bkt = _bucket_np(np.maximum(e - 127, 0))
    oneh = np.zeros((32, 384), np.float32)
    oneh[bkt, e] = 1.0
    cons["oneh"] = oneh
    return cons


def _groups(nkb):
    g, j = [], 0
    while j < nkb:
        g.append((j, min(SG, nkb - j)))
        j += SG
    return g


def build_program():
    import contextlib
    nc = bass.Bass("TRN2", target_bir_lowering=False, debug=False)
    pr = Prog()

    x0T = nc.dram_tensor("x0T", [DM, TOK], F32, kind="ExternalInput")
    o_xT = nc.dram_tensor("o_xT", [DM, TOK], F32, kind="ExternalOutput")
    Wd, Bd = {}, {}
    for l in (0, 1):
        for w in ("q", "k", "v", "p", "f"):
            Wd[l, w] = nc.dram_tensor(f"W{l}{w}", [DM, DM], F32, kind="ExternalInput")
        for bn in ("bq", "bk", "bp", "bf"):
            Bd[l, bn] = nc.dram_tensor(f"B{l}{bn}", [P, NCH], F32, kind="ExternalInput")
        Bd[l, "bv"] = nc.dram_tensor(f"B{l}bv", [1, DM], F32, kind="ExternalInput")
    d_rel8 = nc.dram_tensor("rel8", [32, HO], F32, kind="ExternalInput")
    d_ropeC = nc.dram_tensor("d_ropeC", [P, TOK], BF16, kind="ExternalInput")
    d_ropeS = nc.dram_tensor("d_ropeS", [P, TOK], BF16, kind="ExternalInput")
    d_tri = nc.dram_tensor("d_tri", [P, P], BF16, kind="ExternalInput")
    d_eyef = nc.dram_tensor("d_eyef", [P, P], F32, kind="ExternalInput")
    d_onesp = nc.dram_tensor("d_onesp", [P, 1], F32, kind="ExternalInput")
    d_onesr = nc.dram_tensor("d_onesr", [1, P], F32, kind="ExternalInput")
    d_oneh = nc.dram_tensor("d_oneh", [32, 384], F32, kind="ExternalInput")
    d_jrev = nc.dram_tensor("d_jrev", [P, P], BF16, kind="ExternalInput")
    d_perm = nc.dram_tensor("d_perm", [P, P], BF16, kind="ExternalInput")

    ag1_in, ag1_out, ag2_in, ag2_out, ar_in, ar_out = {}, {}, {}, {}, {}, {}
    for l in (0, 1):
        ag1_in[l] = nc.dram_tensor(f"ag1i{l}", [1, AG1_N], BF16)
        ag1_out[l] = nc.dram_tensor(f"ag1o{l}", [1, 2 * AG1_N], BF16)
        ag2_in[l] = nc.dram_tensor(f"ag2i{l}", [1, AG2_N + TOK], BF16)
        ag2_out[l] = nc.dram_tensor(f"ag2o{l}", [1, 2 * AG2_N + TOK], BF16)
        ar_in[l] = nc.dram_tensor(f"ari{l}", [1, 8], F32)
        ar_out[l] = nc.dram_tensor(f"aro{l}", [1, 8], F32)
    band_dram = nc.dram_tensor("band_dram", [HO, 384], BF16)
    x1loc = nc.dram_tensor("x1loc", [1, DM * TOK], F32)
    qkvloc, ot2loc = {}, {}
    for l in (0, 1):
        qkvloc[l] = nc.dram_tensor(f"qkvloc{l}", [1, 6 * HDT], BF16)
        ot2loc[l] = nc.dram_tensor(f"ot2loc{l}", [1, 2 * (DM // 2) * T], BF16)

    ctx = contextlib.ExitStack()

    def sbt(n, s, d):
        return ctx.enter_context(nc.sbuf_tensor(n, s, d))

    def pst(n, s):
        return ctx.enter_context(nc.psum_tensor(n, s, F32))

    with ctx:
        Xrs = sbt("Xrs", [P, NCH * TOK], BF16)
        zsl = [sbt(f"zsl{i}", [P, NCH * ST], F32) for i in range(2)]
        x1c = sbt("x1c", [P, 1024], F32)
        qst = [sbt("qst0", [P, TOK], BF16)]
        xqb = sbt("xqb", [P, TOK], BF16)
        rt1 = [sbt(f"rt1{i}", [P, ST], BF16) for i in range(2)]
        rt2 = [sbt(f"rt2{i}", [P, ST], BF16) for i in range(2)]
        vstg = [sbt(f"vstg{i}", [P, ST], BF16) for i in range(2)]
        qkt = sbt("qkt", [P, 2 * T], BF16)
        vph = [sbt(f"vph{i}", [P, 65 * NKB], BF16) for i in range(2)]
        atb = [sbt(f"atb{i}", [P, SG * QS], BF16) for i in range(3)]
        ot_sm = [sbt(f"ot_sm{i}", [D, QS], BF16) for i in range(2)]
        ot_sl = [sbt(f"ot_sl{i}", [P, NCH * ST], BF16) for i in range(2)]
        wsl = [sbt(f"wsl{i}", [P, NCH * DM], BF16) for i in range(2)]
        wstg = [sbt(f"wstg{i}", [P, DM], F32) for i in range(2)]
        SCR = max(TOK, DM)
        HSC = max(TOK // 2, DM)
        f8c = sbt("f8c", [P, HSC], FP8)
        mkb = sbt("mkb", [P, ST], I16)
        bgb = sbt("bgb", [P, HSC], BF16)
        tmpF = sbt("tmpF", [P, max(2 * ST, HSC)], F32)
        ropeC = sbt("ropeCs", [P, TOK], BF16)
        ropeS = sbt("ropeSs", [P, TOK], BF16)
        tri = sbt("tris", [P, P], BF16)
        eyef = sbt("eyefs", [P, P], F32)
        onesp = sbt("onesps", [P, 1], F32)
        onesr = sbt("onesrs", [1, P], F32)
        onesF = sbt("onesFs", [P, D], F32)
        onehs = sbt("onehss", [32, 384], F32)
        jrev = sbt("jrevs", [P, P], BF16)
        permm = sbt("permm", [P, P], BF16)
        hank = sbt("hank", [P, P], BF16)
        rel8s = sbt("rel8ss", [32, HO], F32)
        ebias = sbt("ebias", [P, HO], F32)
        ediag = sbt("ediag", [P, HO * P], BF16)
        eprev = sbt("eprev", [P, HO * P], BF16)
        bcols = sbt("bcols", [P, 8 * NCH], F32)
        vbias = sbt("vbias", [P, DM], BF16)
        bvrow = sbt("bvrow", [1, DM], BF16)
        onesrb = sbt("onesrb", [1, P], BF16)
        bandF = sbt("bandF", [HO, 384], F32)
        bandB = sbt("bandB", [HO, 384], BF16)
        statc = sbt("statc", [P, 8], F32)
        row_m = sbt("row_m", [1, ST], F32)
        row_q2 = sbt("row_q2", [1, ST], F32)
        row_v = sbt("row_v", [1, ST], F32)
        row_i = row_q2
        g16 = sbt("g16", [1, 16], BF16)
        sc1 = sbt("sc1", [1, 16], F32)
        scC = sbt("scC", [P, 8], F32)
        tf32 = sbt("tf32", [P, P], F32)
        epsc = sbt("epsc", [P, 1], F32)
        rnorm = wstg[0]

        ps_a = pst("ps_a", [P, SG * QS])
        ps_b = pst("ps_b", [P, SG * QS])
        ps_o = pst("ps_o", [P, 2 * QS])
        ps_m = pst("ps_m", [P, QS])

        sems = {}
        for e in ENGINES:
            sems[("c", e)] = ctx.enter_context(nc.semaphore(f"sc_{e}"))
            for sl in range(Prog.NDQ):
                sems[("dma", e, sl)] = ctx.enter_context(
                    nc.semaphore(f"sd_{e}{sl}"))
        sems[("cc", "gpsimd")] = ctx.enter_context(nc.semaphore("s_cc"))

        bb = {}

        def N(name):
            if name not in bb:
                bb[name] = Buf(name)
            return bb[name]

        def T_(fn, r=(), w=(), a=(), inc=True):
            pr.op("tensor", fn, [N(x) for x in r], [N(x) for x in w],
                  [N(x) for x in a], inc=inc)

        def V_(fn, r=(), w=(), a=()):
            pr.op("vector", fn, [N(x) for x in r], [N(x) for x in w],
                  [N(x) for x in a])

        def S_(fn, r=(), w=(), a=()):
            pr.op("scalar", fn, [N(x) for x in r], [N(x) for x in w],
                  [N(x) for x in a])

        def Gd(fn, r=(), w=(), a=()):
            pr.op("gpsimd", fn, [N(x) for x in r], [N(x) for x in w],
                  [N(x) for x in a], dma=True)

        def Gc(fn, r=(), w=(), a=()):
            pr.op("gpsimd", fn, [N(x) for x in r], [N(x) for x in w],
                  [N(x) for x in a], dma=False)

        def Y_(fn, r=(), w=(), a=()):
            pr.op("sync", fn, [N(x) for x in r], [N(x) for x in w],
                  [N(x) for x in a], dma=True)

        def CC_(fn, r=(), w=()):
            pr.op("gpsimd", fn, [N(x) for x in r], [N(x) for x in w], cc=True)

        bb["rnorm"] = N("wstg0")

        Y_(lambda e: e.dma_start(out=ropeC[:, :], in_=d_ropeC[:, :]), w=["ropeC"])
        Y_(lambda e: e.dma_start(out=ropeS[:, :], in_=d_ropeS[:, :]), w=["ropeS"])
        Y_(lambda e: e.dma_start(out=tri[:, :], in_=d_tri[:, :]), w=["tri"])
        Y_(lambda e: e.dma_start(out=eyef[:, :], in_=d_eyef[:, :]), w=["eyef"])
        Y_(lambda e: e.dma_start(out=onesp[:, :], in_=d_onesp[:, :]), w=["onesp"])
        Y_(lambda e: e.dma_start(out=onesr[:, :], in_=d_onesr[:, :]), w=["onesr"])
        Y_(lambda e: e.dma_start(out=onehs[:, :], in_=d_oneh[:, :]), w=["oneh"])
        Y_(lambda e: e.dma_start(out=jrev[:, :], in_=d_jrev[:, :]), w=["jrev"])
        Y_(lambda e: e.dma_start(out=permm[:, :], in_=d_perm[:, :]), w=["permm"])
        Y_(lambda e: e.dma_start(out=rel8s[:, :], in_=d_rel8[:, :]), w=["rel8"])
        bslot = {}
        si = 0
        for l in (0, 1):
            for bn in ("bq", "bk", "bp", "bf"):
                bslot[l, bn] = si * NCH
                Y_(lambda e, l=l, bn=bn, s0=si * NCH: e.dma_start(
                    out=bcols[:, s0:s0 + NCH], in_=Bd[l, bn][:, :]), w=["bcols"])
                si += 1
        Gd(lambda e: e.dma_start(
            out=bass.AP(Xrs, 0, [[NCH * TOK, P], [TOK, NCH], [1, TOK]]),
            in_=bass.AP(x0T, 0, [[TOK, P], [P * TOK, NCH], [1, TOK]])), w=["Xrs"])
        Gc(lambda e: e.memset(onesF[:, :], 1.0), w=["onesF"])
        Gc(lambda e: e.memset(onesrb[:, :], 1.0), w=["onesrb"])
        Gc(lambda e: e.memset(statc[:, :], 0.0), w=["statc"])
        Gc(lambda e: e.memset(sc1[:, :], 0.0), w=["sc1"])
        Gc(lambda e: e.memset(row_m[:, :], 0.0), w=["rows"])
        Gc(lambda e: e.memset(row_q2[:, :], 0.0), w=["rows"])
        Gc(lambda e: e.memset(row_v[:, :], 0.0), w=["rows"])
        Gc(lambda e: e.memset(row_i[:, :], 0.0), w=["rows"])
        Gc(lambda e: e.memset(rnorm[:, :], 0.0), w=["rnorm"])
        Gc(lambda e: e.memset(ebias[:, :], 0.0), w=["ebias"])
        Gc(lambda e: e.memset(scC[:, :], 0.0), w=["scC"])
        Gc(lambda e: e.memset(epsc[:, :], EPS), w=["epsc"])

        def load_W(l, w, slot):
            Gd(lambda e, l=l, w=w, slot=slot: e.dma_start(
                out=bass.AP(wsl[slot], 0, [[NCH * DM, P], [DM, NCH], [1, DM]]),
                in_=bass.AP(Wd[l, w], 0, [[DM, P], [P * DM, NCH], [1, DM]])),
               w=[f"wsl{slot}"])

        load_W(0, "q", 0)
        load_W(0, "k", 1)

        dynv = {}
        dynv_g = {}

        def dyn_into(e, dynv):
            if "a1q0" not in dynv:
                pid = e.partition_id()
                half = (pid & 1) * HDT
                for r in (0, 1):
                    dynv[f"a1q{r}"] = e.snap(r * AG1_N + half)
                    dynv[f"a1k{r}"] = e.snap(r * AG1_N + DM * TOK + half)
                    dynv[f"a1v{r}"] = e.snap(r * AG1_N + V_OFF + half)
                dynv["ot_loc"] = e.snap((pid & 1) * TOK)
                dynv["ot_par"] = e.snap((1 - (pid & 1)) * AG2_N
                                        + (pid & 1) * TOK)
            return dynv

        def dyn(e):
            return dyn_into(e, dynv)

        def dyng(e):
            return dyn_into(e, dynv_g)

        for l in (0, 1):
            a1i, a1o, a2i, a2o = ag1_in[l], ag1_out[l], ag2_in[l], ag2_out[l]
            A1I, A1O, A2I, A2O = f"ag1i{l}", f"ag1o{l}", f"ag2i{l}", f"ag2o{l}"

            rsrc, rsn = (x0T, "x0T") if l == 0 else (x1loc, "x1loc")

            def res_slab(st, src, srcn):
                sg = st % 2
                Gd(lambda e, st=st, sg=sg, src=src: e.dma_start(
                    out=bass.AP(zsl[sg], 0, [[NCH * ST, P], [ST, NCH], [1, ST]]),
                    in_=bass.AP(src, st * ST, [[TOK, P], [P * TOK, NCH], [1, ST]])),
                   r=[srcn], a=[f"zsl{sg}"])

            res_slab(0, rsrc, rsn)
            if NST > 1:
                res_slab(1, rsrc, rsn)

            if l == 0:
                T_(lambda e: e.matmul(ps_m[0:HO, 0:384], rel8s[:, :], onehs[:, :],
                                      start=True, stop=True),
                   r=["rel8", "oneh"], w=["ps_m"])
                V_(lambda e: e.tensor_copy(bandF[:, :], ps_m[0:HO, 0:384]),
                   r=["ps_m"], w=["bandF"])
                V_(lambda e: e.tensor_copy(statc[0:HO, 6:7], bandF[:, 350:351]),
                   r=["bandF"], w=["statc"])
                V_(lambda e: e.tensor_reduce(statc[0:HO, 7:8], bandF[:, :], AxX,
                                             Alu.max), r=["bandF"], w=["statc"])
                V_(lambda e: e.tensor_single_scalar(bandF[:, :], bandF[:, :],
                                                    bandF[:, 350:351], Alu.subtract),
                   r=["bandF"], w=["bandF"])
                S_(lambda e: e.activation(bandB[:, :], bandF[:, :], Act.Exp, scale=0.125),
                   r=["bandF"], w=["bandB"])
                Gd(lambda e: e.dma_start(out=band_dram[:, :], in_=bandB[:, :]),
                   r=["bandB"], w=["band_dram"])
                for h in range(HO):
                    for dst, dn, off in ((ediag, "ediag", 0), (eprev, "eprev", 128)):
                        Gd(lambda e, h=h, off=off: e.dma_start(
                            out=hank[:, :],
                            in_=bass.AP(band_dram, h * 384 + off, [[1, P], [1, P]])),
                           r=["band_dram"], w=["hank"])
                        T_(lambda e: e.matmul(ps_m[:, 0:P], hank[:, :], jrev[:, :],
                                              start=True, stop=True),
                           r=["hank", "jrev"], w=["ps_m"])
                        V_(lambda e, dst=dst, h=h: e.tensor_copy(
                            dst[:, h * P:(h + 1) * P], ps_m[:, 0:P]),
                           r=["ps_m"], a=[dn])
                for h in range(HO):
                    V_(lambda e, h=h: e.tensor_mul(ediag[:, h * P:(h + 1) * P],
                                                   ediag[:, h * P:(h + 1) * P],
                                                   tri[:, :]),
                       r=["ediag", "tri"], w=["ediag"])

            Gc(lambda e: e.memset(statc[:, 0:2], 0.0), w=["statc"])
            for wname, slot, bn, scol, agrow in (
                    ("q", 0, "bq", 0, 0), ("k", 1, "bk", 1, DM)):
                pend_rope = []

                def emit_rope(ic, st, scol, agrow, bn):
                    rb = st % 2
                    T_(lambda e, st=st: e.matmul(
                        ps_m[:, 0:ST], permm[:, :],
                        qst[0][:, st * ST:(st + 1) * ST],
                        start=True, stop=True),
                       r=["permm", "qst0"], w=["ps_m"])
                    V_(lambda e, st=st, rb=rb: e.tensor_mul(
                        rt1[rb][:, :], qst[0][:, st * ST:(st + 1) * ST],
                        ropeC[:, st * ST:(st + 1) * ST]),
                       r=["qst0", "ropeC"], w=[f"rt1{rb}"])
                    V_(lambda e, st=st, rb=rb: e.tensor_mul(
                        rt2[rb][:, :], ps_m[:, 0:ST],
                        ropeS[:, st * ST:(st + 1) * ST]),
                       r=["ps_m", "ropeS"], w=[f"rt2{rb}"])
                    V_(lambda e, rb=rb: e.tensor_add(
                        rt1[rb][:, :], rt1[rb][:, :], rt2[rb][:, :]),
                       r=[f"rt2{rb}"], w=[f"rt1{rb}"])
                    V_(lambda e, rb=rb: e.tensor_reduce(
                        statc[:, 2:3], rt1[rb][:, :], AxX, Alu.max,
                        apply_absolute_value=True),
                       r=[f"rt1{rb}"], w=["statc"])
                    V_(lambda e, scol=scol: e.tensor_tensor(
                        statc[:, scol:scol + 1], statc[:, scol:scol + 1],
                        statc[:, 2:3], Alu.max),
                       r=["statc"], w=["statc"])
                    Y_(lambda e, rb=rb, ic=ic, st=st, agrow=agrow, a1i=a1i: e.dma_start(
                        out=bass.AP(a1i, (agrow + ic * P) * TOK + st * ST,
                                    [[TOK, P], [1, ST]]),
                        in_=rt1[rb][:, :]),
                       r=[f"rt1{rb}"], a=[A1I])

                for ic in range(NCH):
                    for st in range(NST):
                        psum, pn = (ps_a, "ps_a") if (ic * NST + st) % 2 == 0 else (ps_b, "ps_b")
                        for kc in range(NCH):
                            T_(lambda e, psum=psum, slot=slot, kc=kc, ic=ic, st=st:
                               e.matmul(psum[:, 0:ST],
                                        wsl[slot][:, kc * DM + ic * P:kc * DM + (ic + 1) * P],
                                        Xrs[:, kc * TOK + st * ST:kc * TOK + (st + 1) * ST],
                                        start=(kc == 0), stop=(kc == NCH - 1)),
                               r=[f"wsl{slot}", "Xrs"], w=[pn], inc=(kc == NCH - 1))
                        S_(lambda e, psum=psum, st=st, s0=bslot[l, bn], ic=ic:
                           e.add(qst[0][:, st * ST:(st + 1) * ST], psum[:, 0:ST],
                                 bcols[:, s0 + ic:s0 + ic + 1]),
                           r=[pn, "bcols"], a=["qst0"])
                        pend_rope.append(
                            lambda ic=ic, st=st, scol=scol, agrow=agrow, bn=bn:
                            emit_rope(ic, st, scol, agrow, bn))
                        while len(pend_rope) > (1 if NST > 1 else 0):
                            pend_rope.pop(0)()
                while pend_rope:
                    pend_rope.pop(0)()
                if wname == "q":
                    load_W(l, "v", 0)
                else:
                    load_W(l, "p", 1)

            Gd(lambda e, l=l: e.dma_start(out=bvrow[:, :], in_=Bd[l, "bv"][:, :]),
               w=["bvrow"])
            for j in range(DM // ST):
                T_(lambda e, j=j: e.matmul(ps_m[:, 0:ST], onesrb[:, :],
                                           bvrow[:, j * ST:(j + 1) * ST],
                                           start=True, stop=True),
                   r=["onesrb", "bvrow"], w=["ps_m"])
                V_(lambda e, j=j: e.tensor_copy(vbias[:, j * ST:(j + 1) * ST],
                                                ps_m[:, 0:ST]),
                   r=["ps_m"], w=["vbias"])
            for tt in range(TOK // P):
                for vc in range(DM // ST):
                    sg = (tt * (DM // ST) + vc) % 2
                    psum, pn = (ps_a, "ps_a") if sg == 0 else (ps_b, "ps_b")
                    for kc in range(NCH):
                        T_(lambda e, psum=psum, kc=kc, tt=tt, vc=vc: e.matmul(
                            psum[:, 0:ST],
                            Xrs[:, kc * TOK + tt * P:kc * TOK + (tt + 1) * P],
                            wsl[0][:, kc * DM + vc * ST:kc * DM + (vc + 1) * ST],
                            start=(kc == 0), stop=(kc == NCH - 1)),
                           r=["wsl0", "Xrs"], w=[pn], inc=(kc == NCH - 1))
                    V_(lambda e, psum=psum, sg=sg, vc=vc: e.tensor_add(
                        vstg[sg][:, :], psum[:, 0:ST], vbias[:, vc * ST:(vc + 1) * ST]),
                       r=[pn, "vbias"], w=[f"vstg{sg}"])
                    Y_(lambda e, sg=sg, tt=tt, vc=vc, a1i=a1i: e.dma_start(
                        out=bass.AP(a1i, V_OFF + vc * (ST // D) * TOK * D + tt * P * D,
                                    [[D, P], [TOK * D, ST // D], [1, D]]),
                        in_=bass.AP(vstg[sg], 0, [[ST, P], [D, ST // D], [1, D]])),
                       r=[f"vstg{sg}"], a=[A1I])

            T_(lambda e: e.transpose(ps_m[0:2, 0:P], statc[:, 0:2], eyef[:, :]),
               r=["statc", "eyef"], w=["ps_m"])
            V_(lambda e: e.tensor_reduce(statc[0:2, 3:4], ps_m[0:2, 0:P], AxX, Alu.max),
               r=["ps_m"], w=["statc"])
            Y_(lambda e, a1i=a1i: e.dma_start(
                out=bass.AP(a1i, ST_OFF + 2, [[2046, 1], [1, 2046]]),
                in_=Xrs[0:1, 0:2046]),
               r=["Xrs"], a=[A1I])
            Gd(lambda e, a1i=a1i: e.dma_start(
                out=bass.AP(a1i, ST_OFF, [[1, 2], [1, 1]]), in_=statc[0:2, 3:4]),
               r=["statc"], a=[A1I])
            CC_(lambda e, l=l: e.collective_compute(
                "AllGather", Alu.bypass, replica_groups=PAIRS,
                ins=[ag1_in[l].ap().opt()], outs=[ag1_out[l].ap().opt()]),
                r=[A1I], w=[A1O])

            fcol = 0
            for ic in range(NCH):
                sg = ic % 2
                Y_(lambda e, l=l, ic=ic, sg=sg: e.dma_start(
                    out=wstg[sg][:, :],
                    in_=bass.AP(Wd[l, "f"], ic * P * DM, [[DM, P], [1, DM]])),
                   w=[f"wstg{sg}"])
                V_(lambda e, sg=sg, l=l: e.tensor_reduce(
                    statc[:, 4:5], wstg[sg][:, :], AxX,
                    Alu.add if l == 0 else Alu.max, apply_absolute_value=True),
                   r=[f"wstg{sg}"], w=["statc"])
                V_(lambda e, l=l: e.tensor_tensor(
                    statc[:, 5:6], statc[:, 5:6], statc[:, 4:5],
                    Alu.add if l == 0 else Alu.max),
                   r=["statc"], w=["statc"])
            T_(lambda e: e.transpose(ps_m[0:1, 0:P], statc[:, 5:6], eyef[:, :]),
               r=["statc", "eyef"], w=["ps_m"])
            V_(lambda e, l=l: e.tensor_reduce(sc1[0:1, 0:1], ps_m[0:1, 0:P], AxX,
                                              Alu.add if l == 0 else Alu.max),
               r=["ps_m"], w=["sc1"])
            Gc(lambda e: e.memset(statc[:, 5:6], 0.0), w=["statc"])
            if l == 0:
                V_(lambda e: e.tensor_scalar(sc1[0:1, 1:2], sc1[0:1, 0:1],
                                             0.5 / (DM * DM), None, Alu.mult),
                   r=["sc1"], w=["sc1"])
                V_(lambda e: e.tensor_scalar(sc1[0:1, 2:3], sc1[0:1, 1:2],
                                             -1.0, None, Alu.mult),
                   r=["sc1"], w=["sc1"])
                pairs = [(1, 0), (2, 1)]
            else:
                V_(lambda e: e.tensor_scalar(sc1[0:1, 1:2], sc1[0:1, 0:1],
                                             1e-12, None, Alu.max),
                   r=["sc1"], w=["sc1"])
                V_(lambda e: e.reciprocal(sc1[0:1, 1:2], sc1[0:1, 1:2]),
                   r=["sc1"], w=["sc1"])
                V_(lambda e: e.tensor_scalar(sc1[0:1, 1:2], sc1[0:1, 1:2],
                                             448.0, None, Alu.mult),
                   r=["sc1"], w=["sc1"])
                V_(lambda e: e.tensor_scalar(sc1[0:1, 2:3], sc1[0:1, 1:2],
                                             0.5, None, Alu.mult),
                   r=["sc1"], w=["sc1"])
                pairs = [(1, 0), (2, 1)]
            for (c_in, c_out) in pairs:
                T_(lambda e, c_in=c_in: e.matmul(ps_m[:, 0:1], onesr[:, :],
                                                 sc1[0:1, c_in:c_in + 1],
                                                 start=True, stop=True),
                   r=["onesr", "sc1"], w=["ps_m"])
                V_(lambda e, c_out=c_out: e.tensor_copy(scC[:, c_out:c_out + 1],
                                                        ps_m[:, 0:1]),
                   r=["ps_m"], w=["scC"])
            for oc in range(NCH):
                sg = oc % 2
                Y_(lambda e, l=l, oc=oc, sg=sg: e.dma_start(
                    out=wstg[sg][:, :],
                    in_=bass.AP(Wd[l, "f"], oc * P * DM, [[DM, P], [1, DM]])),
                   w=[f"wstg{sg}"])
                for tc in range(NCH):
                    dst = (slice(None), slice(tc * DM + oc * P, tc * DM + (oc + 1) * P))
                    T_(lambda e, sg=sg, tc=tc: e.transpose(
                        ps_m[:, 0:P], wstg[sg][:, tc * P:(tc + 1) * P], eyef[:, :]),
                       r=[f"wstg{sg}", "eyef"], w=["ps_m"])
                    V_(lambda e: e.tensor_copy(tf32[:, :], ps_m[:, 0:P]),
                       r=["ps_m"], w=["tf32"])
                    if l == 0:
                        V_(lambda e: e.tensor_single_scalar(
                            mkb[:, 0:P], tf32[:, :], scC[:, 0:1], Alu.is_gt),
                           r=["tf32", "scC"], w=["mkb"])
                        V_(lambda e: e.tensor_single_scalar(
                            bgb[:, 0:P], tf32[:, :], scC[:, 1:2], Alu.is_lt),
                           r=["tf32", "scC"], w=["bgb"])
                        V_(lambda e, dst=dst: e.tensor_tensor(
                            wsl[0][dst], mkb[:, 0:P], bgb[:, 0:P], Alu.subtract),
                           r=["mkb", "bgb"], a=["wsl0"])
                    else:
                        V_(lambda e: e.tensor_single_scalar(
                            f8c[:, 0:P], tf32[:, :], scC[:, 1:2], Alu.mult),
                           r=["tf32", "scC"], w=["f8c"])
                        V_(lambda e: e.tensor_scalar(
                            bgb[:, 0:P], f8c[:, 0:P], 2.0, None, Alu.mult),
                           r=["f8c"], w=["bgb"])
                        V_(lambda e: e.tensor_single_scalar(
                            tmpF[:, 0:P], tf32[:, :], scC[:, 0:1], Alu.mult),
                           r=["tf32", "scC"], w=["tmpF"])
                        S_(lambda e: e.activation(
                            tmpF[:, P:2 * P], tmpF[:, 0:P], Act.Abs),
                           r=["tmpF"], w=["tmpF"])
                        V_(lambda e: e.tensor_scalar(
                            mkb[:, 0:P], tmpF[:, P:2 * P], FP8_SMALL, None,
                            Alu.is_ge),
                           r=["tmpF"], w=["mkb"])
                        V_(lambda e, dst=dst: e.tensor_scalar(
                            wsl[0][dst], tmpF[:, 0:P], C_Q9, C_Q9, Alu.add,
                            Alu.subtract),
                           r=["tmpF"], a=["wsl0"])
                        V_(lambda e, dst=dst: e.copy_predicated(
                            wsl[0][dst], mkb[:, 0:P], bgb[:, 0:P]),
                           r=["mkb", "bgb", "wsl0"], a=["wsl0"])

            Y_(lambda e, l=l: e.dma_start(
                out=bass.AP(g16, 0, [[16, 1], [2, 2], [1, 2]]),
                in_=bass.AP(ag1_out[l], ST_OFF, [[1, 1], [AG1_N, 2], [1, 2]])),
               r=[A1O], w=["g16"])
            V_(lambda e: e.tensor_reduce(sc1[0:1, 5:6],
                                         bass.AP(g16, 0, [[16, 1], [2, 2]]),
                                         AxX, Alu.max), r=["g16"], w=["sc1"])
            V_(lambda e: e.tensor_reduce(sc1[0:1, 6:7],
                                         bass.AP(g16, 1, [[16, 1], [2, 2]]),
                                         AxX, Alu.max), r=["g16"], w=["sc1"])
            V_(lambda e: e.tensor_tensor(sc1[0:1, 7:8], sc1[0:1, 5:6],
                                         sc1[0:1, 6:7], Alu.mult),
               r=["sc1"], w=["sc1"])
            V_(lambda e: e.tensor_scalar(sc1[0:1, 7:8], sc1[0:1, 7:8], 8.0,
                                         None, Alu.mult), r=["sc1"], w=["sc1"])
            if l == 0:
                T_(lambda e: e.transpose(ps_m[0:1, 0:P], statc[:, 7:8], eyef[:, :]),
                   r=["statc", "eyef"], w=["ps_m"])
                V_(lambda e: e.tensor_reduce(sc1[0:1, 4:5], ps_m[0:1, 0:HO], AxX,
                                             Alu.max), r=["ps_m"], w=["sc1"])
                V_(lambda e: e.tensor_scalar(sc1[0:1, 4:5], sc1[0:1, 4:5], 0.125,
                                             None, Alu.mult), r=["sc1"], w=["sc1"])
                V_(lambda e: e.tensor_tensor(sc1[0:1, 7:8], sc1[0:1, 7:8],
                                             sc1[0:1, 4:5], Alu.add),
                   r=["sc1"], w=["sc1"])
            V_(lambda e: e.tensor_scalar(sc1[0:1, 7:8], sc1[0:1, 7:8], M_CAP,
                                         None, Alu.min), r=["sc1"], w=["sc1"])
            T_(lambda e: e.matmul(ps_m[:, 0:1], onesr[:, :], sc1[0:1, 7:8],
                                  start=True, stop=True),
               r=["onesr", "sc1"], w=["ps_m"])
            V_(lambda e: e.tensor_copy(scC[:, 4:5], ps_m[:, 0:1]),
               r=["ps_m"], w=["scC"])
            if l == 0:
                T_(lambda e: e.transpose(ps_m[0:1, 0:P], statc[:, 6:7], eyef[:, :]),
                   r=["statc", "eyef"], w=["ps_m"])
                V_(lambda e: e.tensor_copy(sc1[0:1, 8:8 + HO], ps_m[0:1, 0:HO]),
                   r=["ps_m"], w=["sc1"])
                T_(lambda e: e.matmul(ps_m[:, 0:HO], onesr[:, :], sc1[0:1, 8:8 + HO],
                                      start=True, stop=True),
                   r=["onesr", "sc1"], w=["ps_m"])
                V_(lambda e: e.tensor_scalar(ebias[:, :], ps_m[:, 0:HO], 0.125,
                                             None, Alu.mult),
                   r=["ps_m"], w=["ebias"])
                V_(lambda e: e.tensor_single_scalar(ebias[:, :], ebias[:, :],
                                                    scC[:, 4:5], Alu.subtract),
                   r=["scC"], w=["ebias"])
            else:
                Gc(lambda e: e.memset(ebias[:, :], 0.0), w=["ebias"])
                V_(lambda e: e.tensor_single_scalar(ebias[:, :], ebias[:, :],
                                                    scC[:, 4:5], Alu.subtract),
                   r=["scC"], w=["ebias"])

            for (r, c, cn) in ((0, 0, "q"), (0, 1, "k"), (1, 0, "q"),
                               (1, 1, "k"), (0, 2, "v"), (1, 2, "v")):
                def fcp(e, r=r, c=c, cn=cn, l=l, a1o=a1o):
                    dv = dyn(e)
                    return e.dma_start(
                        out=qkvloc[l][0:1, (r * 3 + c) * HDT:(r * 3 + c + 1) * HDT],
                        in_=a1o[0:1, bass.ds(dv[f"a1{cn}{r}"], HDT)],
                        bounds_check=None)
                Y_(fcp, r=[A1O], a=[f"qkvloc{l}_{r}{cn}"])

            def head_dma(h):
                hb = h % 2
                rb0 = hb * D
                for r in (0, 1):
                    Y_(lambda e, h=h, rb0=rb0, r=r, l=l: e.dma_start(
                        out=qkt[rb0:rb0 + D, r * TOK:(r + 1) * TOK],
                        in_=bass.AP(qkvloc[l], r * 3 * HDT + h * D * TOK,
                                    [[TOK, D], [1, TOK]])),
                       r=[f"qkvloc{l}_{r}q"], a=[f"qkt{hb}q{r}"])
                    Y_(lambda e, h=h, rb0=rb0, r=r, l=l: e.dma_start(
                        out=qkt[rb0:rb0 + D, T + r * TOK:T + (r + 1) * TOK],
                        in_=bass.AP(qkvloc[l], (r * 3 + 1) * HDT + h * D * TOK,
                                    [[TOK, D], [1, TOK]])),
                       r=[f"qkvloc{l}_{r}k"], a=[f"qkt{hb}k{r}"])
                    Y_(lambda e, h=h, hb=hb, r=r, l=l: e.dma_start(
                        out=bass.AP(vph[hb], r * (TOK // P) * 65,
                                    [[65 * NKB, P], [65, TOK // P], [1, D]]),
                        in_=bass.AP(qkvloc[l], (r * 3 + 2) * HDT + h * TOK * D,
                                    [[D, P], [P * D, TOK // P], [1, D]])),
                       r=[f"qkvloc{l}_{r}v"], a=[f"vph{hb}"])

            for i in range(2):
                Gc(lambda e, i=i: e.memset(
                    bass.AP(vph[i], 64, [[65 * NKB, P], [65, NKB]]), 1.0),
                   r=[f"vph{i}"], w=[f"vph{i}"])
            head_dma(0)
            pend_av = []
            pend_norm = []

            def emit_S(h, hb, rb0, qs, g0, gsz, gb, psum, pn):
                for j in range(gsz):
                    kb = g0 + j
                    T_(lambda e, psum=psum, rb0=rb0, kb=kb, qs=qs, j=j:
                       e.matmul(psum[:, j * QS:(j + 1) * QS],
                                qkt[rb0:rb0 + D, T + kb * P:T + (kb + 1) * P],
                                qkt[rb0:rb0 + D, qs * QS:(qs + 1) * QS],
                                start=True, stop=True),
                       r=[f"qkt{hb}k{kb // (NKB // 2)}",
                          f"qkt{hb}q{qs // (NQS // 2)}"],
                       w=[pn], inc=(j == gsz - 1))
                S_(lambda e, psum=psum, gb=gb, gsz=gsz, h=h: e.activation(
                    atb[gb][:, 0:gsz * QS], psum[:, 0:gsz * QS], Act.Exp,
                    bias=ebias[:, h:h + 1], scale=0.125),
                   r=[pn, "ebias"], w=[f"atb{gb}"])
                qb0 = qs * KPQ
                for j in range(gsz):
                    kb = g0 + j
                    t = kb - qb0
                    nz = min(max(t, 0), KPQ)
                    if nz > 0:
                        V_(lambda e, gb=gb, j=j, nz=nz: e.memset(
                            atb[gb][:, j * QS:j * QS + nz * P], 0.0),
                           w=[f"atb{gb}"])
                    if 0 <= t < KPQ:
                        if l == 0:
                            V_(lambda e, gb=gb, j=j, t=t, h=h: e.tensor_mul(
                                atb[gb][:, j * QS + t * P:j * QS + (t + 1) * P],
                                atb[gb][:, j * QS + t * P:j * QS + (t + 1) * P],
                                ediag[:, h * P:(h + 1) * P]),
                               r=["ediag"], w=[f"atb{gb}"])
                        else:
                            V_(lambda e, gb=gb, j=j, t=t: e.tensor_mul(
                                atb[gb][:, j * QS + t * P:j * QS + (t + 1) * P],
                                atb[gb][:, j * QS + t * P:j * QS + (t + 1) * P],
                                tri[:, :]),
                               r=["tri"], w=[f"atb{gb}"])
                    if l == 0 and 0 <= t + 1 < KPQ:
                        V_(lambda e, gb=gb, j=j, t=t, h=h: e.tensor_mul(
                            atb[gb][:, j * QS + (t + 1) * P:j * QS + (t + 2) * P],
                            atb[gb][:, j * QS + (t + 1) * P:j * QS + (t + 2) * P],
                            eprev[:, h * P:(h + 1) * P]),
                           r=["eprev"], w=[f"atb{gb}"])

            def emit_AV(hb, qs, g0, gsz, gb, nkb, po):
                for j in range(gsz):
                    kb = g0 + j
                    T_(lambda e, hb=hb, kb=kb, gb=gb, j=j, nkb=nkb, po=po:
                       e.matmul(ps_o[0:65, po * QS:po * QS + QS],
                                vph[hb][:, kb * 65:(kb + 1) * 65],
                                atb[gb][:, j * QS:(j + 1) * QS],
                                start=(kb == 0), stop=(kb == nkb - 1)),
                       r=[f"vph{hb}", f"atb{gb}"], w=[f"ps_o{po}"],
                       inc=(kb == nkb - 1))

            def emit_norm1(h, hb, qs, po):
                V_(lambda e, po=po: e.tensor_scalar(
                    rnorm[64:65, 0:QS], ps_o[64:65, po * QS:po * QS + QS],
                    1e-35, None, Alu.max),
                   r=[f"ps_o{po}"], w=["rnorm"])
                V_(lambda e: e.reciprocal(rnorm[64:65, 0:QS],
                                          rnorm[64:65, 0:QS]),
                   r=["rnorm"], w=["rnorm"])

            def emit_norm2(h, hb, qs, po):
                T_(lambda e: e.matmul(ps_m[0:D, 0:QS], onesF[D:D + 1, :],
                                      rnorm[64:65, 0:QS], start=True, stop=True),
                   r=["onesF", "rnorm"], w=["ps_m"])
                V_(lambda e: e.tensor_copy(rnorm[0:D, 0:QS], ps_m[0:D, 0:QS]),
                   r=["ps_m"], w=["rnorm"])
                ob = qs % 2
                V_(lambda e, ob=ob, po=po: e.tensor_mul(
                    ot_sm[ob][:, :], ps_o[0:D, po * QS:po * QS + QS],
                    rnorm[0:D, 0:QS]),
                   r=[f"ps_o{po}", "rnorm"], w=[f"ot_sm{ob}"])
                Y_(lambda e, ob=ob, h=h, qs=qs, a2i=a2i: e.dma_start(
                    out=bass.AP(a2i, h * D * T + qs * QS, [[T, D], [1, QS]]),
                    in_=ot_sm[ob][:, :]),
                   r=[f"ot_sm{ob}"], a=[A2I])

            NORM_LAG = 3

            def pop_av():
                av, fin, fpo = pend_av.pop(0)
                av()
                if fin is not None:
                    n1, n2 = fin
                    pend_norm.append((gctr + NORM_LAG, fpo, n1, n2))

            def pop_norms(ready_only=True, po=None):
                while pend_norm:
                    rdy, npo, fn, nxt = pend_norm[0]
                    if ready_only and rdy > gctr and not (po is not None and npo == po):
                        break
                    pend_norm.pop(0)
                    fn()
                    if nxt is not None:
                        pend_norm.append((gctr + 3, npo, nxt, None))

            gctr = 0
            for h in range(HO):
                hb = h % 2
                rb0 = hb * D
                hctr = 0
                for qs in range(NQS):
                    po = qs % 2
                    nkb = KPQ * (qs + 1)
                    glist = list(_groups(nkb))
                    for gi, (g0, gsz) in enumerate(glist):
                        gb = gctr % 3
                        psum, pn = (ps_a, "ps_a") if gctr % 2 == 0 else (ps_b, "ps_b")
                        emit_S(h, hb, rb0, qs, g0, gsz, gb, psum, pn)
                        pop_norms(po=po if gi == 0 else None)
                        if len(pend_av) >= 2:
                            pop_av()
                        if hctr == 1 and h + 1 < HO:
                            head_dma(h + 1)
                        fin = ((lambda h=h, hb=hb, qs=qs, po=po:
                                emit_norm1(h, hb, qs, po)),
                               (lambda h=h, hb=hb, qs=qs, po=po:
                                emit_norm2(h, hb, qs, po))) \
                            if gi == len(glist) - 1 else None
                        pend_av.append(
                            (lambda hb=hb, qs=qs, g0=g0, gsz=gsz, gb=gb, nkb=nkb,
                             po=po: emit_AV(hb, qs, g0, gsz, gb, nkb, po), fin, po))
                        gctr += 1
                        hctr += 1
            while pend_av:
                pop_av()
                gctr += 1
                pop_norms()
            pop_norms(ready_only=False)
            Y_(lambda e, a2i=a2i: e.dma_start(
                out=a2i[0:1, AG2_N:AG2_N + TOK], in_=Xrs[0:1, 0:TOK]),
               r=["Xrs"], a=[A2I])

            def floc(e, l=l, a2i=a2i):
                dv = dyng(e)
                return e.dma_start(
                    out=ot2loc[l][0:1, 0:(DM // 2) * T],
                    in_=a2i[0:1, bass.ds(dv["ot_loc"], (DM // 2) * T)],
                    bounds_check=None)
            Gd(floc, r=[A2I], a=[f"ot2loc{l}_0"])
            CC_(lambda e, l=l: e.collective_compute(
                "AllGather", Alu.bypass, replica_groups=PAIRS,
                ins=[ag2_in[l][0:1, 0:AG2_N].opt()],
                outs=[ag2_out[l][0:1, 0:2 * AG2_N].opt()]),
                r=[A2I], w=[A2O])

            Y_(lambda e, a2o=a2o: e.dma_start(
                out=a2o[0:1, 2 * AG2_N:2 * AG2_N + TOK], in_=Xrs[0:1, 0:TOK]),
               r=["Xrs"], a=[A2O])
            QT = P * T
            for j in range(4):
                def fpar(e, j=j, l=l, a2o=a2o):
                    dv = dyn(e)
                    return e.dma_start(
                        out=ot2loc[l][0:1, (DM // 2) * T + j * QT:
                                      (DM // 2) * T + (j + 1) * QT],
                        in_=a2o[0:1, bass.ds(dv["ot_par"], (DM // 2) * T)][
                            0:1, j * QT:(j + 1) * QT],
                        bounds_check=None)
                Y_(fpar, r=[A2O], a=[f"ot2loc{l}_1{j}"])

            def pslice_dma(st, ic):
                sg = st % 2
                r = 0 if ic < NCH // 2 else 1
                row0 = (ic % (NCH // 2)) * P
                rn = (f"ot2loc{l}_0" if r == 0
                      else f"ot2loc{l}_1{ic - NCH // 2}")
                Y_(lambda e, st=st, ic=ic, sg=sg, r=r, row0=row0, l=l: e.dma_start(
                    out=ot_sl[sg][:, ic * ST:(ic + 1) * ST],
                    in_=bass.AP(ot2loc[l], r * (DM // 2) * T + row0 * T + st * ST,
                                [[T, P], [1, ST]])),
                   r=[rn], a=[f"ot_sl{sg}"])

            QW = min(TOK, HSC)
            NQH = TOK // QW

            def ln_sums(st):
                sb = st % 2
                zs, zn = zsl[sb], f"zsl{sb}"
                for ic in range(NCH):
                    S_(lambda e, zs=zs, zn=zn, ic=ic: e.activation(
                        tmpF[:, ST:2 * ST], zs[:, ic * ST:(ic + 1) * ST],
                        Act.Square),
                       r=[zn], w=["tmpF"])
                    T_(lambda e, zs=zs, ic=ic: e.matmul(
                        ps_o[0:1, 0:ST], onesp[:, :], zs[:, ic * ST:(ic + 1) * ST],
                        start=(ic == 0), stop=(ic == NCH - 1)),
                       r=["onesp", zn], w=["ps_o0"], inc=True)
                    T_(lambda e, ic=ic: e.matmul(
                        ps_m[0:1, 0:ST], onesp[:, :], tmpF[:, ST:2 * ST],
                        start=(ic == 0), stop=(ic == NCH - 1)),
                       r=["onesp", "tmpF"], w=["ps_m"], inc=True)
                V_(lambda e: e.tensor_scalar(row_m[:, :], ps_o[0:1, 0:ST],
                                             1.0 / DM, None, Alu.mult),
                   r=["ps_o0"], w=["rows"])
                V_(lambda e: e.tensor_scalar(row_q2[:, :], ps_m[0:1, 0:ST],
                                             1.0 / DM, None, Alu.mult),
                   r=["ps_m"], w=["rows"])
                V_(lambda e: e.tensor_mul(row_v[:, :], row_m[:, :], row_m[:, :]),
                   r=["rows"], w=["rows"])
                V_(lambda e: e.tensor_tensor(row_v[:, :], row_q2[:, :],
                                             row_v[:, :], Alu.subtract),
                   r=["rows"], w=["rows"])
                S_(lambda e: e.activation(row_i[:, :], row_v[:, :], Act.Ln,
                                          bias=epsc[0:1, 0:1]),
                   r=["rows", "epsc"], w=["rows"])
                S_(lambda e: e.activation(row_i[:, :], row_i[:, :], Act.Exp,
                                          scale=-0.5), r=["rows"], w=["rows"])

            def ln_apply(st, spill_x1=False, amax=False, write_out=False):
                sb = st % 2
                zs, zn = zsl[sb], f"zsl{sb}"
                T_(lambda e: e.matmul(ps_o[:, QS:QS + ST], onesr[:, :],
                                      row_m[:, :], start=True, stop=True),
                   r=["onesr", "rows"], w=["ps_o1"])
                T_(lambda e: e.matmul(ps_m[:, 0:ST], onesr[:, :], row_i[:, :],
                                      start=True, stop=True),
                   r=["onesr", "rows"], w=["ps_m"])
                for ic in range(NCH):
                    V_(lambda e, zs=zs, zn=zn, ic=ic: e.tensor_tensor(
                        tmpF[:, 0:ST], zs[:, ic * ST:(ic + 1) * ST],
                        ps_o[:, QS:QS + ST], Alu.subtract),
                       r=[zn, "ps_o1"], w=["tmpF"])
                    V_(lambda e, zs=zs, zn=zn, ic=ic: e.tensor_tensor(
                        zs[:, ic * ST:(ic + 1) * ST], tmpF[:, 0:ST],
                        ps_m[:, 0:ST], Alu.mult),
                       r=["tmpF", "ps_m"], a=[zn])
                    V_(lambda e, zs=zs, zn=zn, ic=ic, st=st: e.tensor_copy(
                        Xrs[:, ic * TOK + st * ST:ic * TOK + (st + 1) * ST],
                        zs[:, ic * ST:(ic + 1) * ST]),
                       r=[zn], a=["Xrs"])
                if spill_x1:
                    Y_(lambda e, zs=zs, st=st: e.dma_start(
                        out=bass.AP(x1loc, st * ST,
                                    [[TOK, P], [P * TOK, NCH], [1, ST]]),
                        in_=bass.AP(zs, 0, [[NCH * ST, P], [ST, NCH], [1, ST]])),
                       r=[zn], a=["x1loc"])
                if amax:
                    V_(lambda e, zs=zs: e.tensor_reduce(
                        statc[:, 4:5], zs[:, :], AxX, Alu.max,
                        apply_absolute_value=True),
                       r=[zn], w=["statc"])
                    V_(lambda e: e.tensor_tensor(statc[:, 5:6], statc[:, 5:6],
                                                 statc[:, 4:5], Alu.max),
                       r=["statc"], w=["statc"])
                if write_out:
                    Y_(lambda e, zs=zs, st=st: e.dma_start(
                        out=bass.AP(o_xT, st * ST,
                                    [[TOK, P], [P * TOK, NCH], [1, ST]]),
                        in_=bass.AP(zs, 0, [[NCH * ST, P], [ST, NCH], [1, ST]])),
                       r=[zn], w=["o_xT"] if st == NST - 1 else [], a=[] if st == NST - 1 else ["o_xT"])

            Gc(lambda e: e.memset(statc[:, 5:6], 0.0), w=["statc"])
            pend_apply = []
            for ic in range(NCH):
                pslice_dma(0, ic)
            for st in range(NST):
                sg = st % 2
                if st + 1 < NST:
                    for ic in range(NCH):
                        pslice_dma(st + 1, ic)
                for oc in range(NCH):
                    psum, pn = (ps_a, "ps_a") if oc % 2 == 0 else (ps_b, "ps_b")
                    for ic in range(NCH):
                        T_(lambda e, psum=psum, ic=ic, oc=oc, sg=sg: e.matmul(
                            psum[:, 0:ST],
                            wsl[1][:, ic * DM + oc * P:ic * DM + (oc + 1) * P],
                            ot_sl[sg][:, ic * ST:(ic + 1) * ST],
                            start=(ic == 0), stop=(ic == NCH - 1)),
                           r=["wsl1", f"ot_sl{sg}"], w=[pn], inc=(ic == NCH - 1))
                    V_(lambda e, psum=psum, oc=oc, sg=sg,
                       s0=bslot[l, "bp"]: e.scalar_tensor_tensor(
                           zsl[sg][:, oc * ST:(oc + 1) * ST],
                           psum[:, 0:ST], bcols[:, s0 + oc:s0 + oc + 1],
                           zsl[sg][:, oc * ST:(oc + 1) * ST],
                           Alu.add, Alu.add),
                       r=[pn, "bcols", f"zsl{sg}"], a=[f"zsl{sg}"])
                    if oc == 2 and pend_apply:
                        pend_apply.pop(0)()
                ln_sums(st)
                pend_apply.append(
                    lambda st=st: (ln_apply(st, spill_x1=True, amax=True),
                                   res_slab(st + 2, rsrc, rsn)
                                   if st + 2 < NST else None))
            while pend_apply:
                pend_apply.pop(0)()

            T_(lambda e: e.transpose(ps_m[0:1, 0:P], statc[:, 5:6], eyef[:, :]),
               r=["statc", "eyef"], w=["ps_m"])
            V_(lambda e: e.tensor_reduce(sc1[0:1, 3:4], ps_m[0:1, 0:P], AxX, Alu.max),
               r=["ps_m"], w=["sc1"])
            Y_(lambda e, l=l: e.dma_start(out=ar_in[l][0:1, 0:8],
                                          in_=sc1[0:1, 3:11]),
               r=["sc1"], w=[f"ari{l}"])
            CC_(lambda e, l=l: e.collective_compute(
                "AllReduce", Alu.max, replica_groups=[list(range(8))],
                ins=[ar_in[l].ap().opt()], outs=[ar_out[l].ap().opt()]),
                r=[f"ari{l}"], w=[f"aro{l}"])
            Y_(lambda e, l=l: e.dma_start(out=sc1[0:1, 4:5],
                                          in_=ar_out[l][0:1, 0:1]),
               r=[f"aro{l}"], w=["sc1"])
            if l == 0:
                V_(lambda e: e.reciprocal(sc1[0:1, 9:10], sc1[0:1, 4:5]),
                   r=["sc1"], w=["sc1"])
                V_(lambda e: e.tensor_scalar(sc1[0:1, 9:10], sc1[0:1, 9:10],
                                             127.0, None, Alu.mult),
                   r=["sc1"], w=["sc1"])
                V_(lambda e: e.tensor_scalar(sc1[0:1, 10:11], sc1[0:1, 4:5],
                                             1.0 / 127.0, None, Alu.mult),
                   r=["sc1"], w=["sc1"])
                V_(lambda e: e.tensor_scalar(sc1[0:1, 11:12], sc1[0:1, 0:1],
                                             1.0 / (DM * DM), None, Alu.mult),
                   r=["sc1"], w=["sc1"])
                V_(lambda e: e.tensor_tensor(sc1[0:1, 10:11], sc1[0:1, 10:11],
                                             sc1[0:1, 11:12], Alu.mult),
                   r=["sc1"], w=["sc1"])
                pairs2 = [(9, 2), (10, 5)]
            else:
                V_(lambda e: e.tensor_scalar(sc1[0:1, 9:10], sc1[0:1, 4:5],
                                             1e-12, None, Alu.max),
                   r=["sc1"], w=["sc1"])
                V_(lambda e: e.reciprocal(sc1[0:1, 9:10], sc1[0:1, 9:10]),
                   r=["sc1"], w=["sc1"])
                V_(lambda e: e.tensor_scalar(sc1[0:1, 9:10], sc1[0:1, 9:10],
                                             448.0, None, Alu.mult),
                   r=["sc1"], w=["sc1"])
                V_(lambda e: e.tensor_scalar(sc1[0:1, 10:11], sc1[0:1, 9:10],
                                             0.5, None, Alu.mult),
                   r=["sc1"], w=["sc1"])
                V_(lambda e: e.tensor_tensor(sc1[0:1, 11:12], sc1[0:1, 9:10],
                                             sc1[0:1, 1:2], Alu.mult),
                   r=["sc1"], w=["sc1"])
                V_(lambda e: e.reciprocal(sc1[0:1, 11:12], sc1[0:1, 11:12]),
                   r=["sc1"], w=["sc1"])
                pairs2 = [(9, 2), (10, 3), (11, 5)]
            for (c_in, c_out) in pairs2:
                T_(lambda e, c_in=c_in: e.matmul(
                    ps_m[:, 0:1], onesr[:, :], sc1[0:1, c_in:c_in + 1],
                    start=True, stop=True), r=["onesr", "sc1"], w=["ps_m"])
                V_(lambda e, c_out=c_out: e.tensor_copy(
                    scC[:, c_out:c_out + 1], ps_m[:, 0:1]),
                   r=["ps_m"], w=["scC"])
            res_slab(0, x1loc, "x1loc")
            if NST > 1:
                res_slab(1, x1loc, "x1loc")
            xq_map = []
            caps = [("ot_sl0", ot_sl[0], NCH * ST), ("ot_sl1", ot_sl[1], NCH * ST),
                    ("qst0", qst[0], TOK), ("xqb", xqb, TOK),
                    ("vph0", vph[0], TOK), ("vph1", vph[1], TOK)]
            ci_, off_ = 0, 0
            for ic in range(NCH):
                while caps[ci_][2] - off_ < TOK:
                    ci_, off_ = ci_ + 1, 0
                xq_map.append((caps[ci_][0], caps[ci_][1], off_))
                off_ += TOK
            def quant_tile(st, ic):
                sg = st % 2
                nm, tns, off = xq_map[ic]
                zt = lambda: zsl[sg][:, ic * ST:(ic + 1) * ST]
                if l == 0:
                    V_(lambda e, zt=zt: e.tensor_scalar(
                        x1c[:, 0:ST], zt(), scC[:, 2:3], C_RND, Alu.mult, Alu.add),
                       r=[f"zsl{sg}", "scC"], w=["x1c"])
                    V_(lambda e, tns=tns, off=off, st=st: e.tensor_scalar(
                        tns[:, off + st * ST:off + (st + 1) * ST],
                        x1c[:, 0:ST], C_RND, None, Alu.subtract),
                       r=["x1c"], a=[nm])
                else:
                    S_(lambda e, zt=zt: e.mul(x1c[:, 0:ST], zt(), scC[:, 2:3]),
                       r=[f"zsl{sg}", "scC"], w=["x1c"])
                    V_(lambda e, zt=zt: e.tensor_single_scalar(
                        f8c[:, 0:ST], zt(), scC[:, 3:4], Alu.mult),
                       r=[f"zsl{sg}", "scC"], w=["f8c"])
                    V_(lambda e: e.tensor_scalar(
                        bgb[:, 0:ST], f8c[:, 0:ST], 2.0, None, Alu.mult),
                       r=["f8c"], w=["bgb"])
                    S_(lambda e: e.activation(
                        x1c[:, ST:2 * ST], x1c[:, 0:ST], Act.Abs),
                       r=["x1c"], w=["x1c"])
                    V_(lambda e: e.tensor_scalar(
                        mkb[:, 0:ST], x1c[:, ST:2 * ST], FP8_SMALL, None,
                        Alu.is_ge),
                       r=["x1c"], w=["mkb"])
                    V_(lambda e, tns=tns, off=off, st=st: e.tensor_scalar(
                        tns[:, off + st * ST:off + (st + 1) * ST],
                        x1c[:, 0:ST], C_Q9, C_Q9, Alu.add, Alu.subtract),
                       r=["x1c"], a=[nm])
                    V_(lambda e, tns=tns, off=off, st=st: e.copy_predicated(
                        tns[:, off + st * ST:off + (st + 1) * ST],
                        mkb[:, 0:ST], bgb[:, 0:ST]),
                       r=["mkb", "bgb", nm], a=[nm])

            for st in range(NST):
                sg = st % 2
                for ic in range(NCH):
                    quant_tile(st, ic)
                for oc in range(NCH):
                    psum, pn = (ps_a, "ps_a") if oc % 2 == 0 else (ps_b, "ps_b")
                    for ic in range(NCH):
                        nm, tns, off = xq_map[ic]
                        T_(lambda e, psum=psum, ic=ic, oc=oc, st=st, tns=tns,
                           off=off: e.matmul(
                               psum[:, 0:ST],
                               wsl[0][:, ic * DM + oc * P:ic * DM + (oc + 1) * P],
                               tns[:, off + st * ST:off + (st + 1) * ST],
                               start=(ic == 0), stop=(ic == NCH - 1)),
                           r=["wsl0", nm], w=[pn], inc=(ic == NCH - 1))
                    V_(lambda e, psum=psum, oc=oc, s0=bslot[l, "bf"]:
                       e.tensor_scalar(
                           tmpF[:, 0:ST], psum[:, 0:ST], scC[:, 5:6],
                           bcols[:, s0 + oc:s0 + oc + 1], Alu.mult, Alu.add),
                       r=[pn, "scC", "bcols"], w=["tmpF"])
                    V_(lambda e, sg=sg, oc=oc: e.tensor_add(
                        zsl[sg][:, oc * ST:(oc + 1) * ST], tmpF[:, 0:ST],
                        zsl[sg][:, oc * ST:(oc + 1) * ST]),
                       r=["tmpF", f"zsl{sg}"], a=[f"zsl{sg}"])
                    if oc == 2 and pend_apply:
                        pend_apply.pop(0)()
                ln_sums(st)
                pend_apply.append(
                    lambda st=st: (ln_apply(st, spill_x1=(l == 0),
                                            write_out=(l == 1)),
                                   res_slab(st + 2, x1loc, "x1loc")
                                   if st + 2 < NST else None))
            while pend_apply:
                pend_apply.pop(0)()
            if l == 0:
                load_W(1, "q", 0)
                load_W(1, "k", 1)

        pr.op("sync", lambda e: e.nop(), [N("o_xT")], [], dma=False)

        pr.seal()
        nc._pr = pr
        with nc.Block() as block:
            @block.tensor
            def _(e):
                pr.emit("tensor", e, sems)

            @block.vector
            def _(e):
                pr.emit("vector", e, sems)

            @block.scalar
            def _(e):
                pr.emit("scalar", e, sems)

            @block.gpsimd
            def _(e):
                pr.emit("gpsimd", e, sems)

            @block.sync
            def _(e):
                pr.emit("sync", e, sems)

    return nc


def _deinterleave_perm():
    p = np.zeros(DM, np.int64)
    for h in range(NH):
        p[h * D:h * D + D // 2] = h * D + 2 * np.arange(D // 2)
        p[h * D + D // 2:(h + 1) * D] = h * D + 2 * np.arange(D // 2) + 1
    return p


def _bias_cols(b):
    return np.ascontiguousarray(b.reshape(NCH, P).T.astype(np.float32))


def kernel(L0_attn_x, L0_attn_Wq, L0_attn_bq, L0_attn_Wk, L0_attn_bk, L0_attn_Wv,
           L0_attn_bv, L0_attn_Wp, L0_attn_bp, L0_attn_rel_bias_table,
           L0_ffn_weight, L0_ffn_bias, L1_attn_Wq, L1_attn_bq, L1_attn_Wk,
           L1_attn_bk, L1_attn_Wv, L1_attn_bv, L1_attn_Wp, L1_attn_bp,
           L1_ffn_weight, L1_ffn_bias):
    perm = _deinterleave_perm()
    W = {
        (0, "q"): L0_attn_Wq, (0, "k"): L0_attn_Wk, (0, "v"): L0_attn_Wv,
        (0, "p"): L0_attn_Wp, (0, "f"): L0_ffn_weight,
        (1, "q"): np.ascontiguousarray(L1_attn_Wq[:, perm]),
        (1, "k"): np.ascontiguousarray(L1_attn_Wk[:, perm]),
        (1, "v"): L1_attn_Wv, (1, "p"): L1_attn_Wp, (1, "f"): L1_ffn_weight,
    }
    bq1 = np.asarray(L1_attn_bq)[perm]
    bk1 = np.asarray(L1_attn_bk)[perm]
    Bc = {
        (0, "bq"): _bias_cols(np.asarray(L0_attn_bq)),
        (0, "bk"): _bias_cols(np.asarray(L0_attn_bk)),
        (0, "bp"): _bias_cols(np.asarray(L0_attn_bp)),
        (0, "bf"): _bias_cols(np.asarray(L0_ffn_bias)),
        (1, "bq"): _bias_cols(bq1),
        (1, "bk"): _bias_cols(bk1),
        (1, "bp"): _bias_cols(np.asarray(L1_attn_bp)),
        (1, "bf"): _bias_cols(np.asarray(L1_ffn_bias)),
    }
    bv = {0: np.asarray(L0_attn_bv).reshape(1, DM).astype(np.float32),
          1: np.asarray(L1_attn_bv).reshape(1, DM).astype(np.float32)}
    relt = np.asarray(L0_attn_rel_bias_table).astype(np.float32)

    nc = build_program()
    in_maps = []
    for c in range(8):
        b, half = c // 2, c % 2
        cons = host_constants(half)
        m = {
            "x0T": np.ascontiguousarray(
                np.asarray(L0_attn_x)[b, half * TOK:(half + 1) * TOK, :].T
            ).astype(np.float32),
            "rel8": np.ascontiguousarray(relt[:, half * HO:(half + 1) * HO]),
            "d_ropeC": cons["ropeC"], "d_ropeS": cons["ropeS"],
            "d_tri": cons["tri01"],
            "d_eyef": cons["eye_f32"], "d_onesp": cons["ones_p"],
            "d_jrev": cons["jrev"], "d_perm": cons["perm"],
            "d_onesr": cons["ones_r"], "d_oneh": cons["oneh"],
        }
        for l in (0, 1):
            for w in ("q", "k", "v", "p", "f"):
                wa = np.asarray(W[l, w])
                if w == "p" and half == 1:
                    wa = np.concatenate([wa[DM // 2:], wa[:DM // 2]], axis=0)
                m[f"W{l}{w}"] = np.ascontiguousarray(wa, dtype=np.float32)
            for bn in ("bq", "bk", "bp", "bf"):
                m[f"B{l}{bn}"] = Bc[l, bn]
            m[f"B{l}bv"] = bv[l]
        in_maps.append(m)

    res = run_bass_kernel_spmd(nc, in_maps, core_ids=list(range(8)))
    out = np.empty((B, T, DM), np.float32)
    for c in range(8):
        b, half = c // 2, c % 2
        out[b, half * TOK:(half + 1) * TOK, :] = np.asarray(
            res.results[c]["o_xT"]).T
    return out
